# version 15
# baseline (speedup 1.0000x reference)
import numpy as np
import ml_dtypes
from contextlib import ExitStack

import concourse.bass as bass
import concourse.mybir as mybir
from concourse.bass_utils import run_bass_kernel_spmd
from concourse.alu_op_type import AluOpType as OP

F32 = mybir.dt.float32
BF16 = mybir.dt.bfloat16
U32 = mybir.dt.uint32
AF = mybir.ActivationFunctionType
AX = mybir.AxisListType

D = 2048
L = 2048
DI = 4096
NH = 64
CH = 128
NCH = L // CH
FH = 1408
NE = 9
KC = D // 128
BIGNEG = -1e9
PH_LN, PH_INPROJ, PH_GATE, PH_F1, PH_F2, PH_SSD, PH_WOUT = range(7)


class Sem:
    def __init__(self, nc, ctx, name):
        self.h = ctx.enter_context(nc.semaphore(name))
        self.v = 0

    def mark(self, inst, n=1):
        inst.then_inc(self.h, n)
        self.v += n
        return self.v


def mkstream(E, name):
    s = Sem(E["nc"], E["ctx"], name)
    E.setdefault("streams", []).append(s)
    return s


def build(max_phase=PH_WOUT, dbg=False):
    nc = bass.Bass("TRN2", target_bir_lowering=False)
    dram = lambda n, s, dt=F32, kind="ExternalInput": nc.dram_tensor(n, s, dt, kind=kind).ap()

    E = {"nc": nc, "dbg": dbg}
    E["x_in"] = dram("x", [L, D])
    E["win_t"] = dram("win_t", [D, 8512])
    E["wconv_r"] = dram("wconv_r", [128, 34 * 4])
    E["bconv_r"] = dram("bconv_r", [128, 34])
    E["dtb_c"] = dram("dtb_c", [NH, 1])
    E["alog_c"] = dram("alog_c", [NH, 1])
    E["dp_row"] = dram("dp_row", [1, NH])
    E["nw_row"] = dram("nw_row", [1, DI])
    E["wout_t"] = dram("wout_t", [DI, D])
    E["gate_t"] = dram("gate_t", [D, 8])
    E["w1"] = dram("w1", [8, D, FH])
    E["w2"] = dram("w2", [8, FH, D])
    E["w3"] = dram("w3", [8, D, FH])
    E["sw1"] = dram("sw1", [D, FH])
    E["sw2"] = dram("sw2", [FH, D])
    E["sw3"] = dram("sw3", [D, FH])
    E["i128_in"] = dram("i128_in", [128, 128], BF16)
    E["i64_in"] = dram("i64_in", [NH, NH])
    E["mneg_in"] = dram("mneg_in", [128, CH])
    E["ones64_in"] = dram("ones64_in", [NH, 128])
    E["mask_in"] = dram("mask_in", [128, CH], BF16)
    E["out"] = dram("out", [L, D], F32, kind="ExternalOutput")

    ik = "ExternalOutput" if dbg else "Internal"
    E["zs_sp"] = dram("zs_sp", [DI, L], BF16, kind=ik)
    E["x_sp"] = dram("x_sp", [DI, L], BF16, kind=ik)
    E["h_sp"] = dram("h_sp", [NE * FH, L], BF16, kind=ik)
    E["yn_sp"] = dram("yn_sp", [DI, L], BF16, kind=ik)
    E["cs_sp"] = dram("cs_sp", [2 * NH, CH], F32, kind="Internal")
    E["g_sp"] = dram("g_sp", [128, 16 * D], BF16, kind=ik)
    if dbg:
        E["dbg_bc"] = dram("dbg_bc", [256, L], F32, kind="ExternalOutput")
        E["dbg_dta"] = dram("dbg_dta", [NH, L], F32, kind="ExternalOutput")
        E["dbg_cmb"] = dram("dbg_cmb", [128, 16 * 8], F32, kind="ExternalOutput")
        E["dbg_xn"] = dram("dbg_xn", [128, KC * L], F32, kind="ExternalOutput")

    with ExitStack() as ctx:
        E["ctx"] = ctx
        sb = lambda n, s, dt: ctx.enter_context(nc.sbuf_tensor(n, s, dt))
        for n, s, dt in [
            ("i128", [128, 128], BF16), ("i64f", [NH, NH], F32),
            ("bfm", [128, L], BF16), ("cfm", [128, L], BF16),
            ("dta", [NH, L], F32), ("cmb", [128, 16 * 8], F32),
            ("wconv", [128, 34 * 4], F32), ("bconv", [128, 34], F32),
            ("dtb", [NH, 1], F32), ("nega", [NH, 1], F32), ("scrc", [NH, 1], F32),
            ("dpb", [128, NH], F32), ("dpbbf", [128, NH], BF16),
            ("nwb", [128, DI], BF16),
            ("epsc", [128, 1], F32), ("ones64", [NH, 128], F32),
            ("mask128", [128, CH], BF16),
        ]:
            E[n] = sb(n, s, dt)
        for n in ("sP", "sV", "sA"):
            E[n] = Sem(nc, ctx, n)
        E["blk"] = ctx.enter_context(nc.Block())

        _init(E)
        with ExitStack() as ctx_xn:
            E["xn"] = ctx_xn.enter_context(nc.sbuf_tensor("xn", [128, KC * L], BF16))
            with ExitStack() as ctx_head:
                E["pT"] = [ctx_head.enter_context(nc.psum_tensor(f"h_pT{i}", [128, 512], F32)) for i in range(2)]
                E["pmm"] = [ctx_head.enter_context(nc.psum_tensor(f"h_pm{i}", [128, 512], F32)) for i in range(4)]
                E["pg"] = ctx_head.enter_context(nc.psum_tensor("h_pg", [128, 128], F32))
                hsb = lambda n, s, dt: ctx_head.enter_context(nc.sbuf_tensor(n, s, dt))
                E["wblk"] = [hsb(f"ip_w{i}", [128, KC * 128], BF16) for i in range(2)]
                E["zstage"] = [hsb(f"ip_zs{i}", [128, L], BF16) for i in range(2)]
                E["xstage"] = [hsb(f"ip_xst{i}", [128, L], BF16) for i in range(2)]
                E["xpre"] = hsb("ip_xpre", [128, L + 3], F32)
                E["cv"] = hsb("ip_cv", [128, L], F32)
                E["cva"] = hsb("ip_cva", [128, L], F32)
                E["dstage"] = hsb("ip_dst", [NH, L], F32)
                E["dt0"] = hsb("ip_dt0", [NH, L], F32)
                E["dt1"] = hsb("ip_dt1", [NH, L], F32)
                E["gwb"] = hsb("gt_w", [128, KC * 8], BF16)
                E["gsc"] = hsb("gt_s", [128, 128], F32)
                _ln(E)
                if max_phase >= PH_INPROJ:
                    _inproj(E)
                if max_phase >= PH_GATE:
                    _gate(E)
            if max_phase >= PH_F1:
                _f1(E)
            if dbg:
                _dbg_xn(E)
        if max_phase >= PH_F2:
            _f2(E)
        if max_phase >= PH_SSD:
            E["whh"] = ctx.enter_context(nc.sbuf_tensor("whh", [128, 32 * 1024], BF16))
            _ssd(E)
        if max_phase >= PH_WOUT:
            _wout(E)
        if dbg:
            _dbg_dump(E)

        @E["blk"].sync
        def _(s):
            for st in E.get("streams", []):
                s.wait_ge(st.h, st.v)
            s.wait_ge(E["sP"].h, E["sP"].v)
            s.wait_ge(E["sV"].h, E["sV"].v)
            s.wait_ge(E["sA"].h, E["sA"].v)
    return nc


def _init(E):
    blk, sA, sV = E["blk"], E["sA"], E["sV"]
    sIC = mkstream(E, "sIC")

    @blk.gpsimd
    def _(g):
        sIC.mark(g.dma_start(out=E["i128"][:], in_=E["i128_in"]), 16)
        sIC.mark(g.dma_start(out=E["i64f"][:], in_=E["i64_in"]), 16)
        sIC.mark(g.dma_start(out=E["wconv"][:], in_=E["wconv_r"]), 16)
        sIC.mark(g.dma_start(out=E["bconv"][:], in_=E["bconv_r"]), 16)
        sIC.mark(g.dma_start(out=E["dtb"][:], in_=E["dtb_c"]), 16)
        sIC.mark(g.dma_start(out=E["scrc"][:], in_=E["alog_c"]), 16)
        sIC.mark(g.dma_start(out=E["dpb"][:], in_=E["dp_row"].broadcast_to([128, NH])), 16)
        sIC.mark(g.dma_start(out=E["nwb"][:], in_=E["nw_row"].broadcast_to([128, DI])), 16)
        sIC.mark(g.dma_start(out=E["ones64"][:], in_=E["ones64_in"]), 16)
        sIC.mark(g.dma_start(out=E["mask128"][:], in_=E["mask_in"]), 16)
    E["init_loads"] = (sIC, sIC.v)

    @blk.scalar
    def _(sc):
        sc.wait_ge(sIC.h, sIC.v)
        sA.mark(sc.activation(E["nega"][:], E["scrc"][:], AF.Exp))
        sA.mark(sc.activation(E["dpbbf"][:], E["dpb"][:], AF.Copy))
    E["dpbbf_ok"] = sA.v

    @blk.vector
    def _(v):
        v.memset(E["epsc"][:], 1e-5)
        v.wait_ge(sA.h, sA.v)
        sV.mark(v.tensor_scalar(E["scrc"][:], E["nega"][:], -1.0, None, OP.mult))
        v.wait_ge(sV.h, sV.v)
        sV.mark(v.tensor_copy(E["nega"][:], E["scrc"][:]))
    E["nega_ready"] = sV.v


def _ln(E):
    nc, blk = E["nc"], E["blk"]
    sP, sV, sA = E["sP"], E["sV"], E["sA"]
    xn, i128, x_in = E["xn"], E["i128"], E["x_in"]
    sLN = [mkstream(E, f"sLN{j}") for j in range(4)]

    with ExitStack() as c2:
        lsb = lambda n, s, dt: c2.enter_context(nc.sbuf_tensor(n, s, dt))
        xq = [lsb(f"ln_x{j}", [128, D], F32) for j in range(2)]
        xnq = [lsb(f"ln_xn{j}", [128, D], BF16) for j in range(4)]
        st = lsb("ln_stats", [128, 8], F32)
        pT = E["pT"]

        E["ln_sm"] = {}
        bank_free = {}
        xq_free = {}
        ln_mm_last = {}
        prev_sm_done = None
        for sm in range(4):
            for j in range(4):
                m = sm * 4 + j

                @blk.sync
                def _(s, m=m, j=j, prev=xq_free.get(j % 2)):
                    if prev is not None:
                        s.wait_ge(sV.h, prev)
                    sLN[j].mark(s.dma_start(out=xq[j % 2][:], in_=x_in[m * 128:(m + 1) * 128, :]), 16)
                ld = sLN[j].v

                @blk.vector
                def _(v, j=j, ld=ld):
                    v.wait_ge(sLN[j].h, ld)
                    sV.mark(v.reduce_sum(st[:, 0:1], xq[j % 2][:], AX.X))
                    v.wait_ge(sV.h, sV.v)
                    sV.mark(v.tensor_scalar(st[:, 4:5], st[:, 0:1], 1.0 / D, None, OP.mult))
                mean_ok = sV.v

                @blk.scalar
                def _(sc, j=j, ld=ld, tml=ln_mm_last.get(0)):
                    sc.wait_ge(sLN[j].h, ld)
                    if tml is not None:
                        sc.wait_ge(sP.h, tml)
                    sA.mark(sc.activation(xnq[j][:], xq[j % 2][:], AF.Square, accum_out=st[:, 1:2]))
                ssq_ok = sA.v

                @blk.vector
                def _(v, mean_ok=mean_ok, ssq_ok=ssq_ok):
                    v.wait_ge(sV.h, mean_ok)
                    v.wait_ge(sA.h, ssq_ok)
                    sV.mark(v.tensor_scalar(st[:, 5:6], st[:, 1:2], 1.0 / D, None, OP.mult))
                    sV.mark(v.tensor_tensor(st[:, 6:7], st[:, 4:5], st[:, 4:5], OP.mult))
                    v.wait_ge(sV.h, sV.v)
                    sV.mark(v.tensor_tensor(st[:, 2:3], st[:, 5:6], st[:, 6:7], OP.subtract))
                var_ok = sV.v

                @blk.scalar
                def _(sc, var_ok=var_ok):
                    sc.wait_ge(sV.h, var_ok)
                    sA.mark(sc.activation(st[:, 7:8], st[:, 2:3], AF.Ln, bias=E["epsc"][:]))
                    sc.wait_ge(sA.h, sA.v)
                    sA.mark(sc.activation(st[:, 3:4], st[:, 7:8], AF.Exp, scale=-0.5))
                rstd_ok = sA.v

                @blk.vector
                def _(v, j=j, rstd_ok=rstd_ok, var_ok=var_ok):
                    v.wait_ge(sA.h, rstd_ok)
                    v.wait_ge(sV.h, var_ok)
                    sV.mark(v.tensor_scalar(xnq[j][:], xq[j % 2][:], st[:, 4:5], st[:, 3:4],
                                            OP.subtract, OP.mult))
                xq_free[j % 2] = sV.v
            xn_ready = sV.v
            for kc in range(KC):
                pb = pT[kc % 2]
                war = bank_free.get(kc % 2)

                @blk.tensor
                def _(t, kc=kc, pb=pb, war=war, xn_ready=xn_ready):
                    t.wait_ge(sV.h, xn_ready)
                    if war is not None:
                        t.wait_ge(sV.h, war)
                    for j in range(4):
                        mi = t.matmul(pb[:, j * 128:(j + 1) * 128], xnq[j][:, kc * 128:(kc + 1) * 128],
                                      i128[:], start=(j == 0), stop=(j == 3))
                    sP.mark(mi)
                mm_ok = sP.v
                if kc == KC - 1:
                    ln_mm_last[0] = sP.v

                @blk.vector
                def _(v, sm=sm, kc=kc, pb=pb, mm_ok=mm_ok):
                    v.wait_ge(sP.h, mm_ok)
                    sV.mark(v.tensor_copy(xn[:, kc * L + sm * 512: kc * L + (sm + 1) * 512], pb[:]))
                bank_free[kc % 2] = sV.v
            prev_sm_done = sV.v
            E["ln_sm"][sm] = sV.v
    E["xn_ready"] = (sV, sV.v)


def _inproj(E):
    nc, blk = E["nc"], E["blk"]
    sP, sV, sA = E["sP"], E["sV"], E["sA"]
    xn = E["xn"]
    wconv, bconv, dtb, nega, dta = E["wconv"], E["bconv"], E["dtb"], E["nega"], E["dta"]
    bfm, cfm = E["bfm"], E["cfm"]
    zs_sp, x_sp, win_t = E["zs_sp"], E["x_sp"], E["win_t"]
    sWB = [mkstream(E, f"sWB{i}") for i in range(2)]
    sZS = [mkstream(E, f"sZS{i}") for i in range(2)]
    sXS = [mkstream(E, f"sXS{i}") for i in range(2)]

    if True:
        wblk = E["wblk"]
        pmm = E["pmm"]
        zstage = E["zstage"]
        xstage = E["xstage"]
        xpre = E["xpre"]
        cv = E["cv"]
        cva = E["cva"]
        dstage = E["dstage"]
        dt0 = E["dt0"]
        dt1 = E["dt1"]

        @blk.vector
        def _(v):
            sV.mark(v.memset(xpre[:, 0:3], 0.0))
        xpre0_ok = sV.v

        wload = {}
        mm_done = {}
        bank_free = {}
        zst_free = {}
        xst_free = {}
        cv_free = None
        conv_done = None

        for mb in range(67):
            msz = 64 if mb == 66 else 128
            buf = wblk[mb % 2]
            war = mm_done.get(mb - 2)

            @blk.gpsimd
            def _(gq, mb=mb, msz=msz, buf=buf, war=war):
                if war is not None:
                    gq.wait_ge(sP.h, war)
                src = win_t[:, mb * 128: mb * 128 + msz].rearrange("(kc p) m -> p kc m", p=128)
                sWB[mb % 2].mark(gq.dma_start(
                    out=buf[:, 0:KC * msz].rearrange("p (kc m) -> p kc m", kc=KC),
                    in_=src), 16)
            wload[mb] = sWB[mb % 2].v

            mm_oks = []
            for tb in range(4):
                pb = pmm[tb]
                pwar = bank_free.get(tb)

                @blk.tensor
                def _(t, mb=mb, tb=tb, msz=msz, buf=buf, pb=pb, pwar=pwar):
                    t.wait_ge(sWB[mb % 2].h, wload[mb])
                    if mb < 2:
                        t.wait_ge(sV.h, E["ln_sm"][tb])
                    if pwar is not None:
                        t.wait_ge(pwar[0].h, pwar[1])
                    dst = pb[0:msz, :] if msz < 128 else pb[:]
                    for kc in range(KC):
                        mi = t.matmul(dst, buf[:, kc * msz:(kc + 1) * msz],
                                      xn[:, kc * L + tb * 512: kc * L + (tb + 1) * 512],
                                      start=(kc == 0), stop=(kc == KC - 1))
                    sP.mark(mi)
                mm_oks.append(sP.v)
            mm_done[mb] = sP.v

            if mb < 32:
                zbuf = zstage[mb % 2]

                @blk.scalar
                def _(sc, mb=mb, zbuf=zbuf, mm_oks=tuple(mm_oks)):
                    w = zst_free.get(mb - 2)
                    if w is not None:
                        sc.wait_ge(w[0].h, w[1])
                    for tb in range(4):
                        sc.wait_ge(sP.h, mm_oks[tb])
                        sA.mark(sc.activation(zbuf[:, tb * 512:(tb + 1) * 512], pmm[tb][:], AF.Silu))
                        bank_free[tb] = (sA, sA.v)

                @blk.sync
                def _(s, mb=mb, zbuf=zbuf, a=sA.v):
                    s.wait_ge(sA.h, a)
                    sZS[mb % 2].mark(s.dma_start(out=zs_sp[mb * 128:(mb + 1) * 128, :],
                                                 in_=zbuf[:]), 16)
                zst_free[mb] = (sZS[mb % 2], sZS[mb % 2].v)
            elif mb < 66:
                cb = mb - 32

                @blk.vector
                def _(v, cb=cb, mm_oks=tuple(mm_oks), cd=conv_done, cvf=cv_free):
                    if cd is not None:
                        v.wait_ge(sV.h, cd)
                    for tb in range(4):
                        v.wait_ge(sP.h, mm_oks[tb])
                        sV.mark(v.tensor_copy(xpre[:, 3 + tb * 512: 3 + (tb + 1) * 512], pmm[tb][:]))
                        bank_free[tb] = (sV, sV.v)
                    if cb == 0:
                        v.wait_ge(sV.h, xpre0_ok)
                    if cvf is not None:
                        v.wait_ge(sA.h, cvf)
                    v.wait_ge(sV.h, sV.v)
                    c0 = cb * 4
                    sV.mark(v.tensor_scalar(cva[:], xpre[:, 0:L], wconv[:, c0:c0 + 1], None, OP.mult))
                    v.wait_ge(sV.h, sV.v)
                    sV.mark(v.scalar_tensor_tensor(cv[:], xpre[:, 1:1 + L], wconv[:, c0 + 1:c0 + 2], cva[:], OP.mult, OP.add))
                    v.wait_ge(sV.h, sV.v)
                    sV.mark(v.scalar_tensor_tensor(cva[:], xpre[:, 2:2 + L], wconv[:, c0 + 2:c0 + 3], cv[:], OP.mult, OP.add))
                    v.wait_ge(sV.h, sV.v)
                    sV.mark(v.scalar_tensor_tensor(cv[:], xpre[:, 3:3 + L], wconv[:, c0 + 3:c0 + 4], cva[:], OP.mult, OP.add))
                conv_done = sV.v
                cv_ok = sV.v
                if cb < 32:
                    xbuf = xstage[cb % 2]

                    @blk.scalar
                    def _(sc, cb=cb, xbuf=xbuf, cv_ok=cv_ok):
                        sc.wait_ge(sV.h, cv_ok)
                        w = xst_free.get(cb - 2)
                        if w is not None:
                            sc.wait_ge(w[0].h, w[1])
                        sA.mark(sc.activation(xbuf[:], cv[:], AF.Silu, bias=bconv[:, cb:cb + 1]))
                    cv_free = sA.v

                    @blk.sync
                    def _(s, cb=cb, xbuf=xbuf, a=sA.v):
                        s.wait_ge(sA.h, a)
                        sXS[cb % 2].mark(s.dma_start(out=x_sp[cb * 128:(cb + 1) * 128, :],
                                                     in_=xbuf[:]), 16)
                    xst_free[cb] = (sXS[cb % 2], sXS[cb % 2].v)
                else:
                    dstb = bfm if cb == 32 else cfm

                    @blk.scalar
                    def _(sc, cb=cb, dstb=dstb, cv_ok=cv_ok):
                        sc.wait_ge(sV.h, cv_ok)
                        sA.mark(sc.activation(dstb[:], cv[:], AF.Silu, bias=bconv[:, cb:cb + 1]))
                    cv_free = sA.v
            else:
                @blk.vector
                def _(v, mm_oks=tuple(mm_oks)):
                    for tb in range(4):
                        v.wait_ge(sP.h, mm_oks[tb])
                        sl = slice(tb * 512, (tb + 1) * 512)
                        sV.mark(v.tensor_scalar(dstage[:, sl], pmm[tb][0:NH, :], dtb[:], None, OP.add))
                        bank_free[tb] = (sV, sV.v)

        raw_ok = sV.v

        @blk.scalar
        def _(sc, raw_ok=raw_ok):
            sc.wait_ge(sV.h, raw_ok)
            sA.mark(sc.activation(dt0[:], dstage[:], AF.Abs))
            sc.wait_ge(sA.h, sA.v)
            sA.mark(sc.activation(dt1[:], dt0[:], AF.Exp, scale=-1.0))
            sc.wait_ge(sA.h, sA.v)
            sA.mark(sc.activation(dt0[:], dt1[:], AF.Ln, bias=1.0))
        sp_ok = sA.v

        @blk.vector
        def _(v, sp_ok=sp_ok):
            v.wait_ge(sA.h, sp_ok)
            v.wait_ge(sV.h, E["nega_ready"])
            sV.mark(v.tensor_scalar(dt1[:], dstage[:], 0.0, None, OP.max))
            v.wait_ge(sV.h, sV.v)
            sV.mark(v.tensor_tensor(dstage[:], dt1[:], dt0[:], OP.add))
            v.wait_ge(sV.h, sV.v)
            sV.mark(v.tensor_scalar(dta[:], dstage[:], nega[:], None, OP.mult))
    E["dta_ready_V"] = sV.v
    E["bc_ready_A"] = sA.v
    E["inproj_spill_finals"] = [(s, s.v) for s in sZS + sXS]


def _gate(E):
    nc, blk = E["nc"], E["blk"]
    sP, sV, sA = E["sP"], E["sV"], E["sA"]
    xn, cmb, gate_t = E["xn"], E["cmb"], E["gate_t"]
    sGW = mkstream(E, "sGW")
    gw, sc_all, pg = E["gwb"], E["gsc"], E["pg"]

    with ExitStack() as c2:
        lsb = lambda n, s, dt: c2.enter_context(nc.sbuf_tensor(n, s, dt))
        m8 = lsb("gt_m8", [128, 8], F32)
        z8 = lsb("gt_z8", [128, 8], F32)
        den = lsb("gt_den", [128, 1], F32)
        rcp = lsb("gt_rcp", [128, 1], F32)
        pick = lsb("gt_pick", [128, 8], F32)
        msk = lsb("gt_msk", [128, 8], U32)

        @blk.gpsimd
        def _(g):
            sGW.mark(g.dma_start(out=gw[:].rearrange("p (kc m) -> p kc m", kc=KC),
                                 in_=gate_t.rearrange("(kc p) m -> p kc m", p=128)), 16)
        gw_ok = sGW.v

        @blk.vector
        def _(v):
            v.memset(z8[:], 0.0)

        @blk.tensor
        def _(t):
            t.wait_ge(sGW.h, gw_ok)
            xr = E["xn_ready"]
            t.wait_ge(xr[0].h, xr[1])
            for m in range(16):
                for kc in range(KC):
                    mi = t.matmul(pg[:, m * 8:(m + 1) * 8],
                                  xn[:, kc * L + m * 128: kc * L + (m + 1) * 128],
                                  gw[:, kc * 8:(kc + 1) * 8],
                                  start=(kc == 0), stop=(kc == KC - 1))
            sP.mark(mi)
        mm_ok = sP.v

        @blk.scalar
        def _(sc, mm_ok=mm_ok):
            sc.wait_ge(sP.h, mm_ok)
            sA.mark(sc.activation(sc_all[:], pg[:], AF.Sigmoid))
        E["gate_sig_A"] = sA.v
        sig_ok = sA.v

        for m in range(16):
            @blk.vector
            def _(v, m=m, sig_ok=sig_ok):
                if m == 0:
                    v.wait_ge(sA.h, sig_ok)
                sV.mark(v.max(out=m8[:], in_=sc_all[:, m * 8:(m + 1) * 8]))
                v.wait_ge(sV.h, sV.v)
                sV.mark(v.tensor_scalar(den[:], m8[:, 0:1], m8[:, 1:2], 1e-20, OP.add, OP.add))
                v.wait_ge(sV.h, sV.v)
                sV.mark(v.reciprocal(rcp[:], den[:]))
                v.wait_ge(sV.h, sV.v)
                sV.mark(v.tensor_scalar(pick[:], sc_all[:, m * 8:(m + 1) * 8], rcp[:], 2.5, OP.mult, OP.mult))
                sV.mark(v.tensor_scalar(msk[:], sc_all[:, m * 8:(m + 1) * 8], m8[:, 1:2], None, OP.is_ge))
                v.wait_ge(sV.h, sV.v)
                sV.mark(v.select(cmb[:, m * 8:(m + 1) * 8], msk[:], pick[:], z8[:], add_drain=True))
    E["cmb_ready_V"] = sV.v


def _f1(E):
    nc, blk = E["nc"], E["blk"]
    sP, sV, sA = E["sP"], E["sV"], E["sA"]
    xn = E["xn"]
    h_sp, w1, w3, sw1, sw3 = E["h_sp"], E["w1"], E["w3"], E["sw1"], E["sw3"]
    sFW = [mkstream(E, f"sFW{i}") for i in range(2)]
    sHS = [mkstream(E, f"sHS{i}") for i in range(2)]

    with ExitStack() as c2:
        lsb = lambda n, s, dt: c2.enter_context(nc.sbuf_tensor(n, s, dt))
        wa = [lsb(f"f1_wa{i}", [128, KC * 128], BF16) for i in range(2)]
        wb = [lsb(f"f1_wb{i}", [128, KC * 128], BF16) for i in range(2)]
        sil = [lsb(f"f1_sil{i}", [128, 512], F32) for i in range(2)]
        hstage = [lsb(f"f1_h{i}", [128, L], BF16) for i in range(2)]
        p1 = [c2.enter_context(nc.psum_tensor(f"f1_p1{i}", [128, 512], F32)) for i in range(2)]
        p3 = [c2.enter_context(nc.psum_tensor(f"f1_p3{i}", [128, 512], F32)) for i in range(2)]

        wload = {}
        mm_done = {}
        bank_free = {}
        hst_free = {}
        sil_free = {}

        for idx in range(NE * 11):
            e, mb = divmod(idx, 11)
            bi = idx % 2
            war = mm_done.get(idx - 2)

            @blk.gpsimd
            def _(gq, idx=idx, e=e, mb=mb, bi=bi, war=war):
                if idx < 2:
                    gq.wait_ge(sV.h, E["cmb_ready_V"])
                    gq.wait_ge(sV.h, E["dta_ready_V"])
                    gq.wait_ge(sA.h, E["bc_ready_A"])
                    for s, v in E["inproj_spill_finals"]:
                        gq.wait_ge(s.h, v)
                if war is not None:
                    gq.wait_ge(sP.h, war)
                if e < 8:
                    s1 = w1[e, :, mb * 128:(mb + 1) * 128]
                    s3 = w3[e, :, mb * 128:(mb + 1) * 128]
                else:
                    s1 = sw1[:, mb * 128:(mb + 1) * 128]
                    s3 = sw3[:, mb * 128:(mb + 1) * 128]
                sFW[bi].mark(gq.dma_start(out=wa[bi][:].rearrange("p (kc m) -> p kc m", kc=KC),
                                          in_=s1.rearrange("(kc p) m -> p kc m", p=128)), 16)
                sFW[bi].mark(gq.dma_start(out=wb[bi][:].rearrange("p (kc m) -> p kc m", kc=KC),
                                          in_=s3.rearrange("(kc p) m -> p kc m", p=128)), 16)
            wload[idx] = sFW[bi].v

            hbuf = hstage[idx % 2]
            for tb in range(4):
                pi = tb % 2
                pwar = bank_free.get(pi)

                @blk.tensor
                def _(t, idx=idx, tb=tb, bi=bi, pi=pi, pwar=pwar):
                    t.wait_ge(sFW[bi].h, wload[idx])
                    if idx == 0 and tb == 0:
                        t.wait_ge(sA.h, E["gate_sig_A"])
                        t.wait_ge(sV.h, E["dta_ready_V"])
                    if pwar is not None:
                        t.wait_ge(sV.h, pwar)
                    for kc in range(KC):
                        t.matmul(p1[pi][:], wa[bi][:, kc * 128:(kc + 1) * 128],
                                 xn[:, kc * L + tb * 512: kc * L + (tb + 1) * 512],
                                 start=(kc == 0), stop=(kc == KC - 1))
                    for kc in range(KC):
                        mi = t.matmul(p3[pi][:], wb[bi][:, kc * 128:(kc + 1) * 128],
                                      xn[:, kc * L + tb * 512: kc * L + (tb + 1) * 512],
                                      start=(kc == 0), stop=(kc == KC - 1))
                    sP.mark(mi)
                mm_ok = sP.v
                if tb == 3:
                    mm_done[idx] = sP.v

                @blk.scalar
                def _(sc, pi=pi, mm_ok=mm_ok, sf=sil_free.get(tb % 2)):
                    sc.wait_ge(sP.h, mm_ok)
                    if sf is not None:
                        sc.wait_ge(sV.h, sf)
                    sA.mark(sc.activation(sil[pi][:], p1[pi][:], AF.Silu))
                sil_ok = sA.v

                @blk.vector
                def _(v, idx=idx, tb=tb, pi=pi, hbuf=hbuf, sil_ok=sil_ok):
                    v.wait_ge(sA.h, sil_ok)
                    if tb == 0:
                        w = hst_free.get(idx - 2)
                        if w is not None:
                            v.wait_ge(w[0].h, w[1])
                    sV.mark(v.tensor_tensor(hbuf[:, tb * 512:(tb + 1) * 512], sil[pi][:],
                                            p3[pi][:], OP.mult))
                bank_free[pi] = sV.v
                sil_free[tb % 2] = sV.v

            @blk.sync
            def _(s, idx=idx, e=e, mb=mb, hbuf=hbuf, vv=sV.v):
                s.wait_ge(sV.h, vv)
                sHS[idx % 2].mark(s.dma_start(
                    out=h_sp[e * FH + mb * 128: e * FH + (mb + 1) * 128, :],
                    in_=hbuf[:]), 16)
            hst_free[idx] = (sHS[idx % 2], sHS[idx % 2].v)
    E["f1_spill_finals"] = [(s, s.v) for s in sHS]


def _f2(E):
    nc, blk = E["nc"], E["blk"]
    sP, sV, sA = E["sP"], E["sV"], E["sA"]
    cmb, g_sp = E["cmb"], E["g_sp"]
    h_sp, w2, sw2 = E["h_sp"], E["w2"], E["sw2"]
    sW2 = [mkstream(E, f"sW2{i}") for i in range(3)]
    sHB = [mkstream(E, f"sHB{i}") for i in range(3)]
    sGS = [mkstream(E, f"sGS{i}") for i in range(2)]

    with ExitStack() as c2:
        lsb = lambda n, s, dt: c2.enter_context(nc.sbuf_tensor(n, s, dt))
        acc = lsb("f2_acc", [128, 8 * D], F32)
        w2h = [lsb(f"f2_w{i}", [128, 11 * 512], BF16) for i in range(3)]
        hb = [lsb(f"f2_h{i}", [128, 11 * 256], BF16) for i in range(3)]
        gst = [lsb(f"f2_g{i}", [128, D], BF16) for i in range(2)]
        pf = [c2.enter_context(nc.psum_tensor(f"f2_p{i}", [128, 512], F32)) for i in range(4)]

        wload = {}
        hload = {}
        w_used = {}
        h_used = {}
        bank_free = {}
        acc_mark = {}
        silu_done = None

        widx = 0
        hidx = 0
        for th in range(2):
            for e in range(NE):
                for nbq in range(4):
                    wbuf = w2h[widx % 3]
                    war = w_used.get(widx - 3)

                    @blk.gpsimd
                    def _(gq, widx=widx, e=e, nbq=nbq, wbuf=wbuf, war=war):
                        if widx < 2:
                            for s, v in E["f1_spill_finals"]:
                                gq.wait_ge(s.h, v)
                            if "dbg_xn_done" in E:
                                gq.wait_ge(E["dbg_xn_done"][0].h, E["dbg_xn_done"][1])
                        if war is not None:
                            gq.wait_ge(sP.h, war)
                        src = w2[e, :, nbq * 512:(nbq + 1) * 512] if e < 8 else \
                            sw2[:, nbq * 512:(nbq + 1) * 512]
                        sW2[widx % 3].mark(gq.dma_start(
                            out=wbuf[:].rearrange("p (kc n) -> p kc n", kc=11),
                            in_=src.rearrange("(kc p) n -> p kc n", p=128)), 16)
                    wload[widx] = sW2[widx % 3].v

                    for mb in range(4):
                        hbuf = hb[hidx % 3]
                        hwar = h_used.get(hidx - 3)

                        @blk.sync
                        def _(s, hidx=hidx, th=th, e=e, mb=mb, hbuf=hbuf, hwar=hwar):
                            if hidx < 2:
                                for st, v in E["f1_spill_finals"]:
                                    s.wait_ge(st.h, v)
                            if hwar is not None:
                                s.wait_ge(sP.h, hwar)
                            c0 = th * 1024 + mb * 256
                            sHB[hidx % 3].mark(s.dma_start(
                                out=hbuf[:].rearrange("p (kc t) -> p kc t", kc=11),
                                in_=h_sp[e * FH:(e + 1) * FH, c0:c0 + 256].rearrange("(kc p) t -> p kc t", p=128)), 16)
                        hload[hidx] = sHB[hidx % 3].v

                        for ms in range(2):
                            bk = (mb % 2) * 2 + ms
                            pwar = bank_free.get(bk)

                            @blk.tensor
                            def _(t, widx=widx, hidx=hidx, ms=ms, bk=bk, pwar=pwar,
                                  wbuf=wbuf, hbuf=hbuf):
                                t.wait_ge(sW2[widx % 3].h, wload[widx])
                                t.wait_ge(sHB[hidx % 3].h, hload[hidx])
                                if pwar is not None:
                                    t.wait_ge(sV.h, pwar)
                                for kc in range(11):
                                    mi = t.matmul(pf[bk][:], hbuf[:, kc * 256 + ms * 128: kc * 256 + (ms + 1) * 128],
                                                  wbuf[:, kc * 512:(kc + 1) * 512],
                                                  start=(kc == 0), stop=(kc == 10))
                                sP.mark(mi)
                            mm_ok = sP.v
                            w_used[widx] = sP.v
                            h_used[hidx] = sP.v

                            j = mb * 2 + ms
                            asl = slice(j * D + nbq * 512, j * D + (nbq + 1) * 512)
                            mtile = th * 8 + j
                            first = e == 0

                            @blk.vector
                            def _(v, e=e, bk=bk, asl=asl, mtile=mtile, nbq=nbq, first=first,
                                  mm_ok=mm_ok, sd=silu_done, am=acc_mark.get((mtile, nbq))):
                                v.wait_ge(sP.h, mm_ok)
                                if first:
                                    if sd is not None:
                                        v.wait_ge(sA.h, sd)
                                    v.wait_ge(sV.h, E["cmb_ready_V"])
                                    if mtile == 0 and "dbg_xn_done" in E:
                                        v.wait_ge(E["dbg_xn_done"][0].h, E["dbg_xn_done"][1])
                                else:
                                    v.wait_ge(sV.h, am)
                                if e < 8:
                                    col = cmb[:, mtile * 8 + e: mtile * 8 + e + 1]
                                    if first:
                                        sV.mark(v.tensor_scalar(acc[:, asl], pf[bk][:], col, None, OP.mult))
                                    else:
                                        sV.mark(v.scalar_tensor_tensor(acc[:, asl], pf[bk][:], col,
                                                                       acc[:, asl], OP.mult, OP.add))
                                else:
                                    sV.mark(v.scalar_tensor_tensor(acc[:, asl], pf[bk][:], 1.0,
                                                                   acc[:, asl], OP.mult, OP.add))
                            bank_free[bk] = sV.v
                            acc_mark[(mtile, nbq)] = sV.v
                        hidx += 1
                    widx += 1
            acc_ok = sV.v
            gsnap = [sGS[0].v, sGS[1].v]

            @blk.scalar
            def _(sc, th=th, acc_ok=acc_ok, gsnap=gsnap):
                sc.wait_ge(sV.h, acc_ok)
                for j in range(8):
                    if j >= 2:
                        sc.wait_ge(sGS[j % 2].h, gsnap[j % 2] + 16 * ((j - 2) // 2 + 1))
                    elif th:
                        sc.wait_ge(sGS[j % 2].h, gsnap[j % 2])
                    sA.mark(sc.activation(gst[j % 2][:], acc[:, j * D:(j + 1) * D], AF.Silu))

            @blk.sync
            def _(s, th=th, base=sA.v - 8):
                for j in range(8):
                    s.wait_ge(sA.h, base + j + 1)
                    sGS[j % 2].mark(s.dma_start(
                        out=g_sp[:, (th * 8 + j) * D:(th * 8 + j + 1) * D], in_=gst[j % 2][:]), 16)
            silu_done = sA.v
    E["g_ready_A"] = sA.v
    E["g_sp_finals"] = [(sGS[0], sGS[0].v), (sGS[1], sGS[1].v)]


def _ssd(E):
    nc, blk = E["nc"], E["blk"]
    sP, sV, sA = E["sP"], E["sV"], E["sA"]
    bfm, cfm, dta, dpb, nwb, i128, i64f = (E[k] for k in ("bfm", "cfm", "dta", "dpb", "nwb", "i128", "i64f"))
    ones64, mask128 = E["ones64"], E["mask128"]
    zs_sp, x_sp, yn_sp, cs_sp = E["zs_sp"], E["x_sp"], E["yn_sp"], E["cs_sp"]
    sXT = mkstream(E, "sXT")
    sZT = mkstream(E, "sZT")
    sCW = mkstream(E, "sCW")
    sBC = [mkstream(E, f"sBC{i}") for i in range(2)]
    sYN = mkstream(E, "sYN")
    sWH0 = [mkstream(E, f"sWH0{i}") for i in range(2)]
    sGT = [mkstream(E, f"sGT{i}") for i in range(2)]
    sXI = [mkstream(E, f"sXI{i}") for i in range(2)]
    sOB2 = [mkstream(E, f"sOB2{i}") for i in range(2)]
    whh, g_sp, x_in, out, wout_t = E["whh"], E["g_sp"], E["x_in"], E["out"], E["wout_t"]
    sG = Sem(E["nc"], E["ctx"], "sG")

    with ExitStack() as c2:
        lsb = lambda n, s, dt: c2.enter_context(nc.sbuf_tensor(n, s, dt))
        lps = lambda n: c2.enter_context(nc.psum_tensor(n, [128, 512], F32))
        gt2 = [lsb(f"ss_gt{i}", [128, 512], BF16) for i in range(2)]
        xt2 = [lsb(f"ss_xt{i}", [128, 512], F32) for i in range(2)]
        xtm = lsb("ss_x", [128, DI], BF16)
        zst = lsb("ss_z", [128, DI], BF16)
        Sbf = lsb("ss_Sbf", [128, DI], BF16)
        xdp = lsb("ss_xdp", [128, DI], BF16)
        bca = [lsb(f"ss_bc{i}", [128, 16 * CH], F32) for i in range(2)]
        ebuf = [lsb(f"ss_eb{i}", [128, 16 * CH], BF16) for i in range(4)]
        pt = [lsb(f"ss_pt{i}", [128, 16 * CH], BF16) for i in range(4)]
        xts = [lsb(f"ss_xts{i}", [128, 16 * 64], BF16) for i in range(4)]
        csf = lsb("ss_csf", [NH, CH], F32)
        zcs = lsb("ss_zcs", [NH, CH], F32)
        cst = lsb("ss_cst", [128, NH], F32)
        diag64 = lsb("ss_dg", [NH, NH], F32)
        expc = lsb("ss_expc", [128, NH], F32)
        eto = lsb("ss_eto", [128, NH], F32)
        btm = lsb("ss_btm", [128, CH], BF16)
        sctm = lsb("ss_sct", [128, CH], BF16)
        ybuf = lsb("ss_y", [128, DI], BF16)
        ytmp = lsb("ss_yt", [128, 512], F32)
        stmp = lsb("ss_st", [128, 512], F32)
        ybF = ybuf[:].bitcast(F32)
        ob2 = [ybF[:, 0:512], ybF[:, 512:1024]]
        t1w = ybF[:, 1024:1536]
        ssq = lsb("ss_ssq", [128, 4], F32)
        rstd = lsb("ss_rstd", [128, 1], F32)
        ynb = lsb("ss_ynb", [128, DI], BF16)
        dT = [ynb[:].bitcast(F32)] * 2
        psm = [lps(f"ss_sm{i}") for i in range(2)]
        pY = [lps(f"ss_pY{i}") for i in range(2)]
        pYo = [lps(f"ss_pYo{i}") for i in range(2)]
        pSt = [lps(f"ss_pSt{i}") for i in range(2)]

        @blk.vector
        def _(v):
            v.wait_ge(sA.h, E["g_ready_A"])
            for st, vv in E.get("g_sp_finals", []):
                v.wait_ge(st.h, vv)
            ics, icv = E["init_loads"]
            v.wait_ge(ics.h, icv)
            v.memset(Sbf[:], 0.0)
            sV.mark(v.memset(zcs[:], 0.0))
        init_ok = sV.v

        @blk.gpsimd
        def _(gq):
            gq.wait_ge(sA.h, E["g_ready_A"])
            for st, vv in E.get("g_sp_finals", []):
                gq.wait_ge(st.h, vv)
            for pc in range(2):
                sWH0[pc].mark(gq.dma_start(
                    out=whh[:, pc * 16384:(pc + 1) * 16384].rearrange("p (kc n) -> p kc n", kc=32),
                    in_=wout_t[:, pc * 512:(pc + 1) * 512].rearrange("(kc p) n -> p kc n", p=128)), 16)

        ebank = {}
        x_free = None
        z_free = None
        yn_free = {}
        bc_free = {}
        pt_free = {}
        bank_mm_g = {}
        sbf_war = None
        btl_prev = {}
        scale_ok = None
        cs_wr = None
        csf_rd = None
        cst_evac = None
        btm_evac = None
        sct_evac = None
        ynf_evac = None
        bcasts = {}

        st_pro = {}

        def xdp_pool(c):
            @blk.gpsimd
            def _(gq, c=c, x_ld=st_pro["x_ld"], pml=st_pro.get("piece_mm"),
                  ynw=yn_free.get("b")):
                gq.wait_ge(sXT.h, x_ld)
                if pml is not None:
                    gq.wait_ge(sP.h, pml)
                if ynw is not None:
                    gq.wait_ge(sYN.h, ynw)
                if c == 0:
                    gq.wait_ge(sA.h, E["dpbbf_ok"])
                    gq.wait_ge(sV.h, init_ok)
                sG.mark(gq.tensor_tensor(xdp[:].rearrange("p (h x) -> p h x", h=NH),
                                         xtm[:].rearrange("p (h x) -> p h x", h=NH),
                                         E["dpbbf"][:].unsqueeze(2).broadcast_to([128, NH, 64]),
                                         OP.mult))
            st_pro[("xdp_ok", c)] = sG.v

        def sched_cs(c):
            nonlocal cs_wr, csf_rd

            @blk.vector
            def _(v, c=c, cw=cs_wr, cr=csf_rd, em=st_pro.get("eto_mm"),
                  dgp=st_pro.get(("dg_ok", c - 1))):
                if c == 0:
                    v.wait_ge(sV.h, E["dta_ready_V"])
                    v.wait_ge(sV.h, init_ok)
                else:
                    v.wait_ge(sCW.h, cw)
                    v.wait_ge(sP.h, cr)
                    if dgp is not None:
                        v.wait_ge(sV.h, dgp)
                sV.mark(v.tensor_tensor_scan(csf[:], dta[:, c * CH:(c + 1) * CH], zcs[:],
                                             0.0, OP.add, OP.add))
                v.wait_ge(sV.h, sV.v)
                if em is not None:
                    v.wait_ge(sP.h, em)
                sV.mark(v.tensor_scalar(diag64[:], i64f[:], csf[:, CH - 1:CH], None, OP.mult))
            st_pro[("cs_ok", c)] = sV.v - 1
            st_pro[("dg_ok", c)] = sV.v

            @blk.gpsimd
            def _(gq, c=c, cs_ok=st_pro[("cs_ok", c)], bcw=bcasts.get(c - 2)):
                gq.wait_ge(sV.h, cs_ok)
                if bcw is not None:
                    for s_, v_ in bcw:
                        gq.wait_ge(s_.h, v_)
                sCW.mark(gq.dma_start(out=cs_sp[(c % 2) * NH:(c % 2 + 1) * NH, :], in_=csf[:]), 16)
            cs_wr = sCW.v
            st_pro[("cs_wr", c)] = sCW.v

        def bcast01(c):
            for g in range(2):
                gp = g % 2

                @blk.gpsimd
                def _(gq, c=c, g=g, gp=gp, bcf=bc_free.get(gp), cw=st_pro[("cs_wr", c)]):
                    if c == 0:
                        gq.wait_ge(sA.h, E["g_ready_A"])
                    if bcf is not None:
                        gq.wait_ge(sA.h, bcf)
                    gq.wait_ge(sCW.h, cw)
                    sBC[gp].mark(gq.dma_start(
                        out=bca[gp][:].rearrange("p (h l) -> p h l", h=16),
                        in_=cs_sp[(c % 2) * NH + g * 16:(c % 2) * NH + (g + 1) * 16, :]
                        .unsqueeze(0).broadcast_to([128, 16, CH])), 16)
                st_pro[("bc_ok", g)] = sBC[gp].v

        def prologue(c):
            t0 = c * CH
            nonlocal cst_evac, btm_evac, sct_evac, csf_rd
            x_free = st_pro.get("x_free")
            z_free = st_pro.get("z_free")

            @blk.sync
            def _(s, c=c, t0=t0, xf=x_free, zf=z_free, xtl=st_pro.get("xts_last")):
                if xf is not None:
                    s.wait_ge(sP.h, xf)
                if xtl is not None:
                    s.wait_ge(sG.h, xtl)
                if zf is not None:
                    s.wait_ge(sV.h, zf)
                if c == 0:
                    for st, v in E["inproj_spill_finals"]:
                        s.wait_ge(st.h, v)
                    s.wait_ge(sA.h, E["g_ready_A"])
                    for st, v in E.get("g_sp_finals", []):
                        s.wait_ge(st.h, v)
                sXT.mark(s.dma_start_transpose(xtm[:], x_sp[:, t0:t0 + CH]), 16)
                sZT.mark(s.dma_start_transpose(zst[:], zs_sp[:, t0:t0 + CH]), 16)
            st_pro["x_ld"] = sXT.v
            st_pro["z_ld"] = sZT.v

            if c == 0:
                sched_cs(0)
                bcast01(0)
                xdp_pool(0)
            cs_ok = st_pro[("cs_ok", c)]
            st_pro["dg_ok"] = st_pro[("dg_ok", c)]

            @blk.tensor
            def _(t, c=c, cs_ok=cs_ok, so=scale_ok):
                t.wait_ge(sV.h, cs_ok)
                if c == 0:
                    t.wait_ge(sA.h, E["g_ready_A"])
                else:
                    t.wait_ge(sA.h, so)
                sP.mark(t.matmul(psm[0][0:128, 0:NH], csf[:], i64f[:], start=True, stop=True))
            csf_rd = sP.v
            cst_mm = sP.v

            @blk.scalar
            def _(sc, c=c, cst_mm=cst_mm, el=st_pro.get("evac_last"),
                  bl=st_pro.get("bank_last")):
                sc.wait_ge(sP.h, cst_mm)
                if el is not None:
                    sc.wait_ge(sV.h, el)
                sA.mark(sc.activation(cst[:], psm[0][:, 0:NH], AF.Copy))
                sc.wait_ge(sA.h, sA.v)
                sA.mark(sc.activation(expc[:], cst[:], AF.Exp))
            cst_evac = sA.v - 1
            st_pro["cst_evac"] = cst_evac
            expc_ok = sA.v

            @blk.tensor
            def _(t, c=c, t0=t0, ce=cst_evac):
                t.wait_ge(sA.h, ce)
                if c == 0:
                    t.wait_ge(sA.h, E["bc_ready_A"])
                sP.mark(t.matmul(psm[0][:, 256:384], bfm[:, t0:t0 + CH], i128[:], start=True, stop=True))
            btm_mm = sP.v

            @blk.scalar
            def _(sc, btm_mm=btm_mm, bl=st_pro.get("bank_last")):
                sc.wait_ge(sP.h, btm_mm)
                if bl is not None:
                    sc.wait_ge(sP.h, bl)
                sA.mark(sc.activation(btm[:], psm[0][:, 256:384], AF.Copy))
            btm_evac = sA.v
            st_pro["btm_evac"] = btm_evac

            @blk.tensor
            def _(t, t0=t0, be=btm_evac):
                t.wait_ge(sA.h, be)
                sP.mark(t.matmul(psm[0][:, 128:256], bfm[:, t0:t0 + CH], cfm[:, t0:t0 + CH],
                                 start=True, stop=True))
            sct_mm = sP.v

            @blk.vector
            def _(v, sct_mm=sct_mm):
                v.wait_ge(sP.h, sct_mm)
                sV.mark(v.tensor_tensor(sctm[:], psm[0][:, 128:256], mask128[:], OP.mult))
            sct_evac = sV.v
            st_pro["sct_evac"] = sV.v

            @blk.tensor
            def _(t, dg_ok=st_pro["dg_ok"], se=sct_evac):
                t.wait_ge(sV.h, dg_ok)
                t.wait_ge(sV.h, se)
                sP.mark(t.matmul(psm[0][:, 64:128], ones64[:], diag64[:], start=True, stop=True))
            st_pro["eto_mm"] = sP.v

            @blk.scalar
            def _(sc, eto_mm=st_pro["eto_mm"], el=st_pro.get("evac_last")):
                if el is not None:
                    sc.wait_ge(sV.h, el)
                sc.wait_ge(sP.h, eto_mm)
                sA.mark(sc.activation(eto[:], psm[0][:, 64:128], AF.Exp))
            st_pro["scale_ok"] = sA.v
            st_pro["expc_ok"] = expc_ok

        def main(c):
            nonlocal sbf_war, btl_prev, scale_ok, x_free_out
            scale_ok = st_pro["scale_ok"]
            t0 = c * CH
            x_ld = st_pro["x_ld"]
            z_ld = st_pro["z_ld"]

            if c + 1 < NCH:
                sched_cs(c + 1)
            xdp_ok = st_pro[("xdp_ok", c)]
            cst_ev = st_pro["cst_evac"]
            btm_ev = st_pro["btm_evac"]
            sct_ev = st_pro["sct_evac"]
            eto_mm = st_pro["eto_mm"]
            pt_oks = {}
            btl_oks = {}
            d_oks = {}
            e_oks = {}

            for g in range(4):
                gp = g % 2
                if g >= 2:
                    @blk.gpsimd
                    def _(gq, c=c, g=g, gp=gp, bcf=e_oks[g - 2], cw=st_pro[("cs_wr", c)]):
                        gq.wait_ge(sA.h, bcf)
                        gq.wait_ge(sCW.h, cw)
                        sBC[gp].mark(gq.dma_start(
                            out=bca[gp][:].rearrange("p (h l) -> p h l", h=16),
                            in_=cs_sp[(c % 2) * NH + g * 16:(c % 2) * NH + (g + 1) * 16, :]
                            .unsqueeze(0).broadcast_to([128, 16, CH])), 16)
                    st_pro[("bc_ok", g)] = sBC[gp].v
                    bcasts[c] = [(sBC[0], sBC[0].v), (sBC[1], sBC[1].v)]

                @blk.vector
                def _(v, g=g, gp=gp, bc_ok=st_pro[("bc_ok", g)], cst_ev=cst_ev,
                      ef=e_oks.get(g - 2), dw=d_oks.get(g - 1),
                      tpl=st_pro.get("tp_last") if g == 0 else None):
                    v.wait_ge(sBC[gp].h, bc_ok)
                    if tpl is not None:
                        v.wait_ge(sP.h, tpl)
                    v.wait_ge(sA.h, cst_ev)
                    if ef is not None:
                        v.wait_ge(sA.h, ef)
                    if dw is not None:
                        v.wait_ge(sV.h, dw)
                    sV.mark(v.tensor_tensor(
                        dT[0].rearrange("p (h l) -> p h l", h=16),
                        bca[gp][:].rearrange("p (h l) -> p h l", h=16),
                        cst[:, g * 16:(g + 1) * 16].unsqueeze(2).broadcast_to([128, 16, CH]),
                        OP.subtract))
                    v.wait_ge(sV.h, sV.v)
                    sV.mark(v.tensor_scalar(bca[gp][:], dT[0], 0.0, None, OP.min))
                d_oks[g] = sV.v

                @blk.scalar
                def _(sc, c=c, g=g, gp=gp, d_ok=d_oks[g],
                      ew=btl_prev.get(g) if btl_prev else None):
                    sc.wait_ge(sV.h, d_ok)
                    if ew is not None:
                        sc.wait_ge(sV.h, ew[0])
                        sc.wait_ge(sG.h, ew[1])
                    sA.mark(sc.activation(ebuf[g][:], bca[gp][:], AF.Exp))
                e_oks[g] = sA.v
                bc_free[gp] = sA.v

            xts_oks = {}
            for g in range(4):
                gp = g % 2

                @blk.vector
                def _(v, g=g, gp=gp, e_ok=e_oks[g], sct_ev=sct_ev, ptf=pt_free.get(g)):
                    v.wait_ge(sA.h, e_ok)
                    v.wait_ge(sV.h, sct_ev)
                    if ptf is not None:
                        v.wait_ge(sP.h, ptf)
                    sV.mark(v.tensor_tensor(pt[g][:].rearrange("p (h l) -> p h l", h=16),
                                            ebuf[g][:].rearrange("p (h l) -> p h l", h=16),
                                            sctm[:].unsqueeze(1).broadcast_to([128, 16, CH]),
                                            OP.mult))
                pt_oks[g] = sV.v

                @blk.gpsimd
                def _(gq, g=g, e_ok=e_oks[g], ptf=pt_free.get(g)):
                    gq.wait_ge(sA.h, e_ok)
                    if ptf is not None:
                        gq.wait_ge(sP.h, ptf)
                    sG.mark(gq.tensor_tensor(
                        xts[g][:].rearrange("p (h x) -> p h x", h=16),
                        xtm[:, g * 1024:(g + 1) * 1024].rearrange("p (h x) -> p h x", h=16),
                        ebuf[g][:, CH - 1::CH].unsqueeze(2).broadcast_to([128, 16, 64]),
                        OP.mult))
                xts_oks[g] = sG.v
                btl_oks[g] = (sV.v, sG.v)
            st_pro["xts_last"] = sG.v

            if c + 1 < NCH:
                bcast01(c + 1)

            if c > 0:
                pieces_evac(c - 1)

            @blk.gpsimd
            def _(gq, c=c, pw0=st_pro.get(("pev", 0)), pw1=st_pro.get(("pev", 1))):
                if c == 0:
                    for st, vv in E.get("g_sp_finals", []):
                        gq.wait_ge(st.h, vv)
                for pc, pw in ((0, pw0), (1, pw1)):
                    if pw is not None:
                        gq.wait_ge(sV.h, pw)
                    sGT[pc].mark(gq.dma_start(
                        out=gt2[pc][:], in_=g_sp[:, c * D + pc * 512:c * D + (pc + 1) * 512]), 16)
                    sXI[pc].mark(gq.dma_start(
                        out=xt2[pc][:], in_=x_in[c * CH:(c + 1) * CH, pc * 512:(pc + 1) * 512]), 16)
            st_pro[("gt_ok", 0)] = sGT[0].v
            st_pro[("gt_ok", 1)] = sGT[1].v
            st_pro[("xi_ok", 0)] = sXI[0].v
            st_pro[("xi_ok", 1)] = sXI[1].v

            for b in range(8):
                g = b // 2
                bi = b % 2

                @blk.tensor
                def _(t, c=c, g=g, b=b, bi=bi, x_ld=x_ld, btl_ok=btl_oks[g],
                      btm_ev=btm_ev, sw=sbf_war, ew=ebank.get(bi), xdp_ok=xdp_ok,
                      pev=st_pro.get(("pev", b)) if b < 2 else None):
                    t.wait_ge(sV.h, btl_ok[0])
                    t.wait_ge(sG.h, btl_ok[1])
                    t.wait_ge(sXT.h, x_ld)
                    t.wait_ge(sG.h, xdp_ok)
                    if pev is not None:
                        t.wait_ge(sV.h, pev)
                    if sw is not None:
                        t.wait_ge(sV.h, sw)
                    if ew is not None:
                        t.wait_ge(sV.h, ew)
                    t.matmul(pY[bi][:], i128[:], xdp[:, b * 512:(b + 1) * 512],
                             start=True, stop=False)
                    for hh in range(8):
                        h = b * 8 + hh
                        hl = (h % 16) * CH
                        t.matmul(pY[bi][:, hh * 64:(hh + 1) * 64],
                                 pt[g][:, hl:hl + CH],
                                 xtm[:, h * 64:(h + 1) * 64], start=False, stop=(hh == 7))
                    for hh in range(8):
                        h = b * 8 + hh
                        t.matmul(pYo[bi][:, hh * 64:(hh + 1) * 64],
                                 cfm[:, c * CH:(c + 1) * CH],
                                 Sbf[:, h * 64:(h + 1) * 64], start=True, stop=True)
                    t.wait_ge(sA.h, btm_ev)
                    for hh in range(8):
                        h = b * 8 + hh
                        hx = (h % 16) * 64
                        mi = t.matmul(pSt[bi][:, hh * 64:(hh + 1) * 64],
                                      btm[:],
                                      xts[g][:, hx:hx + 64], start=True, stop=True)
                    sP.mark(mi)
                bank_mm = sP.v
                if b == 7:
                    st_pro["x_free"] = sP.v
                    st_pro["bank_last"] = sP.v
                pt_free[g] = sP.v

                @blk.vector
                def _(v, b=b, bi=bi, bank_mm=bank_mm, so=scale_ok, z_ld=z_ld,
                      obs=(st_pro.get(("ob_st", 0)), st_pro.get(("ob_st", 1)))):
                    v.wait_ge(sP.h, bank_mm)
                    v.wait_ge(sA.h, so)
                    if b == 0:
                        for pc in range(2):
                            if obs[pc] is not None:
                                v.wait_ge(sOB2[pc].h, obs[pc])
                    ecv = expc[:, b * 8:(b + 1) * 8].unsqueeze(2).broadcast_to([128, 8, 64])
                    sV.mark(v.tensor_tensor(ytmp[:].rearrange("p (h x) -> p h x", h=8),
                                            pYo[bi][:].rearrange("p (h x) -> p h x", h=8), ecv, OP.mult))
                    v.wait_ge(sZT.h, z_ld)
                    v.wait_ge(sV.h, sV.v)
                    sV.mark(v.scalar_tensor_tensor(stmp[:], ytmp[:], 1.0, pY[bi][:], OP.mult, OP.add))
                    v.wait_ge(sV.h, sV.v)
                    sV.mark(v.tensor_tensor(ybuf[:, b * 512:(b + 1) * 512], stmp[:],
                                            zst[:, b * 512:(b + 1) * 512], OP.mult))
                    v.wait_ge(sV.h, sV.v)
                    etv = eto[:, b * 8:(b + 1) * 8].unsqueeze(2).broadcast_to([128, 8, 64])
                    sV.mark(v.tensor_tensor(stmp[:].rearrange("p (h x) -> p h x", h=8),
                                            Sbf[:, b * 512:(b + 1) * 512].rearrange("p (h x) -> p h x", h=8),
                                            etv, OP.mult))
                    v.wait_ge(sV.h, sV.v)
                    sV.mark(v.tensor_tensor(Sbf[:, b * 512:(b + 1) * 512], stmp[:], pSt[bi][:], OP.add))
                ebank[bi] = sV.v
            sbf_war = sV.v
            btl_prev = dict(btl_oks)
            st_pro["evac_last"] = sV.v
            st_pro["z_free"] = sV.v
            st_pro["y_ok"] = sV.v

        def tail(c):
            nonlocal ynf_evac
            y_ok = st_pro["y_ok"]

            @blk.scalar
            def _(sc, y_ok=y_ok, ynf=yn_free.get("b")):
                sc.wait_ge(sV.h, y_ok)
                if ynf is not None:
                    sc.wait_ge(sYN.h, ynf)
                sA.mark(sc.activation(ynb[:], ybuf[:], AF.Square, accum_out=ssq[:, 0:1]))
                sc.wait_ge(sA.h, sA.v)
                sA.mark(sc.activation(ssq[:, 3:4], ssq[:, 0:1], AF.Ln, scale=1.0 / DI, bias=E["epsc"][:]))
                sc.wait_ge(sA.h, sA.v)
                sA.mark(sc.activation(rstd[:], ssq[:, 3:4], AF.Exp, scale=-0.5))
            rstd_ok = sA.v

            @blk.vector
            def _(v, rstd_ok=rstd_ok):
                v.wait_ge(sA.h, rstd_ok)
                sV.mark(v.scalar_tensor_tensor(ynb[:], ybuf[:], rstd[:], nwb[:], OP.mult, OP.mult))
            yn_ok = sV.v

            sPb, sAb = sP.v, sA.v
            for grp in range(8):
                pb = psm[1] if grp % 2 == 0 else pY[0]

                @blk.tensor
                def _(t, grp=grp, pb=pb, yn_ok=yn_ok, sAb=sAb):
                    t.wait_ge(sV.h, yn_ok)
                    if grp >= 2:
                        t.wait_ge(sA.h, sAb + grp - 1)
                    for j in range(4):
                        mi = t.matmul(pb[:, j * 128:(j + 1) * 128],
                                      ynb[:, (grp * 4 + j) * 128:(grp * 4 + j + 1) * 128],
                                      i128[:], start=(j == 0), stop=(j == 3))
                    sP.mark(mi)
                if grp == 7:
                    st_pro["tp_last"] = sP.v

            @blk.scalar
            def _(sc, c=c, sPb=sPb, ynw=yn_free.get("b")):
                for grp in range(8):
                    sc.wait_ge(sP.h, sPb + grp + 1)
                    if grp == 0 and ynw is not None:
                        sc.wait_ge(sYN.h, ynw)
                    pb = psm[1] if grp % 2 == 0 else pY[0]
                    sA.mark(sc.activation(xdp[:, grp * 512:(grp + 1) * 512], pb[:], AF.Copy))
            ynf_evac = sA.v

            @blk.sync
            def _(s, c=c, ye=ynf_evac):
                s.wait_ge(sA.h, ye)
                sYN.mark(s.dma_start(
                    out=yn_sp[:, c * CH:(c + 1) * CH].rearrange("(kc p) t -> p kc t", p=128),
                    in_=xdp[:].rearrange("p (kc t) -> p kc t", kc=32)), 16)
            yn_free["b"] = sYN.v

        def pieces_mm(c):
            ye = ynf_evac
            for pc in range(2):
                @blk.tensor
                def _(t, c=c, pc=pc, ye=ye, ebk=ebank.get(pc)):
                    t.wait_ge(sA.h, ye)
                    if ebk is not None:
                        t.wait_ge(sV.h, ebk)
                    if c == 0:
                        t.wait_ge(sWH0[pc].h, 16)
                    for kc in range(32):
                        mi = t.matmul(pSt[pc][:],
                                      xdp[:, kc * 128:(kc + 1) * 128],
                                      whh[:, pc * 16384 + kc * 512:pc * 16384 + (kc + 1) * 512],
                                      start=(kc == 0), stop=(kc == 31))
                    sP.mark(mi)
                st_pro[("pmm", pc)] = sP.v
            st_pro["piece_mm"] = sP.v
            if c + 1 < NCH:
                xdp_pool(c + 1)

        def pieces_evac(cc):
            for pc in range(2):
                @blk.vector
                def _(v, cc=cc, pc=pc, p_mm=st_pro[("pmm", pc)],
                      gt_ok=st_pro[("gt_ok", pc)], xi_ok=st_pro[("xi_ok", pc)],
                      obw=st_pro.get(("ob_st", pc)),
                      pev0=st_pro.get(("pev", 0)) if pc == 1 else None):
                    v.wait_ge(sP.h, p_mm)
                    v.wait_ge(sGT[pc].h, gt_ok)
                    v.wait_ge(sXI[pc].h, xi_ok)
                    if obw is not None:
                        v.wait_ge(sOB2[pc].h, obw)
                    if pev0 is not None:
                        v.wait_ge(sV.h, pev0)
                    sV.mark(v.tensor_tensor(t1w[:], pSt[pc][:], gt2[pc][:], OP.mult))
                    v.wait_ge(sV.h, sV.v)
                    sV.mark(v.tensor_tensor(ob2[pc][:], t1w[:], xt2[pc][:], OP.add))
                st_pro[("pev", pc)] = sV.v
                p_ev = sV.v

                @blk.sync
                def _(s, cc=cc, pc=pc, p_ev=p_ev):
                    s.wait_ge(sV.h, p_ev)
                    sOB2[pc].mark(s.dma_start(
                        out=out[cc * CH:(cc + 1) * CH, pc * 512:(pc + 1) * 512], in_=ob2[pc][:]), 16)
                st_pro[("ob_st", pc)] = sOB2[pc].v

        x_free_out = None
        prologue(0)
        for c in range(NCH):
            main(c)
            if c + 1 < NCH:
                prologue(c + 1)
            tail(c)
            pieces_mm(c)
        pieces_evac(NCH - 1)
    E["ssd_spill_final"] = (sYN, sYN.v)
    E["ssd_pe_final"] = E["sP"].v
    E["ssd_v_final"] = E["sV"].v
    E["ssd_a_final"] = E["sA"].v
    E["ssd_ob_finals"] = [(sOB2[i], sOB2[i].v) for i in range(2)]


def _wout(E):
    nc, blk = E["nc"], E["blk"]
    sP, sV, sA = E["sP"], E["sV"], E["sA"]
    g_sp, x_in, out, yn_sp, wout_t = E["g_sp"], E["x_in"], E["out"], E["yn_sp"], E["wout_t"]
    sWH = mkstream(E, "sWH")
    sYF = [mkstream(E, f"sYF{i}") for i in range(2)]
    sWX = [mkstream(E, f"sWX{i}") for i in range(4)]
    sGL = [mkstream(E, f"sGL{i}") for i in range(4)]
    sOB = [mkstream(E, f"sOB{i}") for i in range(4)]

    with ExitStack() as c2:
        lsb = lambda n, s, dt: c2.enter_context(nc.sbuf_tensor(n, s, dt))
        wh = [E["whh"]] * 2
        ynf = [lsb(f"wo_yn{i}", [128, 32 * 256], BF16) for i in range(2)]
        xt = [lsb(f"wo_x{i}", [128, 512], F32) for i in range(4)]
        gtl = [lsb(f"wo_g{i}", [128, 512], BF16) for i in range(4)]
        ob = [lsb(f"wo_o{i}", [128, 512], F32) for i in range(4)]
        t1 = [lsb(f"wo_t{i}", [128, 512], F32) for i in range(2)]
        pw = [c2.enter_context(nc.psum_tensor(f"wo_p{i}", [128, 512], F32)) for i in range(4)]

        bank_free = {}
        wh_rd = None
        ynf_rd = {}
        xt_free = {}
        ob_free = {}
        ii = 0
        syn, syv = E["ssd_spill_final"]

        def ssd_guards(q):
            q.wait_ge(syn.h, syv)
            q.wait_ge(sP.h, E["ssd_pe_final"])
            q.wait_ge(sV.h, E["ssd_v_final"])
            q.wait_ge(sA.h, E["ssd_a_final"])
            for st, v in E["ssd_ob_finals"]:
                q.wait_ge(st.h, v)

        for nh in (1,):
            whb = wh[0]

            @blk.gpsimd
            def _(gq):
                ssd_guards(gq)
                sWH.mark(gq.dma_start(
                    out=whb[:].rearrange("p (kc n) -> p kc n", kc=32),
                    in_=wout_t[:, 1024:2048].rearrange("(kc p) n -> p kc n", p=128)), 16)
            wh_ok = (sWH, sWH.v)

            for blk8 in range(8):
                yi = blk8 % 2
                ybuf2 = ynf[yi]
                yrd = ynf_rd.get(yi)
                @blk.sync
                def _(s, nh=nh, blk8=blk8, yi=yi, ybuf2=ybuf2, yrd=yrd):
                    if yrd is not None:
                        s.wait_ge(sP.h, yrd)
                    if blk8 < 2:
                        ssd_guards(s)
                    sYF[yi].mark(s.dma_start(
                        out=ybuf2[:].rearrange("p (kc t) -> p kc t", kc=32),
                        in_=yn_sp[:, blk8 * 256:(blk8 + 1) * 256].rearrange("(kc p) t -> p kc t", p=128)), 16)
                yn_ok0 = sYF[yi].v

                for ms in range(2):
                    for nb in range(2):
                        bi = ii % 4
                        pwar = bank_free.get(bi)
                        xwar = xt_free.get(bi)

                        mtile0 = blk8 * 2 + ms
                        gsl0 = slice(mtile0 * D + nh * 1024 + nb * 512,
                                     mtile0 * D + nh * 1024 + (nb + 1) * 512)

                        @blk.sync
                        def _(s, ii=ii, nh=nh, blk8=blk8, ms=ms, nb=nb, bi=bi, xwar=xwar, gsl0=gsl0):
                            if ii < 4:
                                ssd_guards(s)
                                for st, v in E["g_sp_finals"]:
                                    s.wait_ge(st.h, v)
                            if xwar is not None:
                                s.wait_ge(sV.h, xwar)
                            r0 = blk8 * 256 + ms * 128
                            c0 = nh * 1024 + nb * 512
                            sWX[bi].mark(s.dma_start(out=xt[bi][:], in_=x_in[r0:r0 + 128, c0:c0 + 512]), 16)
                            sGL[bi].mark(s.dma_start(out=gtl[bi][:], in_=g_sp[:, gsl0]), 16)
                        x_ok = sWX[bi].v
                        g_ok = sGL[bi].v

                        @blk.tensor
                        def _(t, ii=ii, yi=yi, ms=ms, nb=nb, bi=bi, wh_ok=wh_ok, whb2=whb,
                              yn_ok0=yn_ok0, ybuf2=ybuf2, pwar=pwar):
                            if ii == 0:
                                t.wait_ge(syn.h, syv)
                            t.wait_ge(wh_ok[0].h, wh_ok[1])
                            t.wait_ge(sYF[yi].h, yn_ok0)
                            if pwar is not None:
                                t.wait_ge(sV.h, pwar)
                            for kc in range(32):
                                mi = t.matmul(pw[bi][:], ybuf2[:, kc * 256 + ms * 128: kc * 256 + (ms + 1) * 128],
                                              whb2[:, kc * 1024 + nb * 512: kc * 1024 + (nb + 1) * 512],
                                              start=(kc == 0), stop=(kc == 31))
                            sP.mark(mi)
                        mm_ok = sP.v
                        wh_rd = sP.v
                        ynf_rd[yi] = sP.v

                        @blk.vector
                        def _(v, ii=ii, bi=bi, mm_ok=mm_ok, x_ok=x_ok, g_ok=g_ok,
                              obw=ob_free.get(ii % 4)):
                            v.wait_ge(sP.h, mm_ok)
                            v.wait_ge(sWX[bi].h, x_ok)
                            v.wait_ge(sGL[bi].h, g_ok)
                            if ii < 4:
                                for st, vv in E["ssd_ob_finals"]:
                                    v.wait_ge(st.h, vv)
                            if obw is not None:
                                v.wait_ge(obw[0].h, obw[1])
                            sV.mark(v.tensor_tensor(t1[bi % 2][:], pw[bi][:], gtl[bi][:], OP.mult))
                            v.wait_ge(sV.h, sV.v)
                            sV.mark(v.tensor_tensor(ob[bi][:], t1[bi % 2][:], xt[bi][:], OP.add))
                        bank_free[bi] = sV.v
                        xt_free[bi] = sV.v
                        o_ok = sV.v

                        @blk.sync
                        def _(s, nh=nh, blk8=blk8, ms=ms, nb=nb, bi=bi, o_ok=o_ok):
                            s.wait_ge(sV.h, o_ok)
                            r0 = blk8 * 256 + ms * 128
                            c0 = nh * 1024 + nb * 512
                            sOB[bi].mark(s.dma_start(out=out[r0:r0 + 128, c0:c0 + 512], in_=ob[bi][:]), 16)
                        ob_free[bi] = (sOB[bi], sOB[bi].v)
                        ii += 1


def _dbg_xn(E):
    blk, sA = E["blk"], E["sA"]
    sDX = mkstream(E, "sDX")

    @blk.gpsimd
    def _(g):
        xr = E["xn_ready"]
        g.wait_ge(xr[0].h, xr[1])
        sDX.mark(g.dma_start(out=E["dbg_xn"], in_=E["xn"][:]), 16)
    E["dbg_xn_done"] = (sDX, sDX.v)


def _dbg_dump(E):
    blk, sV, sA = E["blk"], E["sV"], E["sA"]
    sDB = mkstream(E, "sDB")

    @blk.gpsimd
    def _(g):
        g.wait_ge(sV.h, sV.v)
        g.wait_ge(sA.h, sA.v)
        if "cmb_ready_V" in E:
            sDB.mark(g.dma_start(out=E["dbg_cmb"], in_=E["cmb"][:]), 16)
        if "bc_ready_A" in E:
            sDB.mark(g.dma_start(out=E["dbg_bc"][0:128, :], in_=E["bfm"][:]), 16)
            sDB.mark(g.dma_start(out=E["dbg_bc"][128:256, :], in_=E["cfm"][:]), 16)
        if "dta_ready_V" in E:
            sDB.mark(g.dma_start(out=E["dbg_dta"], in_=E["dta"][:]), 16)


def make_in_maps(u, W_in, W_conv, b_conv, dt_bias, A_log, Dp, norm_w, W_out,
                 gate_w, w1, w2, w3, sw1, sw2, sw3):
    f = lambda a: np.ascontiguousarray(np.asarray(a, dtype=np.float32))
    shared = dict(
        win_t=f(np.asarray(W_in).T),
        wconv_r=f(np.asarray(W_conv)[:, 0, :].reshape(34, 128, 4).transpose(1, 0, 2).reshape(128, 34 * 4)),
        bconv_r=f(np.asarray(b_conv).reshape(34, 128).T),
        dtb_c=f(np.asarray(dt_bias).reshape(NH, 1)),
        alog_c=f(np.asarray(A_log).reshape(NH, 1)),
        dp_row=f(np.asarray(Dp).reshape(1, NH)),
        nw_row=f(np.asarray(norm_w).reshape(1, DI)),
        wout_t=f(np.asarray(W_out).T),
        gate_t=f(np.asarray(gate_w).T),
        w1=f(w1), w2=f(w2), w3=f(w3),
        sw1=f(sw1), sw2=f(sw2), sw3=f(sw3),
        i128_in=np.eye(128, dtype=ml_dtypes.bfloat16),
        i64_in=np.eye(NH, dtype=np.float32),
        mneg_in=np.where(np.arange(128)[:, None] > np.arange(CH)[None, :],
                         np.float32(BIGNEG), np.float32(0.0)),
        ones64_in=np.ones((NH, 128), np.float32),
        mask_in=np.where(np.arange(128)[:, None] <= np.arange(CH)[None, :],
                         ml_dtypes.bfloat16(1.0), ml_dtypes.bfloat16(0.0)),
    )
    u = np.asarray(u, np.float32)
    return [dict(shared, x=np.ascontiguousarray(u[i])) for i in range(8)]


def kernel(L0_attn_u, L0_attn_W_in, L0_attn_W_conv, L0_attn_b_conv,
           L0_attn_dt_bias, L0_attn_A_log, L0_attn_Dp, L0_attn_norm_w,
           L0_attn_W_out, L0_ffn_gate_w, L0_ffn_w1, L0_ffn_w2, L0_ffn_w3,
           L0_ffn_sw1, L0_ffn_sw2, L0_ffn_sw3):
    nc = build()
    in_maps = make_in_maps(
        L0_attn_u, L0_attn_W_in, L0_attn_W_conv, L0_attn_b_conv,
        L0_attn_dt_bias, L0_attn_A_log, L0_attn_Dp, L0_attn_norm_w,
        L0_attn_W_out, L0_ffn_gate_w, L0_ffn_w1, L0_ffn_w2, L0_ffn_w3,
        L0_ffn_sw1, L0_ffn_sw2, L0_ffn_sw3)
    res = run_bass_kernel_spmd(nc, in_maps, core_ids=list(range(8)))
    return np.stack([np.asarray(r["out"], np.float32) for r in res.results], axis=0)


# version 16
# speedup vs baseline: 1.0568x; 1.0568x over previous
import numpy as np
import ml_dtypes
from contextlib import ExitStack

import concourse.bass as bass
import concourse.mybir as mybir
from concourse.bass_utils import run_bass_kernel_spmd
from concourse.alu_op_type import AluOpType as OP

F32 = mybir.dt.float32
BF16 = mybir.dt.bfloat16
U32 = mybir.dt.uint32
AF = mybir.ActivationFunctionType
AX = mybir.AxisListType

D = 2048
L = 2048
DI = 4096
NH = 64
CH = 128
NCH = L // CH
FH = 1408
NE = 9
KC = D // 128
BIGNEG = -1e9
PH_LN, PH_INPROJ, PH_GATE, PH_F1, PH_F2, PH_SSD, PH_WOUT = range(7)


class Sem:
    def __init__(self, nc, ctx, name):
        self.h = ctx.enter_context(nc.semaphore(name))
        self.v = 0

    def mark(self, inst, n=1):
        inst.then_inc(self.h, n)
        self.v += n
        return self.v


def mkstream(E, name):
    s = Sem(E["nc"], E["ctx"], name)
    E.setdefault("streams", []).append(s)
    return s


def build(max_phase=PH_WOUT, dbg=False):
    nc = bass.Bass("TRN2", target_bir_lowering=False)
    dram = lambda n, s, dt=F32, kind="ExternalInput": nc.dram_tensor(n, s, dt, kind=kind).ap()

    E = {"nc": nc, "dbg": dbg}
    E["x_in"] = dram("x", [L, D])
    E["win_t"] = dram("win_t", [D, 8512])
    E["wconv_r"] = dram("wconv_r", [128, 34 * 4])
    E["bconv_r"] = dram("bconv_r", [128, 34])
    E["dtb_c"] = dram("dtb_c", [NH, 1])
    E["alog_c"] = dram("alog_c", [NH, 1])
    E["dp_row"] = dram("dp_row", [1, NH])
    E["nw_row"] = dram("nw_row", [1, DI])
    E["wout_t"] = dram("wout_t", [DI, D])
    E["gate_t"] = dram("gate_t", [D, 8])
    E["w1"] = dram("w1", [8, D, FH])
    E["w2"] = dram("w2", [8, FH, D])
    E["w3"] = dram("w3", [8, D, FH])
    E["sw1"] = dram("sw1", [D, FH])
    E["sw2"] = dram("sw2", [FH, D])
    E["sw3"] = dram("sw3", [D, FH])
    E["i128_in"] = dram("i128_in", [128, 128], BF16)
    E["i64_in"] = dram("i64_in", [NH, NH])
    E["mneg_in"] = dram("mneg_in", [128, CH])
    E["ones64_in"] = dram("ones64_in", [NH, 128])
    E["mask_in"] = dram("mask_in", [128, CH], BF16)
    E["out"] = dram("out", [L, D], F32, kind="ExternalOutput")

    ik = "ExternalOutput" if dbg else "Internal"
    E["zs_sp"] = dram("zs_sp", [DI, L], BF16, kind=ik)
    E["x_sp"] = dram("x_sp", [DI, L], BF16, kind=ik)
    E["h_sp"] = dram("h_sp", [NE * FH, L], BF16, kind=ik)
    E["yn_sp"] = dram("yn_sp", [DI, L], BF16, kind=ik)
    E["cs_sp"] = dram("cs_sp", [2 * NH, CH], F32, kind="Internal")
    E["g_sp"] = dram("g_sp", [128, 16 * D], BF16, kind=ik)
    if dbg:
        E["dbg_bc"] = dram("dbg_bc", [256, L], F32, kind="ExternalOutput")
        E["dbg_dta"] = dram("dbg_dta", [NH, L], F32, kind="ExternalOutput")
        E["dbg_cmb"] = dram("dbg_cmb", [128, 16 * 8], F32, kind="ExternalOutput")
        E["dbg_xn"] = dram("dbg_xn", [128, KC * L], F32, kind="ExternalOutput")

    with ExitStack() as ctx:
        E["ctx"] = ctx
        sb = lambda n, s, dt: ctx.enter_context(nc.sbuf_tensor(n, s, dt))
        for n, s, dt in [
            ("i128", [128, 128], BF16), ("i64f", [NH, NH], F32),
            ("bfm", [128, L], BF16), ("cfm", [128, L], BF16),
            ("dta", [NH, L], F32), ("cmb", [128, 16 * 8], F32),
            ("wconv", [128, 34 * 4], F32), ("bconv", [128, 34], F32),
            ("dtb", [NH, 1], F32), ("nega", [NH, 1], F32), ("scrc", [NH, 1], F32),
            ("dpb", [128, NH], F32), ("dpbbf", [128, NH], BF16),
            ("nwb", [128, DI], BF16),
            ("epsc", [128, 1], F32), ("ones64", [NH, 128], F32),
            ("mask128", [128, CH], BF16),
        ]:
            E[n] = sb(n, s, dt)
        for n in ("sP", "sV", "sA"):
            E[n] = Sem(nc, ctx, n)
        E["blk"] = ctx.enter_context(nc.Block())

        _init(E)
        with ExitStack() as ctx_xn:
            E["xn"] = ctx_xn.enter_context(nc.sbuf_tensor("xn", [128, KC * L], BF16))
            with ExitStack() as ctx_head:
                E["pT"] = [ctx_head.enter_context(nc.psum_tensor(f"h_pT{i}", [128, 512], F32)) for i in range(2)]
                E["pmm"] = [ctx_head.enter_context(nc.psum_tensor(f"h_pm{i}", [128, 512], F32)) for i in range(4)]
                E["pg"] = ctx_head.enter_context(nc.psum_tensor("h_pg", [128, 128], F32))
                hsb = lambda n, s, dt: ctx_head.enter_context(nc.sbuf_tensor(n, s, dt))
                E["wblk"] = [hsb(f"ip_w{i}", [128, KC * 128], BF16) for i in range(2)]
                E["zstage"] = [hsb(f"ip_zs{i}", [128, L], BF16) for i in range(2)]
                E["xstage"] = [hsb(f"ip_xst{i}", [128, L], BF16) for i in range(2)]
                E["xpre"] = hsb("ip_xpre", [128, L + 3], F32)
                E["cv"] = hsb("ip_cv", [128, L], F32)
                E["cva"] = hsb("ip_cva", [128, L], F32)
                E["dstage"] = hsb("ip_dst", [NH, L], F32)
                E["dt0"] = hsb("ip_dt0", [NH, L], F32)
                E["dt1"] = hsb("ip_dt1", [NH, L], F32)
                E["gwb"] = hsb("gt_w", [128, KC * 8], BF16)
                E["gsc"] = hsb("gt_s", [128, 128], F32)
                _ln(E)
                if max_phase >= PH_INPROJ:
                    _inproj(E)
                if max_phase >= PH_GATE:
                    _gate(E)
            if max_phase >= PH_F1:
                _f1(E)
            if dbg:
                _dbg_xn(E)
        if max_phase >= PH_F2:
            _f2(E)
        if max_phase >= PH_SSD:
            E["whh"] = ctx.enter_context(nc.sbuf_tensor("whh", [128, 32 * 1024], BF16))
            _ssd(E)
        if max_phase >= PH_WOUT:
            _wout(E)
        if dbg:
            _dbg_dump(E)

        @E["blk"].sync
        def _(s):
            for st in E.get("streams", []):
                s.wait_ge(st.h, st.v)
            s.wait_ge(E["sP"].h, E["sP"].v)
            s.wait_ge(E["sV"].h, E["sV"].v)
            s.wait_ge(E["sA"].h, E["sA"].v)
    return nc


def _init(E):
    blk, sA, sV = E["blk"], E["sA"], E["sV"]
    sIC = mkstream(E, "sIC")

    @blk.gpsimd
    def _(g):
        sIC.mark(g.dma_start(out=E["i128"][:], in_=E["i128_in"]), 16)
        sIC.mark(g.dma_start(out=E["i64f"][:], in_=E["i64_in"]), 16)
        sIC.mark(g.dma_start(out=E["wconv"][:], in_=E["wconv_r"]), 16)
        sIC.mark(g.dma_start(out=E["bconv"][:], in_=E["bconv_r"]), 16)
        sIC.mark(g.dma_start(out=E["dtb"][:], in_=E["dtb_c"]), 16)
        sIC.mark(g.dma_start(out=E["scrc"][:], in_=E["alog_c"]), 16)
        sIC.mark(g.dma_start(out=E["dpb"][:], in_=E["dp_row"].broadcast_to([128, NH])), 16)
        sIC.mark(g.dma_start(out=E["nwb"][:], in_=E["nw_row"].broadcast_to([128, DI])), 16)
        sIC.mark(g.dma_start(out=E["ones64"][:], in_=E["ones64_in"]), 16)
        sIC.mark(g.dma_start(out=E["mask128"][:], in_=E["mask_in"]), 16)
    E["init_loads"] = (sIC, sIC.v)

    @blk.scalar
    def _(sc):
        sc.wait_ge(sIC.h, sIC.v)
        sA.mark(sc.activation(E["nega"][:], E["scrc"][:], AF.Exp))
        sA.mark(sc.activation(E["dpbbf"][:], E["dpb"][:], AF.Copy))
    E["dpbbf_ok"] = sA.v

    @blk.vector
    def _(v):
        v.memset(E["epsc"][:], 1e-5)
        v.wait_ge(sA.h, sA.v)
        sV.mark(v.tensor_scalar(E["scrc"][:], E["nega"][:], -1.0, None, OP.mult))
        v.wait_ge(sV.h, sV.v)
        sV.mark(v.tensor_copy(E["nega"][:], E["scrc"][:]))
    E["nega_ready"] = sV.v


def _ln(E):
    nc, blk = E["nc"], E["blk"]
    sP, sV, sA = E["sP"], E["sV"], E["sA"]
    xn, i128, x_in = E["xn"], E["i128"], E["x_in"]
    sLN = [mkstream(E, f"sLN{j}") for j in range(4)]

    with ExitStack() as c2:
        lsb = lambda n, s, dt: c2.enter_context(nc.sbuf_tensor(n, s, dt))
        xq = [lsb(f"ln_x{j}", [128, D], F32) for j in range(2)]
        xnq = [lsb(f"ln_xn{j}", [128, D], BF16) for j in range(4)]
        st = lsb("ln_stats", [128, 8], F32)
        pT = E["pT"]

        E["ln_sm"] = {}
        bank_free = {}
        xq_free = {}
        ln_mm_last = {}
        prev_sm_done = None
        for sm in range(4):
            for j in range(4):
                m = sm * 4 + j

                @blk.sync
                def _(s, m=m, j=j, prev=xq_free.get(j % 2)):
                    if prev is not None:
                        s.wait_ge(sV.h, prev)
                    sLN[j].mark(s.dma_start(out=xq[j % 2][:], in_=x_in[m * 128:(m + 1) * 128, :]), 16)
                ld = sLN[j].v

                @blk.vector
                def _(v, j=j, ld=ld):
                    v.wait_ge(sLN[j].h, ld)
                    sV.mark(v.reduce_sum(st[:, 0:1], xq[j % 2][:], AX.X))
                    v.wait_ge(sV.h, sV.v)
                    sV.mark(v.tensor_scalar(st[:, 4:5], st[:, 0:1], 1.0 / D, None, OP.mult))
                mean_ok = sV.v

                @blk.scalar
                def _(sc, j=j, ld=ld, tml=ln_mm_last.get(0)):
                    sc.wait_ge(sLN[j].h, ld)
                    if tml is not None:
                        sc.wait_ge(sP.h, tml)
                    sA.mark(sc.activation(xnq[j][:], xq[j % 2][:], AF.Square, accum_out=st[:, 1:2]))
                ssq_ok = sA.v

                @blk.vector
                def _(v, mean_ok=mean_ok, ssq_ok=ssq_ok):
                    v.wait_ge(sV.h, mean_ok)
                    v.wait_ge(sA.h, ssq_ok)
                    sV.mark(v.tensor_scalar(st[:, 5:6], st[:, 1:2], 1.0 / D, None, OP.mult))
                    sV.mark(v.tensor_tensor(st[:, 6:7], st[:, 4:5], st[:, 4:5], OP.mult))
                    v.wait_ge(sV.h, sV.v)
                    sV.mark(v.tensor_tensor(st[:, 2:3], st[:, 5:6], st[:, 6:7], OP.subtract))
                var_ok = sV.v

                @blk.scalar
                def _(sc, var_ok=var_ok):
                    sc.wait_ge(sV.h, var_ok)
                    sA.mark(sc.activation(st[:, 7:8], st[:, 2:3], AF.Ln, bias=E["epsc"][:]))
                    sc.wait_ge(sA.h, sA.v)
                    sA.mark(sc.activation(st[:, 3:4], st[:, 7:8], AF.Exp, scale=-0.5))
                rstd_ok = sA.v

                @blk.vector
                def _(v, j=j, rstd_ok=rstd_ok, var_ok=var_ok):
                    v.wait_ge(sA.h, rstd_ok)
                    v.wait_ge(sV.h, var_ok)
                    sV.mark(v.tensor_scalar(xnq[j][:], xq[j % 2][:], st[:, 4:5], st[:, 3:4],
                                            OP.subtract, OP.mult))
                xq_free[j % 2] = sV.v
            xn_ready = sV.v
            for kc in range(KC):
                pb = pT[kc % 2]
                war = bank_free.get(kc % 2)

                @blk.tensor
                def _(t, kc=kc, pb=pb, war=war, xn_ready=xn_ready):
                    t.wait_ge(sV.h, xn_ready)
                    if war is not None:
                        t.wait_ge(sV.h, war)
                    for j in range(4):
                        mi = t.matmul(pb[:, j * 128:(j + 1) * 128], xnq[j][:, kc * 128:(kc + 1) * 128],
                                      i128[:], start=(j == 0), stop=(j == 3))
                    sP.mark(mi)
                mm_ok = sP.v
                if kc == KC - 1:
                    ln_mm_last[0] = sP.v

                @blk.vector
                def _(v, sm=sm, kc=kc, pb=pb, mm_ok=mm_ok):
                    v.wait_ge(sP.h, mm_ok)
                    sV.mark(v.tensor_copy(xn[:, kc * L + sm * 512: kc * L + (sm + 1) * 512], pb[:]))
                bank_free[kc % 2] = sV.v
            prev_sm_done = sV.v
            E["ln_sm"][sm] = sV.v
    E["xn_ready"] = (sV, sV.v)


def _inproj(E):
    nc, blk = E["nc"], E["blk"]
    sP, sV, sA = E["sP"], E["sV"], E["sA"]
    xn = E["xn"]
    wconv, bconv, dtb, nega, dta = E["wconv"], E["bconv"], E["dtb"], E["nega"], E["dta"]
    bfm, cfm = E["bfm"], E["cfm"]
    zs_sp, x_sp, win_t = E["zs_sp"], E["x_sp"], E["win_t"]
    sWB = [mkstream(E, f"sWB{i}") for i in range(2)]
    sZS = [mkstream(E, f"sZS{i}") for i in range(2)]
    sXS = [mkstream(E, f"sXS{i}") for i in range(2)]

    if True:
        wblk = E["wblk"]
        pmm = E["pmm"]
        zstage = E["zstage"]
        xstage = E["xstage"]
        xpre = E["xpre"]
        cv = E["cv"]
        cva = E["cva"]
        dstage = E["dstage"]
        dt0 = E["dt0"]
        dt1 = E["dt1"]

        @blk.vector
        def _(v):
            sV.mark(v.memset(xpre[:, 0:3], 0.0))
        xpre0_ok = sV.v

        wload = {}
        mm_done = {}
        bank_free = {}
        zst_free = {}
        xst_free = {}
        cv_free = None
        conv_done = None

        for mb in range(67):
            msz = 64 if mb == 66 else 128
            buf = wblk[mb % 2]
            war = mm_done.get(mb - 2)

            @blk.gpsimd
            def _(gq, mb=mb, msz=msz, buf=buf, war=war):
                if war is not None:
                    gq.wait_ge(sP.h, war)
                src = win_t[:, mb * 128: mb * 128 + msz].rearrange("(kc p) m -> p kc m", p=128)
                sWB[mb % 2].mark(gq.dma_start(
                    out=buf[:, 0:KC * msz].rearrange("p (kc m) -> p kc m", kc=KC),
                    in_=src), 16)
            wload[mb] = sWB[mb % 2].v

            mm_oks = []
            for tb in range(4):
                pb = pmm[tb]
                pwar = bank_free.get(tb)

                @blk.tensor
                def _(t, mb=mb, tb=tb, msz=msz, buf=buf, pb=pb, pwar=pwar):
                    t.wait_ge(sWB[mb % 2].h, wload[mb])
                    if mb < 2:
                        t.wait_ge(sV.h, E["ln_sm"][tb])
                    if pwar is not None:
                        t.wait_ge(pwar[0].h, pwar[1])
                    dst = pb[0:msz, :] if msz < 128 else pb[:]
                    for kc in range(KC):
                        mi = t.matmul(dst, buf[:, kc * msz:(kc + 1) * msz],
                                      xn[:, kc * L + tb * 512: kc * L + (tb + 1) * 512],
                                      start=(kc == 0), stop=(kc == KC - 1))
                    sP.mark(mi)
                mm_oks.append(sP.v)
            mm_done[mb] = sP.v

            if mb < 32:
                zbuf = zstage[mb % 2]

                @blk.scalar
                def _(sc, mb=mb, zbuf=zbuf, mm_oks=tuple(mm_oks)):
                    w = zst_free.get(mb - 2)
                    if w is not None:
                        sc.wait_ge(w[0].h, w[1])
                    for tb in range(4):
                        sc.wait_ge(sP.h, mm_oks[tb])
                        sA.mark(sc.activation(zbuf[:, tb * 512:(tb + 1) * 512], pmm[tb][:], AF.Silu))
                        bank_free[tb] = (sA, sA.v)

                @blk.sync
                def _(s, mb=mb, zbuf=zbuf, a=sA.v):
                    s.wait_ge(sA.h, a)
                    sZS[mb % 2].mark(s.dma_start(out=zs_sp[mb * 128:(mb + 1) * 128, :],
                                                 in_=zbuf[:]), 16)
                zst_free[mb] = (sZS[mb % 2], sZS[mb % 2].v)
            elif mb < 66:
                cb = mb - 32

                @blk.vector
                def _(v, cb=cb, mm_oks=tuple(mm_oks), cd=conv_done, cvf=cv_free):
                    if cd is not None:
                        v.wait_ge(sV.h, cd)
                    for tb in range(4):
                        v.wait_ge(sP.h, mm_oks[tb])
                        sV.mark(v.tensor_copy(xpre[:, 3 + tb * 512: 3 + (tb + 1) * 512], pmm[tb][:]))
                        bank_free[tb] = (sV, sV.v)
                    if cb == 0:
                        v.wait_ge(sV.h, xpre0_ok)
                    if cvf is not None:
                        v.wait_ge(sA.h, cvf)
                    v.wait_ge(sV.h, sV.v)
                    c0 = cb * 4
                    sV.mark(v.tensor_scalar(cva[:], xpre[:, 0:L], wconv[:, c0:c0 + 1], None, OP.mult))
                    v.wait_ge(sV.h, sV.v)
                    sV.mark(v.scalar_tensor_tensor(cv[:], xpre[:, 1:1 + L], wconv[:, c0 + 1:c0 + 2], cva[:], OP.mult, OP.add))
                    v.wait_ge(sV.h, sV.v)
                    sV.mark(v.scalar_tensor_tensor(cva[:], xpre[:, 2:2 + L], wconv[:, c0 + 2:c0 + 3], cv[:], OP.mult, OP.add))
                    v.wait_ge(sV.h, sV.v)
                    sV.mark(v.scalar_tensor_tensor(cv[:], xpre[:, 3:3 + L], wconv[:, c0 + 3:c0 + 4], cva[:], OP.mult, OP.add))
                conv_done = sV.v
                cv_ok = sV.v
                if cb < 32:
                    xbuf = xstage[cb % 2]

                    @blk.scalar
                    def _(sc, cb=cb, xbuf=xbuf, cv_ok=cv_ok):
                        sc.wait_ge(sV.h, cv_ok)
                        w = xst_free.get(cb - 2)
                        if w is not None:
                            sc.wait_ge(w[0].h, w[1])
                        sA.mark(sc.activation(xbuf[:], cv[:], AF.Silu, bias=bconv[:, cb:cb + 1]))
                    cv_free = sA.v

                    @blk.sync
                    def _(s, cb=cb, xbuf=xbuf, a=sA.v):
                        s.wait_ge(sA.h, a)
                        sXS[cb % 2].mark(s.dma_start(out=x_sp[cb * 128:(cb + 1) * 128, :],
                                                     in_=xbuf[:]), 16)
                    xst_free[cb] = (sXS[cb % 2], sXS[cb % 2].v)
                else:
                    dstb = bfm if cb == 32 else cfm

                    @blk.scalar
                    def _(sc, cb=cb, dstb=dstb, cv_ok=cv_ok):
                        sc.wait_ge(sV.h, cv_ok)
                        sA.mark(sc.activation(dstb[:], cv[:], AF.Silu, bias=bconv[:, cb:cb + 1]))
                    cv_free = sA.v
            else:
                @blk.vector
                def _(v, mm_oks=tuple(mm_oks)):
                    for tb in range(4):
                        v.wait_ge(sP.h, mm_oks[tb])
                        sl = slice(tb * 512, (tb + 1) * 512)
                        sV.mark(v.tensor_scalar(dstage[:, sl], pmm[tb][0:NH, :], dtb[:], None, OP.add))
                        bank_free[tb] = (sV, sV.v)

        raw_ok = sV.v

        @blk.scalar
        def _(sc, raw_ok=raw_ok):
            sc.wait_ge(sV.h, raw_ok)
            sA.mark(sc.activation(dt0[:], dstage[:], AF.Abs))
            sc.wait_ge(sA.h, sA.v)
            sA.mark(sc.activation(dt1[:], dt0[:], AF.Exp, scale=-1.0))
            sc.wait_ge(sA.h, sA.v)
            sA.mark(sc.activation(dt0[:], dt1[:], AF.Ln, bias=1.0))
        sp_ok = sA.v

        @blk.vector
        def _(v, sp_ok=sp_ok):
            v.wait_ge(sA.h, sp_ok)
            v.wait_ge(sV.h, E["nega_ready"])
            sV.mark(v.tensor_scalar(dt1[:], dstage[:], 0.0, None, OP.max))
            v.wait_ge(sV.h, sV.v)
            sV.mark(v.tensor_tensor(dstage[:], dt1[:], dt0[:], OP.add))
            v.wait_ge(sV.h, sV.v)
            sV.mark(v.tensor_scalar(dta[:], dstage[:], nega[:], None, OP.mult))
    E["dta_ready_V"] = sV.v
    E["bc_ready_A"] = sA.v
    E["inproj_spill_finals"] = [(s, s.v) for s in sZS + sXS]


def _gate(E):
    nc, blk = E["nc"], E["blk"]
    sP, sV, sA = E["sP"], E["sV"], E["sA"]
    xn, cmb, gate_t = E["xn"], E["cmb"], E["gate_t"]
    sGW = mkstream(E, "sGW")
    gw, sc_all, pg = E["gwb"], E["gsc"], E["pg"]

    with ExitStack() as c2:
        lsb = lambda n, s, dt: c2.enter_context(nc.sbuf_tensor(n, s, dt))
        m8 = lsb("gt_m8", [128, 8], F32)
        z8 = lsb("gt_z8", [128, 8], F32)
        den = lsb("gt_den", [128, 1], F32)
        rcp = lsb("gt_rcp", [128, 1], F32)
        pick = lsb("gt_pick", [128, 8], F32)
        msk = lsb("gt_msk", [128, 8], U32)

        @blk.gpsimd
        def _(g):
            sGW.mark(g.dma_start(out=gw[:].rearrange("p (kc m) -> p kc m", kc=KC),
                                 in_=gate_t.rearrange("(kc p) m -> p kc m", p=128)), 16)
        gw_ok = sGW.v

        @blk.vector
        def _(v):
            v.memset(z8[:], 0.0)

        @blk.tensor
        def _(t):
            t.wait_ge(sGW.h, gw_ok)
            xr = E["xn_ready"]
            t.wait_ge(xr[0].h, xr[1])
            for m in range(16):
                for kc in range(KC):
                    mi = t.matmul(pg[:, m * 8:(m + 1) * 8],
                                  xn[:, kc * L + m * 128: kc * L + (m + 1) * 128],
                                  gw[:, kc * 8:(kc + 1) * 8],
                                  start=(kc == 0), stop=(kc == KC - 1))
            sP.mark(mi)
        mm_ok = sP.v

        @blk.scalar
        def _(sc, mm_ok=mm_ok):
            sc.wait_ge(sP.h, mm_ok)
            sA.mark(sc.activation(sc_all[:], pg[:], AF.Sigmoid))
        E["gate_sig_A"] = sA.v
        sig_ok = sA.v

        for m in range(16):
            @blk.vector
            def _(v, m=m, sig_ok=sig_ok):
                if m == 0:
                    v.wait_ge(sA.h, sig_ok)
                sV.mark(v.max(out=m8[:], in_=sc_all[:, m * 8:(m + 1) * 8]))
                v.wait_ge(sV.h, sV.v)
                sV.mark(v.tensor_scalar(den[:], m8[:, 0:1], m8[:, 1:2], 1e-20, OP.add, OP.add))
                v.wait_ge(sV.h, sV.v)
                sV.mark(v.reciprocal(rcp[:], den[:]))
                v.wait_ge(sV.h, sV.v)
                sV.mark(v.tensor_scalar(pick[:], sc_all[:, m * 8:(m + 1) * 8], rcp[:], 2.5, OP.mult, OP.mult))
                sV.mark(v.tensor_scalar(msk[:], sc_all[:, m * 8:(m + 1) * 8], m8[:, 1:2], None, OP.is_ge))
                v.wait_ge(sV.h, sV.v)
                sV.mark(v.select(cmb[:, m * 8:(m + 1) * 8], msk[:], pick[:], z8[:], add_drain=True))
    E["cmb_ready_V"] = sV.v


def _f1(E):
    nc, blk = E["nc"], E["blk"]
    sP, sV, sA = E["sP"], E["sV"], E["sA"]
    xn = E["xn"]
    h_sp, w1, w3, sw1, sw3 = E["h_sp"], E["w1"], E["w3"], E["sw1"], E["sw3"]
    sFW = [mkstream(E, f"sFW{i}") for i in range(2)]
    sHS = [mkstream(E, f"sHS{i}") for i in range(2)]

    with ExitStack() as c2:
        lsb = lambda n, s, dt: c2.enter_context(nc.sbuf_tensor(n, s, dt))
        wa = [lsb(f"f1_wa{i}", [128, KC * 128], BF16) for i in range(2)]
        wb = [lsb(f"f1_wb{i}", [128, KC * 128], BF16) for i in range(2)]
        sil = [lsb(f"f1_sil{i}", [128, 512], F32) for i in range(2)]
        hstage = [lsb(f"f1_h{i}", [128, L], BF16) for i in range(2)]
        p1 = [c2.enter_context(nc.psum_tensor(f"f1_p1{i}", [128, 512], F32)) for i in range(2)]
        p3 = [c2.enter_context(nc.psum_tensor(f"f1_p3{i}", [128, 512], F32)) for i in range(2)]

        wload = {}
        mm_done = {}
        bank_free = {}
        hst_free = {}
        sil_free = {}

        for idx in range(NE * 11):
            e, mb = divmod(idx, 11)
            bi = idx % 2
            war = mm_done.get(idx - 2)

            @blk.gpsimd
            def _(gq, idx=idx, e=e, mb=mb, bi=bi, war=war):
                if idx < 2:
                    gq.wait_ge(sV.h, E["cmb_ready_V"])
                    gq.wait_ge(sV.h, E["dta_ready_V"])
                    gq.wait_ge(sA.h, E["bc_ready_A"])
                    for s, v in E["inproj_spill_finals"]:
                        gq.wait_ge(s.h, v)
                if war is not None:
                    gq.wait_ge(sP.h, war)
                if e < 8:
                    s1 = w1[e, :, mb * 128:(mb + 1) * 128]
                    s3 = w3[e, :, mb * 128:(mb + 1) * 128]
                else:
                    s1 = sw1[:, mb * 128:(mb + 1) * 128]
                    s3 = sw3[:, mb * 128:(mb + 1) * 128]
                sFW[bi].mark(gq.dma_start(out=wa[bi][:].rearrange("p (kc m) -> p kc m", kc=KC),
                                          in_=s1.rearrange("(kc p) m -> p kc m", p=128)), 16)
                sFW[bi].mark(gq.dma_start(out=wb[bi][:].rearrange("p (kc m) -> p kc m", kc=KC),
                                          in_=s3.rearrange("(kc p) m -> p kc m", p=128)), 16)
            wload[idx] = sFW[bi].v

            hbuf = hstage[idx % 2]
            for tb in range(4):
                pi = tb % 2
                pwar = bank_free.get(pi)

                @blk.tensor
                def _(t, idx=idx, tb=tb, bi=bi, pi=pi, pwar=pwar):
                    t.wait_ge(sFW[bi].h, wload[idx])
                    if idx == 0 and tb == 0:
                        t.wait_ge(sA.h, E["gate_sig_A"])
                        t.wait_ge(sV.h, E["dta_ready_V"])
                    if pwar is not None:
                        t.wait_ge(sV.h, pwar)
                    for kc in range(KC):
                        t.matmul(p1[pi][:], wa[bi][:, kc * 128:(kc + 1) * 128],
                                 xn[:, kc * L + tb * 512: kc * L + (tb + 1) * 512],
                                 start=(kc == 0), stop=(kc == KC - 1))
                    for kc in range(KC):
                        mi = t.matmul(p3[pi][:], wb[bi][:, kc * 128:(kc + 1) * 128],
                                      xn[:, kc * L + tb * 512: kc * L + (tb + 1) * 512],
                                      start=(kc == 0), stop=(kc == KC - 1))
                    sP.mark(mi)
                mm_ok = sP.v
                if tb == 3:
                    mm_done[idx] = sP.v

                @blk.scalar
                def _(sc, pi=pi, mm_ok=mm_ok, sf=sil_free.get(tb % 2)):
                    sc.wait_ge(sP.h, mm_ok)
                    if sf is not None:
                        sc.wait_ge(sV.h, sf)
                    sA.mark(sc.activation(sil[pi][:], p1[pi][:], AF.Silu))
                sil_ok = sA.v

                @blk.vector
                def _(v, idx=idx, tb=tb, pi=pi, hbuf=hbuf, sil_ok=sil_ok):
                    v.wait_ge(sA.h, sil_ok)
                    if tb == 0:
                        w = hst_free.get(idx - 2)
                        if w is not None:
                            v.wait_ge(w[0].h, w[1])
                    sV.mark(v.tensor_tensor(hbuf[:, tb * 512:(tb + 1) * 512], sil[pi][:],
                                            p3[pi][:], OP.mult))
                bank_free[pi] = sV.v
                sil_free[tb % 2] = sV.v

            @blk.sync
            def _(s, idx=idx, e=e, mb=mb, hbuf=hbuf, vv=sV.v):
                s.wait_ge(sV.h, vv)
                sHS[idx % 2].mark(s.dma_start(
                    out=h_sp[e * FH + mb * 128: e * FH + (mb + 1) * 128, :],
                    in_=hbuf[:]), 16)
            hst_free[idx] = (sHS[idx % 2], sHS[idx % 2].v)
    E["f1_spill_finals"] = [(s, s.v) for s in sHS]


def _f2(E):
    nc, blk = E["nc"], E["blk"]
    sP, sV, sA = E["sP"], E["sV"], E["sA"]
    cmb, g_sp = E["cmb"], E["g_sp"]
    h_sp, w2, sw2 = E["h_sp"], E["w2"], E["sw2"]
    sW2 = [mkstream(E, f"sW2{i}") for i in range(3)]
    sHB = [mkstream(E, f"sHB{i}") for i in range(3)]
    sGS = [mkstream(E, f"sGS{i}") for i in range(2)]

    with ExitStack() as c2:
        lsb = lambda n, s, dt: c2.enter_context(nc.sbuf_tensor(n, s, dt))
        acc = lsb("f2_acc", [128, 8 * D], F32)
        w2h = [lsb(f"f2_w{i}", [128, 11 * 512], BF16) for i in range(3)]
        hb = [lsb(f"f2_h{i}", [128, 11 * 256], BF16) for i in range(3)]
        gst = [lsb(f"f2_g{i}", [128, D], BF16) for i in range(2)]
        pf = [c2.enter_context(nc.psum_tensor(f"f2_p{i}", [128, 512], F32)) for i in range(4)]

        wload = {}
        hload = {}
        w_used = {}
        h_used = {}
        bank_free = {}
        acc_mark = {}
        silu_done = None

        widx = 0
        hidx = 0
        for th in range(2):
            for e in range(NE):
                for nbq in range(4):
                    wbuf = w2h[widx % 3]
                    war = w_used.get(widx - 3)

                    @blk.gpsimd
                    def _(gq, widx=widx, e=e, nbq=nbq, wbuf=wbuf, war=war):
                        if widx < 2:
                            for s, v in E["f1_spill_finals"]:
                                gq.wait_ge(s.h, v)
                            if "dbg_xn_done" in E:
                                gq.wait_ge(E["dbg_xn_done"][0].h, E["dbg_xn_done"][1])
                        if war is not None:
                            gq.wait_ge(sP.h, war)
                        src = w2[e, :, nbq * 512:(nbq + 1) * 512] if e < 8 else \
                            sw2[:, nbq * 512:(nbq + 1) * 512]
                        sW2[widx % 3].mark(gq.dma_start(
                            out=wbuf[:].rearrange("p (kc n) -> p kc n", kc=11),
                            in_=src.rearrange("(kc p) n -> p kc n", p=128)), 16)
                    wload[widx] = sW2[widx % 3].v

                    for mb in range(4):
                        hbuf = hb[hidx % 3]
                        hwar = h_used.get(hidx - 3)

                        @blk.sync
                        def _(s, hidx=hidx, th=th, e=e, mb=mb, hbuf=hbuf, hwar=hwar):
                            if hidx < 2:
                                for st, v in E["f1_spill_finals"]:
                                    s.wait_ge(st.h, v)
                            if hwar is not None:
                                s.wait_ge(sP.h, hwar)
                            c0 = th * 1024 + mb * 256
                            sHB[hidx % 3].mark(s.dma_start(
                                out=hbuf[:].rearrange("p (kc t) -> p kc t", kc=11),
                                in_=h_sp[e * FH:(e + 1) * FH, c0:c0 + 256].rearrange("(kc p) t -> p kc t", p=128)), 16)
                        hload[hidx] = sHB[hidx % 3].v

                        for ms in range(2):
                            bk = (mb % 2) * 2 + ms
                            pwar = bank_free.get(bk)

                            @blk.tensor
                            def _(t, widx=widx, hidx=hidx, ms=ms, bk=bk, pwar=pwar,
                                  wbuf=wbuf, hbuf=hbuf):
                                t.wait_ge(sW2[widx % 3].h, wload[widx])
                                t.wait_ge(sHB[hidx % 3].h, hload[hidx])
                                if pwar is not None:
                                    t.wait_ge(sV.h, pwar)
                                for kc in range(11):
                                    mi = t.matmul(pf[bk][:], hbuf[:, kc * 256 + ms * 128: kc * 256 + (ms + 1) * 128],
                                                  wbuf[:, kc * 512:(kc + 1) * 512],
                                                  start=(kc == 0), stop=(kc == 10))
                                sP.mark(mi)
                            mm_ok = sP.v
                            w_used[widx] = sP.v
                            h_used[hidx] = sP.v

                            j = mb * 2 + ms
                            asl = slice(j * D + nbq * 512, j * D + (nbq + 1) * 512)
                            mtile = th * 8 + j
                            first = e == 0

                            @blk.vector
                            def _(v, e=e, bk=bk, asl=asl, mtile=mtile, nbq=nbq, first=first,
                                  mm_ok=mm_ok, sd=silu_done, am=acc_mark.get((mtile, nbq))):
                                v.wait_ge(sP.h, mm_ok)
                                if first:
                                    if sd is not None:
                                        v.wait_ge(sA.h, sd)
                                    v.wait_ge(sV.h, E["cmb_ready_V"])
                                    if mtile == 0 and "dbg_xn_done" in E:
                                        v.wait_ge(E["dbg_xn_done"][0].h, E["dbg_xn_done"][1])
                                else:
                                    v.wait_ge(sV.h, am)
                                if e < 8:
                                    col = cmb[:, mtile * 8 + e: mtile * 8 + e + 1]
                                    if first:
                                        sV.mark(v.tensor_scalar(acc[:, asl], pf[bk][:], col, None, OP.mult))
                                    else:
                                        sV.mark(v.scalar_tensor_tensor(acc[:, asl], pf[bk][:], col,
                                                                       acc[:, asl], OP.mult, OP.add))
                                else:
                                    sV.mark(v.scalar_tensor_tensor(acc[:, asl], pf[bk][:], 1.0,
                                                                   acc[:, asl], OP.mult, OP.add))
                            bank_free[bk] = sV.v
                            acc_mark[(mtile, nbq)] = sV.v
                        hidx += 1
                    widx += 1
            acc_ok = sV.v
            gsnap = [sGS[0].v, sGS[1].v]

            @blk.scalar
            def _(sc, th=th, acc_ok=acc_ok, gsnap=gsnap):
                sc.wait_ge(sV.h, acc_ok)
                for j in range(8):
                    if j >= 2:
                        sc.wait_ge(sGS[j % 2].h, gsnap[j % 2] + 16 * ((j - 2) // 2 + 1))
                    elif th:
                        sc.wait_ge(sGS[j % 2].h, gsnap[j % 2])
                    sA.mark(sc.activation(gst[j % 2][:], acc[:, j * D:(j + 1) * D], AF.Silu))

            @blk.sync
            def _(s, th=th, base=sA.v - 8):
                for j in range(8):
                    s.wait_ge(sA.h, base + j + 1)
                    sGS[j % 2].mark(s.dma_start(
                        out=g_sp[:, (th * 8 + j) * D:(th * 8 + j + 1) * D], in_=gst[j % 2][:]), 16)
            silu_done = sA.v
    E["g_ready_A"] = sA.v
    E["g_sp_finals"] = [(sGS[0], sGS[0].v), (sGS[1], sGS[1].v)]


def _ssd(E):
    nc, blk = E["nc"], E["blk"]
    sP, sV, sA = E["sP"], E["sV"], E["sA"]
    bfm, cfm, dta, dpb, nwb, i128, i64f = (E[k] for k in ("bfm", "cfm", "dta", "dpb", "nwb", "i128", "i64f"))
    ones64, mask128 = E["ones64"], E["mask128"]
    zs_sp, x_sp, yn_sp, cs_sp = E["zs_sp"], E["x_sp"], E["yn_sp"], E["cs_sp"]
    sXT = mkstream(E, "sXT")
    sZT = mkstream(E, "sZT")
    sCW = mkstream(E, "sCW")
    sBC = [mkstream(E, f"sBC{i}") for i in range(2)]
    sYN = mkstream(E, "sYN")
    sWH0 = [mkstream(E, f"sWH0{i}") for i in range(2)]
    sGT = [mkstream(E, f"sGT{i}") for i in range(2)]
    sXI = [mkstream(E, f"sXI{i}") for i in range(2)]
    sOB2 = [mkstream(E, f"sOB2{i}") for i in range(2)]
    whh, g_sp, x_in, out, wout_t = E["whh"], E["g_sp"], E["x_in"], E["out"], E["wout_t"]
    sG = Sem(E["nc"], E["ctx"], "sG")

    with ExitStack() as c2:
        lsb = lambda n, s, dt: c2.enter_context(nc.sbuf_tensor(n, s, dt))
        lps = lambda n: c2.enter_context(nc.psum_tensor(n, [128, 512], F32))
        gt2 = [lsb(f"ss_gt{i}", [128, 512], BF16) for i in range(2)]
        xt2 = [lsb(f"ss_xt{i}", [128, 512], F32) for i in range(2)]
        xtm = lsb("ss_x", [128, DI], BF16)
        zst = lsb("ss_z", [128, DI], BF16)
        Sbf = lsb("ss_Sbf", [128, DI], BF16)
        xdp = lsb("ss_xdp", [128, DI], BF16)
        bca = [lsb(f"ss_bc{i}", [128, 16 * CH], F32) for i in range(2)]
        ebuf = [lsb(f"ss_eb{i}", [128, 16 * CH], BF16) for i in range(4)]
        pt = [lsb(f"ss_pt{i}", [128, 16 * CH], BF16) for i in range(4)]
        xts = [lsb(f"ss_xts{i}", [128, 16 * 64], BF16) for i in range(4)]
        csf = lsb("ss_csf", [NH, CH], F32)
        zcs = lsb("ss_zcs", [NH, CH], F32)
        cst = lsb("ss_cst", [128, NH], F32)
        diag64 = lsb("ss_dg", [NH, NH], F32)
        expc = lsb("ss_expc", [128, NH], F32)
        eto = lsb("ss_eto", [128, NH], F32)
        btm = lsb("ss_btm", [128, CH], BF16)
        sctm = lsb("ss_sct", [128, CH], BF16)
        ybuf = lsb("ss_y", [128, DI], BF16)
        ytmp = lsb("ss_yt", [128, 512], F32)
        stmp = lsb("ss_st", [128, 512], F32)
        ybF = ybuf[:].bitcast(F32)
        ob2 = [ybF[:, 0:512], ybF[:, 512:1024]]
        t1w = ybF[:, 1024:1536]
        ssq = lsb("ss_ssq", [128, 4], F32)
        rstd = lsb("ss_rstd", [128, 1], F32)
        ynb = lsb("ss_ynb", [128, DI], BF16)
        dT = [ynb[:].bitcast(F32)] * 2
        psm = [lps(f"ss_sm{i}") for i in range(2)]
        pY = [lps(f"ss_pY{i}") for i in range(2)]
        pYo = [lps(f"ss_pYo{i}") for i in range(2)]
        pSt = [lps(f"ss_pSt{i}") for i in range(2)]

        @blk.vector
        def _(v):
            v.wait_ge(sA.h, E["g_ready_A"])
            for st, vv in E.get("g_sp_finals", []):
                v.wait_ge(st.h, vv)
            ics, icv = E["init_loads"]
            v.wait_ge(ics.h, icv)
            v.memset(Sbf[:], 0.0)
            sV.mark(v.memset(zcs[:], 0.0))
        init_ok = sV.v

        @blk.gpsimd
        def _(gq):
            gq.wait_ge(sA.h, E["g_ready_A"])
            for st, vv in E.get("g_sp_finals", []):
                gq.wait_ge(st.h, vv)
            for pc in range(2):
                sWH0[pc].mark(gq.dma_start(
                    out=whh[:, pc * 16384:(pc + 1) * 16384].rearrange("p (kc n) -> p kc n", kc=32),
                    in_=wout_t[:, pc * 512:(pc + 1) * 512].rearrange("(kc p) n -> p kc n", p=128)), 16)

        ebank = {}
        x_free = None
        z_free = None
        yn_free = {}
        bc_free = {}
        pt_free = {}
        bank_mm_g = {}
        sbf_war = None
        btl_prev = {}
        scale_ok = None
        cs_wr = None
        csf_rd = None
        cst_evac = None
        btm_evac = None
        sct_evac = None
        ynf_evac = None
        bcasts = {}

        st_pro = {}

        def xdp_pool(c):
            @blk.gpsimd
            def _(gq, c=c, x_ld=st_pro["x_ld"], pml=st_pro.get("piece_mm"),
                  ynw=yn_free.get("b")):
                gq.wait_ge(sXT.h, x_ld)
                if pml is not None:
                    gq.wait_ge(sP.h, pml)
                if ynw is not None:
                    gq.wait_ge(sYN.h, ynw)
                if c == 0:
                    gq.wait_ge(sA.h, E["dpbbf_ok"])
                    gq.wait_ge(sV.h, init_ok)
                sG.mark(gq.tensor_tensor(xdp[:].rearrange("p (h x) -> p h x", h=NH),
                                         xtm[:].rearrange("p (h x) -> p h x", h=NH),
                                         E["dpbbf"][:].unsqueeze(2).broadcast_to([128, NH, 64]),
                                         OP.mult))
            st_pro[("xdp_ok", c)] = sG.v

        def sched_cs(c):
            nonlocal cs_wr, csf_rd

            @blk.vector
            def _(v, c=c, cw=cs_wr, cr=csf_rd, em=st_pro.get("eto_mm"),
                  dgp=st_pro.get(("dg_ok", c - 1))):
                if c == 0:
                    v.wait_ge(sV.h, E["dta_ready_V"])
                    v.wait_ge(sV.h, init_ok)
                else:
                    v.wait_ge(sCW.h, cw)
                    v.wait_ge(sP.h, cr)
                    if dgp is not None:
                        v.wait_ge(sV.h, dgp)
                sV.mark(v.tensor_tensor_scan(csf[:], dta[:, c * CH:(c + 1) * CH], zcs[:],
                                             0.0, OP.add, OP.add))
                v.wait_ge(sV.h, sV.v)
                if em is not None:
                    v.wait_ge(sP.h, em)
                sV.mark(v.tensor_scalar(diag64[:], i64f[:], csf[:, CH - 1:CH], None, OP.mult))
            st_pro[("cs_ok", c)] = sV.v - 1
            st_pro[("dg_ok", c)] = sV.v

            @blk.gpsimd
            def _(gq, c=c, cs_ok=st_pro[("cs_ok", c)], bcw=bcasts.get(c - 2)):
                gq.wait_ge(sV.h, cs_ok)
                if bcw is not None:
                    for s_, v_ in bcw:
                        gq.wait_ge(s_.h, v_)
                sCW.mark(gq.dma_start(out=cs_sp[(c % 2) * NH:(c % 2 + 1) * NH, :], in_=csf[:]), 16)
            cs_wr = sCW.v
            st_pro[("cs_wr", c)] = sCW.v

        def bcast01(c):
            for g in range(2):
                gp = g % 2

                @blk.gpsimd
                def _(gq, c=c, g=g, gp=gp, bcf=bc_free.get(gp), cw=st_pro[("cs_wr", c)]):
                    if c == 0:
                        gq.wait_ge(sA.h, E["g_ready_A"])
                    if bcf is not None:
                        gq.wait_ge(sA.h, bcf)
                    gq.wait_ge(sCW.h, cw)
                    sBC[gp].mark(gq.dma_start(
                        out=bca[gp][:].rearrange("p (h l) -> p h l", h=16),
                        in_=cs_sp[(c % 2) * NH + g * 16:(c % 2) * NH + (g + 1) * 16, :]
                        .unsqueeze(0).broadcast_to([128, 16, CH])), 16)
                st_pro[("bc_ok", g)] = sBC[gp].v

        def prologue(c):
            t0 = c * CH
            nonlocal cst_evac, btm_evac, sct_evac, csf_rd
            x_free = st_pro.get("x_free")
            z_free = st_pro.get("z_free")

            @blk.sync
            def _(s, c=c, t0=t0, xf=x_free, zf=z_free, xtl=st_pro.get("xts_last")):
                if xf is not None:
                    s.wait_ge(sP.h, xf)
                if xtl is not None:
                    s.wait_ge(sG.h, xtl)
                if zf is not None:
                    s.wait_ge(sG.h, zf)
                if c == 0:
                    for st, v in E["inproj_spill_finals"]:
                        s.wait_ge(st.h, v)
                    s.wait_ge(sA.h, E["g_ready_A"])
                    for st, v in E.get("g_sp_finals", []):
                        s.wait_ge(st.h, v)
                sXT.mark(s.dma_start_transpose(xtm[:], x_sp[:, t0:t0 + CH]), 16)
                sZT.mark(s.dma_start_transpose(zst[:], zs_sp[:, t0:t0 + CH]), 16)
            st_pro["x_ld"] = sXT.v
            st_pro["z_ld"] = sZT.v

            if c == 0:
                sched_cs(0)
                bcast01(0)
                xdp_pool(0)
            cs_ok = st_pro[("cs_ok", c)]
            st_pro["dg_ok"] = st_pro[("dg_ok", c)]

            @blk.tensor
            def _(t, c=c, cs_ok=cs_ok, so=scale_ok):
                t.wait_ge(sV.h, cs_ok)
                if c == 0:
                    t.wait_ge(sA.h, E["g_ready_A"])
                else:
                    t.wait_ge(sA.h, so)
                sP.mark(t.matmul(psm[0][0:128, 0:NH], csf[:], i64f[:], start=True, stop=True))
            csf_rd = sP.v
            cst_mm = sP.v

            @blk.scalar
            def _(sc, c=c, cst_mm=cst_mm, el=st_pro.get("evac_last"),
                  bl=st_pro.get("bank_last")):
                sc.wait_ge(sP.h, cst_mm)
                if el is not None:
                    sc.wait_ge(sV.h, el[0])
                sA.mark(sc.activation(cst[:], psm[0][:, 0:NH], AF.Copy))
                sc.wait_ge(sA.h, sA.v)
                sA.mark(sc.activation(expc[:], cst[:], AF.Exp))
            cst_evac = sA.v - 1
            st_pro["cst_evac"] = cst_evac
            expc_ok = sA.v

            @blk.tensor
            def _(t, c=c, t0=t0, ce=cst_evac):
                t.wait_ge(sA.h, ce)
                if c == 0:
                    t.wait_ge(sA.h, E["bc_ready_A"])
                sP.mark(t.matmul(psm[0][:, 256:384], bfm[:, t0:t0 + CH], i128[:], start=True, stop=True))
            btm_mm = sP.v

            @blk.scalar
            def _(sc, btm_mm=btm_mm, bl=st_pro.get("bank_last")):
                sc.wait_ge(sP.h, btm_mm)
                if bl is not None:
                    sc.wait_ge(sP.h, bl)
                sA.mark(sc.activation(btm[:], psm[0][:, 256:384], AF.Copy))
            btm_evac = sA.v
            st_pro["btm_evac"] = btm_evac

            @blk.tensor
            def _(t, t0=t0, be=btm_evac):
                t.wait_ge(sA.h, be)
                sP.mark(t.matmul(psm[0][:, 128:256], bfm[:, t0:t0 + CH], cfm[:, t0:t0 + CH],
                                 start=True, stop=True))
            sct_mm = sP.v

            @blk.vector
            def _(v, sct_mm=sct_mm):
                v.wait_ge(sP.h, sct_mm)
                sV.mark(v.tensor_tensor(sctm[:], psm[0][:, 128:256], mask128[:], OP.mult))
            sct_evac = sV.v
            st_pro["sct_evac"] = sV.v

            @blk.tensor
            def _(t, dg_ok=st_pro["dg_ok"], se=sct_evac):
                t.wait_ge(sV.h, dg_ok)
                t.wait_ge(sV.h, se)
                sP.mark(t.matmul(psm[0][:, 64:128], ones64[:], diag64[:], start=True, stop=True))
            st_pro["eto_mm"] = sP.v

            @blk.scalar
            def _(sc, eto_mm=st_pro["eto_mm"], el=st_pro.get("evac_last")):
                if el is not None:
                    sc.wait_ge(sV.h, el[0])
                    sc.wait_ge(sG.h, el[1])
                sc.wait_ge(sP.h, eto_mm)
                sA.mark(sc.activation(eto[:], psm[0][:, 64:128], AF.Exp))
            st_pro["scale_ok"] = sA.v
            st_pro["expc_ok"] = expc_ok

        def main(c):
            nonlocal sbf_war, btl_prev, scale_ok, x_free_out
            scale_ok = st_pro["scale_ok"]
            t0 = c * CH
            x_ld = st_pro["x_ld"]
            z_ld = st_pro["z_ld"]

            if c + 1 < NCH:
                sched_cs(c + 1)
            xdp_ok = st_pro[("xdp_ok", c)]
            cst_ev = st_pro["cst_evac"]
            btm_ev = st_pro["btm_evac"]
            sct_ev = st_pro["sct_evac"]
            eto_mm = st_pro["eto_mm"]
            pt_oks = {}
            btl_oks = {}
            d_oks = {}
            e_oks = {}

            for g in range(4):
                gp = g % 2
                if g >= 2:
                    @blk.gpsimd
                    def _(gq, c=c, g=g, gp=gp, bcf=e_oks[g - 2], cw=st_pro[("cs_wr", c)]):
                        gq.wait_ge(sA.h, bcf)
                        gq.wait_ge(sCW.h, cw)
                        sBC[gp].mark(gq.dma_start(
                            out=bca[gp][:].rearrange("p (h l) -> p h l", h=16),
                            in_=cs_sp[(c % 2) * NH + g * 16:(c % 2) * NH + (g + 1) * 16, :]
                            .unsqueeze(0).broadcast_to([128, 16, CH])), 16)
                    st_pro[("bc_ok", g)] = sBC[gp].v
                    bcasts[c] = [(sBC[0], sBC[0].v), (sBC[1], sBC[1].v)]

                @blk.vector
                def _(v, g=g, gp=gp, bc_ok=st_pro[("bc_ok", g)], cst_ev=cst_ev,
                      ef=e_oks.get(g - 2), dw=d_oks.get(g - 1),
                      tpl=st_pro.get("tp_last") if g == 0 else None):
                    v.wait_ge(sBC[gp].h, bc_ok)
                    if tpl is not None:
                        v.wait_ge(sP.h, tpl)
                    v.wait_ge(sA.h, cst_ev)
                    if ef is not None:
                        v.wait_ge(sA.h, ef)
                    if dw is not None:
                        v.wait_ge(sV.h, dw)
                    sV.mark(v.tensor_tensor(
                        dT[0].rearrange("p (h l) -> p h l", h=16),
                        bca[gp][:].rearrange("p (h l) -> p h l", h=16),
                        cst[:, g * 16:(g + 1) * 16].unsqueeze(2).broadcast_to([128, 16, CH]),
                        OP.subtract))
                    v.wait_ge(sV.h, sV.v)
                    sV.mark(v.tensor_scalar(bca[gp][:], dT[0], 0.0, None, OP.min))
                d_oks[g] = sV.v

                @blk.scalar
                def _(sc, c=c, g=g, gp=gp, d_ok=d_oks[g],
                      ew=btl_prev.get(g) if btl_prev else None):
                    sc.wait_ge(sV.h, d_ok)
                    if ew is not None:
                        sc.wait_ge(sV.h, ew[0])
                        sc.wait_ge(sG.h, ew[1])
                    sA.mark(sc.activation(ebuf[g][:], bca[gp][:], AF.Exp))
                e_oks[g] = sA.v
                bc_free[gp] = sA.v

            xts_oks = {}
            for g in range(4):
                gp = g % 2

                @blk.vector
                def _(v, g=g, gp=gp, e_ok=e_oks[g], sct_ev=sct_ev, ptf=pt_free.get(g)):
                    v.wait_ge(sA.h, e_ok)
                    v.wait_ge(sV.h, sct_ev)
                    if ptf is not None:
                        v.wait_ge(sP.h, ptf)
                    sV.mark(v.tensor_tensor(pt[g][:].rearrange("p (h l) -> p h l", h=16),
                                            ebuf[g][:].rearrange("p (h l) -> p h l", h=16),
                                            sctm[:].unsqueeze(1).broadcast_to([128, 16, CH]),
                                            OP.mult))
                pt_oks[g] = sV.v

                @blk.gpsimd
                def _(gq, g=g, e_ok=e_oks[g], ptf=pt_free.get(g)):
                    gq.wait_ge(sA.h, e_ok)
                    if ptf is not None:
                        gq.wait_ge(sP.h, ptf)
                    sG.mark(gq.tensor_tensor(
                        xts[g][:].rearrange("p (h x) -> p h x", h=16),
                        xtm[:, g * 1024:(g + 1) * 1024].rearrange("p (h x) -> p h x", h=16),
                        ebuf[g][:, CH - 1::CH].unsqueeze(2).broadcast_to([128, 16, 64]),
                        OP.mult))
                xts_oks[g] = sG.v
                btl_oks[g] = (sV.v, sG.v)
            st_pro["xts_last"] = sG.v

            if c + 1 < NCH:
                bcast01(c + 1)

            if c > 0:
                pieces_evac(c - 1)

            @blk.gpsimd
            def _(gq, c=c, pw0=st_pro.get(("pev", 0)), pw1=st_pro.get(("pev", 1))):
                if c == 0:
                    for st, vv in E.get("g_sp_finals", []):
                        gq.wait_ge(st.h, vv)
                for pc, pw in ((0, pw0), (1, pw1)):
                    if pw is not None:
                        gq.wait_ge(sV.h, pw)
                    sGT[pc].mark(gq.dma_start(
                        out=gt2[pc][:], in_=g_sp[:, c * D + pc * 512:c * D + (pc + 1) * 512]), 16)
                    sXI[pc].mark(gq.dma_start(
                        out=xt2[pc][:], in_=x_in[c * CH:(c + 1) * CH, pc * 512:(pc + 1) * 512]), 16)
            st_pro[("gt_ok", 0)] = sGT[0].v
            st_pro[("gt_ok", 1)] = sGT[1].v
            st_pro[("xi_ok", 0)] = sXI[0].v
            st_pro[("xi_ok", 1)] = sXI[1].v

            sdec = [ynb[:, 0:512], ynb[:, 512:1024]]
            sbf_marks = st_pro.get("sbf_marks") or {b: init_ok for b in range(8)}
            new_sbf = {}
            stmp_oks = {}
            ybuf_oks = {}
            for b in range(8):
                g = b // 2
                bi = b % 2

                @blk.gpsimd
                def _(gq, c=c, b=b, so=scale_ok, sbw=sbf_marks[b],
                      sdw=new_sbf.get(b - 2), tpl=st_pro.get("tp_last") if b == 0 else None):
                    gq.wait_ge(sA.h, so)
                    gq.wait_ge(sV.h, sbw)
                    if tpl is not None:
                        gq.wait_ge(sP.h, tpl)
                    if sdw is not None:
                        gq.wait_ge(sV.h, sdw)
                    etv = eto[:, b * 8:(b + 1) * 8].unsqueeze(2).broadcast_to([128, 8, 64])
                    sG.mark(gq.tensor_tensor(
                        sdec[b % 2].rearrange("p (h x) -> p h x", h=8),
                        Sbf[:, b * 512:(b + 1) * 512].rearrange("p (h x) -> p h x", h=8),
                        etv, OP.mult))
                dec_ok = sG.v

                @blk.tensor
                def _(t, c=c, g=g, b=b, bi=bi, x_ld=x_ld, btl_ok=btl_oks[g],
                      btm_ev=btm_ev, sw=sbf_war, ew=ebank.get(bi), xdp_ok=xdp_ok,
                      pev=st_pro.get(("pev", b)) if b < 2 else None):
                    t.wait_ge(sV.h, btl_ok[0])
                    t.wait_ge(sG.h, btl_ok[1])
                    t.wait_ge(sXT.h, x_ld)
                    t.wait_ge(sG.h, xdp_ok)
                    if pev is not None:
                        t.wait_ge(sV.h, pev)
                    if sw is not None:
                        t.wait_ge(sV.h, sw)
                    if ew is not None:
                        t.wait_ge(sV.h, ew)
                    t.matmul(pY[bi][:], i128[:], xdp[:, b * 512:(b + 1) * 512],
                             start=True, stop=False)
                    for hh in range(8):
                        h = b * 8 + hh
                        hl = (h % 16) * CH
                        t.matmul(pY[bi][:, hh * 64:(hh + 1) * 64],
                                 pt[g][:, hl:hl + CH],
                                 xtm[:, h * 64:(h + 1) * 64], start=False, stop=(hh == 7))
                    for hh in range(8):
                        h = b * 8 + hh
                        t.matmul(pYo[bi][:, hh * 64:(hh + 1) * 64],
                                 cfm[:, c * CH:(c + 1) * CH],
                                 Sbf[:, h * 64:(h + 1) * 64], start=True, stop=True)
                    t.wait_ge(sA.h, btm_ev)
                    for hh in range(8):
                        h = b * 8 + hh
                        hx = (h % 16) * 64
                        mi = t.matmul(pSt[bi][:, hh * 64:(hh + 1) * 64],
                                      btm[:],
                                      xts[g][:, hx:hx + 64], start=True, stop=True)
                    sP.mark(mi)
                bank_mm = sP.v
                if b == 7:
                    st_pro["x_free"] = sP.v
                    st_pro["bank_last"] = sP.v
                pt_free[g] = sP.v

                @blk.vector
                def _(v, b=b, bi=bi, bank_mm=bank_mm, so=scale_ok, dec_ok=dec_ok,
                      ybw=ybuf_oks.get(b - 1)):
                    v.wait_ge(sP.h, bank_mm)
                    v.wait_ge(sA.h, so)
                    ecv = expc[:, b * 8:(b + 1) * 8].unsqueeze(2).broadcast_to([128, 8, 64])
                    sV.mark(v.tensor_tensor(ytmp[:].rearrange("p (h x) -> p h x", h=8),
                                            pYo[bi][:].rearrange("p (h x) -> p h x", h=8), ecv, OP.mult))
                    v.wait_ge(sV.h, sV.v)
                    if ybw is not None:
                        v.wait_ge(sG.h, ybw)
                    sV.mark(v.scalar_tensor_tensor(stmp[:], ytmp[:], 1.0, pY[bi][:], OP.mult, OP.add))
                    v.wait_ge(sV.h, sV.v)
                    v.wait_ge(sG.h, dec_ok)
                    sV.mark(v.tensor_tensor(Sbf[:, b * 512:(b + 1) * 512], sdec[b % 2], pSt[bi][:], OP.add))
                stmp_oks[b] = sV.v - 1
                new_sbf[b] = sV.v
                ebank[bi] = sV.v

                @blk.gpsimd
                def _(gq, b=b, z_ld=z_ld, smk=stmp_oks[b],
                      obs=(st_pro.get(("ob_st", 0)), st_pro.get(("ob_st", 1)))):
                    gq.wait_ge(sV.h, smk)
                    gq.wait_ge(sZT.h, z_ld)
                    if b == 0:
                        for pc in range(2):
                            if obs[pc] is not None:
                                gq.wait_ge(sOB2[pc].h, obs[pc])
                    sG.mark(gq.tensor_tensor(ybuf[:, b * 512:(b + 1) * 512], stmp[:],
                                             zst[:, b * 512:(b + 1) * 512], OP.mult))
                ybuf_oks[b] = sG.v
            sbf_war = sV.v
            btl_prev = dict(btl_oks)
            st_pro["sbf_marks"] = new_sbf
            st_pro["evac_last"] = (sV.v, sG.v)
            st_pro["z_free"] = sG.v
            st_pro["y_ok"] = (sV.v, sG.v)

        def tail(c):
            nonlocal ynf_evac
            y_ok = st_pro["y_ok"]

            @blk.scalar
            def _(sc, y_ok=y_ok, ynf=yn_free.get("b")):
                sc.wait_ge(sV.h, y_ok[0])
                sc.wait_ge(sG.h, y_ok[1])
                if ynf is not None:
                    sc.wait_ge(sYN.h, ynf)
                sA.mark(sc.activation(ynb[:], ybuf[:], AF.Square, accum_out=ssq[:, 0:1]))
                sc.wait_ge(sA.h, sA.v)
                sA.mark(sc.activation(ssq[:, 3:4], ssq[:, 0:1], AF.Ln, scale=1.0 / DI, bias=E["epsc"][:]))
                sc.wait_ge(sA.h, sA.v)
                sA.mark(sc.activation(rstd[:], ssq[:, 3:4], AF.Exp, scale=-0.5))
            rstd_ok = sA.v

            @blk.vector
            def _(v, rstd_ok=rstd_ok):
                v.wait_ge(sA.h, rstd_ok)
                sV.mark(v.scalar_tensor_tensor(ynb[:], ybuf[:], rstd[:], nwb[:], OP.mult, OP.mult))
            yn_ok = sV.v

            sPb, sAb = sP.v, sA.v
            for grp in range(8):
                pb = psm[1] if grp % 2 == 0 else pY[0]

                @blk.tensor
                def _(t, grp=grp, pb=pb, yn_ok=yn_ok, sAb=sAb):
                    t.wait_ge(sV.h, yn_ok)
                    if grp >= 2:
                        t.wait_ge(sA.h, sAb + grp - 1)
                    for j in range(4):
                        mi = t.matmul(pb[:, j * 128:(j + 1) * 128],
                                      ynb[:, (grp * 4 + j) * 128:(grp * 4 + j + 1) * 128],
                                      i128[:], start=(j == 0), stop=(j == 3))
                    sP.mark(mi)
                if grp == 7:
                    st_pro["tp_last"] = sP.v

            @blk.scalar
            def _(sc, c=c, sPb=sPb, ynw=yn_free.get("b")):
                for grp in range(8):
                    sc.wait_ge(sP.h, sPb + grp + 1)
                    if grp == 0 and ynw is not None:
                        sc.wait_ge(sYN.h, ynw)
                    pb = psm[1] if grp % 2 == 0 else pY[0]
                    sA.mark(sc.activation(xdp[:, grp * 512:(grp + 1) * 512], pb[:], AF.Copy))
            ynf_evac = sA.v

            @blk.sync
            def _(s, c=c, ye=ynf_evac):
                s.wait_ge(sA.h, ye)
                sYN.mark(s.dma_start(
                    out=yn_sp[:, c * CH:(c + 1) * CH].rearrange("(kc p) t -> p kc t", p=128),
                    in_=xdp[:].rearrange("p (kc t) -> p kc t", kc=32)), 16)
            yn_free["b"] = sYN.v

        def pieces_mm(c):
            ye = ynf_evac
            for pc in range(2):
                @blk.tensor
                def _(t, c=c, pc=pc, ye=ye, ebk=ebank.get(pc)):
                    t.wait_ge(sA.h, ye)
                    if ebk is not None:
                        t.wait_ge(sV.h, ebk)
                    if c == 0:
                        t.wait_ge(sWH0[pc].h, 16)
                    for kc in range(32):
                        mi = t.matmul(pSt[pc][:],
                                      xdp[:, kc * 128:(kc + 1) * 128],
                                      whh[:, pc * 16384 + kc * 512:pc * 16384 + (kc + 1) * 512],
                                      start=(kc == 0), stop=(kc == 31))
                    sP.mark(mi)
                st_pro[("pmm", pc)] = sP.v
            st_pro["piece_mm"] = sP.v
            if c + 1 < NCH:
                xdp_pool(c + 1)

        def pieces_evac(cc):
            for pc in range(2):
                @blk.vector
                def _(v, cc=cc, pc=pc, p_mm=st_pro[("pmm", pc)],
                      gt_ok=st_pro[("gt_ok", pc)], xi_ok=st_pro[("xi_ok", pc)],
                      obw=st_pro.get(("ob_st", pc)),
                      pev0=st_pro.get(("pev", 0)) if pc == 1 else None):
                    v.wait_ge(sP.h, p_mm)
                    v.wait_ge(sGT[pc].h, gt_ok)
                    v.wait_ge(sXI[pc].h, xi_ok)
                    if obw is not None:
                        v.wait_ge(sOB2[pc].h, obw)
                    if pev0 is not None:
                        v.wait_ge(sV.h, pev0)
                    sV.mark(v.tensor_tensor(t1w[:], pSt[pc][:], gt2[pc][:], OP.mult))
                    v.wait_ge(sV.h, sV.v)
                    sV.mark(v.tensor_tensor(ob2[pc][:], t1w[:], xt2[pc][:], OP.add))
                st_pro[("pev", pc)] = sV.v
                p_ev = sV.v

                @blk.sync
                def _(s, cc=cc, pc=pc, p_ev=p_ev):
                    s.wait_ge(sV.h, p_ev)
                    sOB2[pc].mark(s.dma_start(
                        out=out[cc * CH:(cc + 1) * CH, pc * 512:(pc + 1) * 512], in_=ob2[pc][:]), 16)
                st_pro[("ob_st", pc)] = sOB2[pc].v

        x_free_out = None
        prologue(0)
        for c in range(NCH):
            main(c)
            if c + 1 < NCH:
                prologue(c + 1)
            tail(c)
            pieces_mm(c)
        pieces_evac(NCH - 1)
    E["ssd_spill_final"] = (sYN, sYN.v)
    E["ssd_pe_final"] = E["sP"].v
    E["ssd_v_final"] = E["sV"].v
    E["ssd_a_final"] = E["sA"].v
    E["ssd_ob_finals"] = [(sOB2[i], sOB2[i].v) for i in range(2)]


def _wout(E):
    nc, blk = E["nc"], E["blk"]
    sP, sV, sA = E["sP"], E["sV"], E["sA"]
    g_sp, x_in, out, yn_sp, wout_t = E["g_sp"], E["x_in"], E["out"], E["yn_sp"], E["wout_t"]
    sWH = mkstream(E, "sWH")
    sYF = [mkstream(E, f"sYF{i}") for i in range(2)]
    sWX = [mkstream(E, f"sWX{i}") for i in range(4)]
    sGL = [mkstream(E, f"sGL{i}") for i in range(4)]
    sOB = [mkstream(E, f"sOB{i}") for i in range(4)]

    with ExitStack() as c2:
        lsb = lambda n, s, dt: c2.enter_context(nc.sbuf_tensor(n, s, dt))
        wh = [E["whh"]] * 2
        ynf = [lsb(f"wo_yn{i}", [128, 32 * 256], BF16) for i in range(2)]
        xt = [lsb(f"wo_x{i}", [128, 512], F32) for i in range(4)]
        gtl = [lsb(f"wo_g{i}", [128, 512], BF16) for i in range(4)]
        ob = [lsb(f"wo_o{i}", [128, 512], F32) for i in range(4)]
        t1 = [lsb(f"wo_t{i}", [128, 512], F32) for i in range(2)]
        pw = [c2.enter_context(nc.psum_tensor(f"wo_p{i}", [128, 512], F32)) for i in range(4)]

        bank_free = {}
        wh_rd = None
        ynf_rd = {}
        xt_free = {}
        ob_free = {}
        ii = 0
        syn, syv = E["ssd_spill_final"]

        def ssd_guards(q):
            q.wait_ge(syn.h, syv)
            q.wait_ge(sP.h, E["ssd_pe_final"])
            q.wait_ge(sV.h, E["ssd_v_final"])
            q.wait_ge(sA.h, E["ssd_a_final"])
            for st, v in E["ssd_ob_finals"]:
                q.wait_ge(st.h, v)

        for nh in (1,):
            whb = wh[0]

            @blk.gpsimd
            def _(gq):
                ssd_guards(gq)
                sWH.mark(gq.dma_start(
                    out=whb[:].rearrange("p (kc n) -> p kc n", kc=32),
                    in_=wout_t[:, 1024:2048].rearrange("(kc p) n -> p kc n", p=128)), 16)
            wh_ok = (sWH, sWH.v)

            for blk8 in range(8):
                yi = blk8 % 2
                ybuf2 = ynf[yi]
                yrd = ynf_rd.get(yi)
                @blk.sync
                def _(s, nh=nh, blk8=blk8, yi=yi, ybuf2=ybuf2, yrd=yrd):
                    if yrd is not None:
                        s.wait_ge(sP.h, yrd)
                    if blk8 < 2:
                        ssd_guards(s)
                    sYF[yi].mark(s.dma_start(
                        out=ybuf2[:].rearrange("p (kc t) -> p kc t", kc=32),
                        in_=yn_sp[:, blk8 * 256:(blk8 + 1) * 256].rearrange("(kc p) t -> p kc t", p=128)), 16)
                yn_ok0 = sYF[yi].v

                for ms in range(2):
                    for nb in range(2):
                        bi = ii % 4
                        pwar = bank_free.get(bi)
                        xwar = xt_free.get(bi)

                        mtile0 = blk8 * 2 + ms
                        gsl0 = slice(mtile0 * D + nh * 1024 + nb * 512,
                                     mtile0 * D + nh * 1024 + (nb + 1) * 512)

                        @blk.sync
                        def _(s, ii=ii, nh=nh, blk8=blk8, ms=ms, nb=nb, bi=bi, xwar=xwar, gsl0=gsl0):
                            if ii < 4:
                                ssd_guards(s)
                                for st, v in E["g_sp_finals"]:
                                    s.wait_ge(st.h, v)
                            if xwar is not None:
                                s.wait_ge(sV.h, xwar)
                            r0 = blk8 * 256 + ms * 128
                            c0 = nh * 1024 + nb * 512
                            sWX[bi].mark(s.dma_start(out=xt[bi][:], in_=x_in[r0:r0 + 128, c0:c0 + 512]), 16)
                            sGL[bi].mark(s.dma_start(out=gtl[bi][:], in_=g_sp[:, gsl0]), 16)
                        x_ok = sWX[bi].v
                        g_ok = sGL[bi].v

                        @blk.tensor
                        def _(t, ii=ii, yi=yi, ms=ms, nb=nb, bi=bi, wh_ok=wh_ok, whb2=whb,
                              yn_ok0=yn_ok0, ybuf2=ybuf2, pwar=pwar):
                            if ii == 0:
                                t.wait_ge(syn.h, syv)
                            t.wait_ge(wh_ok[0].h, wh_ok[1])
                            t.wait_ge(sYF[yi].h, yn_ok0)
                            if pwar is not None:
                                t.wait_ge(sV.h, pwar)
                            for kc in range(32):
                                mi = t.matmul(pw[bi][:], ybuf2[:, kc * 256 + ms * 128: kc * 256 + (ms + 1) * 128],
                                              whb2[:, kc * 1024 + nb * 512: kc * 1024 + (nb + 1) * 512],
                                              start=(kc == 0), stop=(kc == 31))
                            sP.mark(mi)
                        mm_ok = sP.v
                        wh_rd = sP.v
                        ynf_rd[yi] = sP.v

                        @blk.vector
                        def _(v, ii=ii, bi=bi, mm_ok=mm_ok, x_ok=x_ok, g_ok=g_ok,
                              obw=ob_free.get(ii % 4)):
                            v.wait_ge(sP.h, mm_ok)
                            v.wait_ge(sWX[bi].h, x_ok)
                            v.wait_ge(sGL[bi].h, g_ok)
                            if ii < 4:
                                for st, vv in E["ssd_ob_finals"]:
                                    v.wait_ge(st.h, vv)
                            if obw is not None:
                                v.wait_ge(obw[0].h, obw[1])
                            sV.mark(v.tensor_tensor(t1[bi % 2][:], pw[bi][:], gtl[bi][:], OP.mult))
                            v.wait_ge(sV.h, sV.v)
                            sV.mark(v.tensor_tensor(ob[bi][:], t1[bi % 2][:], xt[bi][:], OP.add))
                        bank_free[bi] = sV.v
                        xt_free[bi] = sV.v
                        o_ok = sV.v

                        @blk.sync
                        def _(s, nh=nh, blk8=blk8, ms=ms, nb=nb, bi=bi, o_ok=o_ok):
                            s.wait_ge(sV.h, o_ok)
                            r0 = blk8 * 256 + ms * 128
                            c0 = nh * 1024 + nb * 512
                            sOB[bi].mark(s.dma_start(out=out[r0:r0 + 128, c0:c0 + 512], in_=ob[bi][:]), 16)
                        ob_free[bi] = (sOB[bi], sOB[bi].v)
                        ii += 1


def _dbg_xn(E):
    blk, sA = E["blk"], E["sA"]
    sDX = mkstream(E, "sDX")

    @blk.gpsimd
    def _(g):
        xr = E["xn_ready"]
        g.wait_ge(xr[0].h, xr[1])
        sDX.mark(g.dma_start(out=E["dbg_xn"], in_=E["xn"][:]), 16)
    E["dbg_xn_done"] = (sDX, sDX.v)


def _dbg_dump(E):
    blk, sV, sA = E["blk"], E["sV"], E["sA"]
    sDB = mkstream(E, "sDB")

    @blk.gpsimd
    def _(g):
        g.wait_ge(sV.h, sV.v)
        g.wait_ge(sA.h, sA.v)
        if "cmb_ready_V" in E:
            sDB.mark(g.dma_start(out=E["dbg_cmb"], in_=E["cmb"][:]), 16)
        if "bc_ready_A" in E:
            sDB.mark(g.dma_start(out=E["dbg_bc"][0:128, :], in_=E["bfm"][:]), 16)
            sDB.mark(g.dma_start(out=E["dbg_bc"][128:256, :], in_=E["cfm"][:]), 16)
        if "dta_ready_V" in E:
            sDB.mark(g.dma_start(out=E["dbg_dta"], in_=E["dta"][:]), 16)


def make_in_maps(u, W_in, W_conv, b_conv, dt_bias, A_log, Dp, norm_w, W_out,
                 gate_w, w1, w2, w3, sw1, sw2, sw3):
    f = lambda a: np.ascontiguousarray(np.asarray(a, dtype=np.float32))
    shared = dict(
        win_t=f(np.asarray(W_in).T),
        wconv_r=f(np.asarray(W_conv)[:, 0, :].reshape(34, 128, 4).transpose(1, 0, 2).reshape(128, 34 * 4)),
        bconv_r=f(np.asarray(b_conv).reshape(34, 128).T),
        dtb_c=f(np.asarray(dt_bias).reshape(NH, 1)),
        alog_c=f(np.asarray(A_log).reshape(NH, 1)),
        dp_row=f(np.asarray(Dp).reshape(1, NH)),
        nw_row=f(np.asarray(norm_w).reshape(1, DI)),
        wout_t=f(np.asarray(W_out).T),
        gate_t=f(np.asarray(gate_w).T),
        w1=f(w1), w2=f(w2), w3=f(w3),
        sw1=f(sw1), sw2=f(sw2), sw3=f(sw3),
        i128_in=np.eye(128, dtype=ml_dtypes.bfloat16),
        i64_in=np.eye(NH, dtype=np.float32),
        mneg_in=np.where(np.arange(128)[:, None] > np.arange(CH)[None, :],
                         np.float32(BIGNEG), np.float32(0.0)),
        ones64_in=np.ones((NH, 128), np.float32),
        mask_in=np.where(np.arange(128)[:, None] <= np.arange(CH)[None, :],
                         ml_dtypes.bfloat16(1.0), ml_dtypes.bfloat16(0.0)),
    )
    u = np.asarray(u, np.float32)
    return [dict(shared, x=np.ascontiguousarray(u[i])) for i in range(8)]


def kernel(L0_attn_u, L0_attn_W_in, L0_attn_W_conv, L0_attn_b_conv,
           L0_attn_dt_bias, L0_attn_A_log, L0_attn_Dp, L0_attn_norm_w,
           L0_attn_W_out, L0_ffn_gate_w, L0_ffn_w1, L0_ffn_w2, L0_ffn_w3,
           L0_ffn_sw1, L0_ffn_sw2, L0_ffn_sw3):
    nc = build()
    in_maps = make_in_maps(
        L0_attn_u, L0_attn_W_in, L0_attn_W_conv, L0_attn_b_conv,
        L0_attn_dt_bias, L0_attn_A_log, L0_attn_Dp, L0_attn_norm_w,
        L0_attn_W_out, L0_ffn_gate_w, L0_ffn_w1, L0_ffn_w2, L0_ffn_w3,
        L0_ffn_sw1, L0_ffn_sw2, L0_ffn_sw3)
    res = run_bass_kernel_spmd(nc, in_maps, core_ids=list(range(8)))
    return np.stack([np.asarray(r["out"], np.float32) for r in res.results], axis=0)


# version 17
# speedup vs baseline: 1.0632x; 1.0061x over previous
import numpy as np
import ml_dtypes
from contextlib import ExitStack

import concourse.bass as bass
import concourse.mybir as mybir
from concourse.bass_utils import run_bass_kernel_spmd
from concourse.alu_op_type import AluOpType as OP

F32 = mybir.dt.float32
BF16 = mybir.dt.bfloat16
U32 = mybir.dt.uint32
AF = mybir.ActivationFunctionType
AX = mybir.AxisListType

D = 2048
L = 2048
DI = 4096
NH = 64
CH = 128
NCH = L // CH
FH = 1408
NE = 9
KC = D // 128
BIGNEG = -1e9
PH_LN, PH_INPROJ, PH_GATE, PH_F1, PH_F2, PH_SSD, PH_WOUT = range(7)


class Sem:
    def __init__(self, nc, ctx, name):
        self.h = ctx.enter_context(nc.semaphore(name))
        self.v = 0

    def mark(self, inst, n=1):
        inst.then_inc(self.h, n)
        self.v += n
        return self.v


def mkstream(E, name):
    s = Sem(E["nc"], E["ctx"], name)
    E.setdefault("streams", []).append(s)
    return s


def build(max_phase=PH_WOUT, dbg=False):
    nc = bass.Bass("TRN2", target_bir_lowering=False)
    dram = lambda n, s, dt=F32, kind="ExternalInput": nc.dram_tensor(n, s, dt, kind=kind).ap()

    E = {"nc": nc, "dbg": dbg}
    E["x_in"] = dram("x", [L, D])
    E["win_t"] = dram("win_t", [D, 8512])
    E["wconv_r"] = dram("wconv_r", [128, 34 * 4])
    E["bconv_r"] = dram("bconv_r", [128, 34])
    E["dtb_c"] = dram("dtb_c", [NH, 1])
    E["alog_c"] = dram("alog_c", [NH, 1])
    E["dp_row"] = dram("dp_row", [1, NH])
    E["nw_row"] = dram("nw_row", [1, DI])
    E["wout_t"] = dram("wout_t", [DI, D])
    E["gate_t"] = dram("gate_t", [D, 8])
    E["w1"] = dram("w1", [8, D, FH])
    E["w2"] = dram("w2", [8, FH, D])
    E["w3"] = dram("w3", [8, D, FH])
    E["sw1"] = dram("sw1", [D, FH])
    E["sw2"] = dram("sw2", [FH, D])
    E["sw3"] = dram("sw3", [D, FH])
    E["i128_in"] = dram("i128_in", [128, 128], BF16)
    E["i64_in"] = dram("i64_in", [NH, NH])
    E["mneg_in"] = dram("mneg_in", [128, CH])
    E["ones64_in"] = dram("ones64_in", [NH, 128])
    E["mask_in"] = dram("mask_in", [128, CH], BF16)
    E["out"] = dram("out", [L, D], F32, kind="ExternalOutput")

    ik = "ExternalOutput" if dbg else "Internal"
    E["zs_sp"] = dram("zs_sp", [DI, L], BF16, kind=ik)
    E["x_sp"] = dram("x_sp", [DI, L], BF16, kind=ik)
    E["h_sp"] = dram("h_sp", [NE * FH, L], BF16, kind=ik)
    E["yn_sp"] = dram("yn_sp", [DI, L], BF16, kind=ik)
    E["cs_sp"] = dram("cs_sp", [2 * NH, CH], F32, kind="Internal")
    E["g_sp"] = dram("g_sp", [128, 16 * D], BF16, kind=ik)
    if dbg:
        E["dbg_bc"] = dram("dbg_bc", [256, L], F32, kind="ExternalOutput")
        E["dbg_dta"] = dram("dbg_dta", [NH, L], F32, kind="ExternalOutput")
        E["dbg_cmb"] = dram("dbg_cmb", [128, 16 * 8], F32, kind="ExternalOutput")
        E["dbg_xn"] = dram("dbg_xn", [128, KC * L], F32, kind="ExternalOutput")

    with ExitStack() as ctx:
        E["ctx"] = ctx
        sb = lambda n, s, dt: ctx.enter_context(nc.sbuf_tensor(n, s, dt))
        for n, s, dt in [
            ("i128", [128, 128], BF16), ("i64f", [NH, NH], F32),
            ("bfm", [128, L], BF16), ("cfm", [128, L], BF16),
            ("dta", [NH, L], F32), ("cmb", [128, 16 * 8], F32),
            ("wconv", [128, 34 * 4], F32), ("bconv", [128, 34], F32),
            ("dtb", [NH, 1], F32), ("nega", [NH, 1], F32), ("scrc", [NH, 1], F32),
            ("dpb", [128, NH], F32), ("dpbbf", [128, NH], BF16),
            ("nwb", [128, DI], BF16),
            ("epsc", [128, 1], F32), ("ones64", [NH, 128], F32),
            ("mask128", [128, CH], BF16),
        ]:
            E[n] = sb(n, s, dt)
        for n in ("sP", "sV", "sA"):
            E[n] = Sem(nc, ctx, n)
        E["blk"] = ctx.enter_context(nc.Block())

        _init(E)
        with ExitStack() as ctx_xn:
            E["xn"] = ctx_xn.enter_context(nc.sbuf_tensor("xn", [128, KC * L], BF16))
            with ExitStack() as ctx_head:
                E["pT"] = [ctx_head.enter_context(nc.psum_tensor(f"h_pT{i}", [128, 512], F32)) for i in range(2)]
                E["pmm"] = [ctx_head.enter_context(nc.psum_tensor(f"h_pm{i}", [128, 512], F32)) for i in range(4)]
                E["pg"] = ctx_head.enter_context(nc.psum_tensor("h_pg", [128, 128], F32))
                hsb = lambda n, s, dt: ctx_head.enter_context(nc.sbuf_tensor(n, s, dt))
                E["wblk"] = [hsb(f"ip_w{i}", [128, KC * 128], BF16) for i in range(2)]
                E["zstage"] = [hsb(f"ip_zs{i}", [128, L], BF16) for i in range(2)]
                E["xstage"] = [hsb(f"ip_xst{i}", [128, L], BF16) for i in range(2)]
                E["xpre"] = hsb("ip_xpre", [128, L + 3], F32)
                E["cv"] = hsb("ip_cv", [128, L], F32)
                E["cva"] = hsb("ip_cva", [128, L], F32)
                E["dstage"] = hsb("ip_dst", [NH, L], F32)
                E["dt0"] = hsb("ip_dt0", [NH, L], F32)
                E["dt1"] = hsb("ip_dt1", [NH, L], F32)
                E["gwb"] = hsb("gt_w", [128, KC * 8], BF16)
                E["gsc"] = hsb("gt_s", [128, 128], F32)
                _ln(E)
                if max_phase >= PH_INPROJ:
                    _inproj(E)
                if max_phase >= PH_GATE:
                    _gate(E)
            if max_phase >= PH_F1:
                _f1(E)
            if dbg:
                _dbg_xn(E)
        if max_phase >= PH_F2:
            _f2(E)
        if max_phase >= PH_SSD:
            E["whh"] = ctx.enter_context(nc.sbuf_tensor("whh", [128, 32 * 1024], BF16))
            _ssd(E)
        if max_phase >= PH_WOUT:
            _wout(E)
        if dbg:
            _dbg_dump(E)

        @E["blk"].sync
        def _(s):
            for st in E.get("streams", []):
                s.wait_ge(st.h, st.v)
            s.wait_ge(E["sP"].h, E["sP"].v)
            s.wait_ge(E["sV"].h, E["sV"].v)
            s.wait_ge(E["sA"].h, E["sA"].v)
    return nc


def _init(E):
    blk, sA, sV = E["blk"], E["sA"], E["sV"]
    sIC = mkstream(E, "sIC")

    @blk.gpsimd
    def _(g):
        sIC.mark(g.dma_start(out=E["i128"][:], in_=E["i128_in"]), 16)
        sIC.mark(g.dma_start(out=E["i64f"][:], in_=E["i64_in"]), 16)
        sIC.mark(g.dma_start(out=E["wconv"][:], in_=E["wconv_r"]), 16)
        sIC.mark(g.dma_start(out=E["bconv"][:], in_=E["bconv_r"]), 16)
        sIC.mark(g.dma_start(out=E["dtb"][:], in_=E["dtb_c"]), 16)
        sIC.mark(g.dma_start(out=E["scrc"][:], in_=E["alog_c"]), 16)
        sIC.mark(g.dma_start(out=E["dpb"][:], in_=E["dp_row"].broadcast_to([128, NH])), 16)
        sIC.mark(g.dma_start(out=E["nwb"][:], in_=E["nw_row"].broadcast_to([128, DI])), 16)
        sIC.mark(g.dma_start(out=E["ones64"][:], in_=E["ones64_in"]), 16)
        sIC.mark(g.dma_start(out=E["mask128"][:], in_=E["mask_in"]), 16)
    E["init_loads"] = (sIC, sIC.v)

    @blk.scalar
    def _(sc):
        sc.wait_ge(sIC.h, sIC.v)
        sA.mark(sc.activation(E["nega"][:], E["scrc"][:], AF.Exp))
        sA.mark(sc.activation(E["dpbbf"][:], E["dpb"][:], AF.Copy))
    E["dpbbf_ok"] = sA.v

    @blk.vector
    def _(v):
        v.memset(E["epsc"][:], 1e-5)
        v.wait_ge(sA.h, sA.v)
        sV.mark(v.tensor_scalar(E["scrc"][:], E["nega"][:], -1.0, None, OP.mult))
        v.wait_ge(sV.h, sV.v)
        sV.mark(v.tensor_copy(E["nega"][:], E["scrc"][:]))
    E["nega_ready"] = sV.v


def _ln(E):
    nc, blk = E["nc"], E["blk"]
    sP, sV, sA = E["sP"], E["sV"], E["sA"]
    xn, i128, x_in = E["xn"], E["i128"], E["x_in"]
    sLN = [mkstream(E, f"sLN{j}") for j in range(4)]

    with ExitStack() as c2:
        lsb = lambda n, s, dt: c2.enter_context(nc.sbuf_tensor(n, s, dt))
        xq = [lsb(f"ln_x{j}", [128, D], F32) for j in range(2)]
        xnq = [lsb(f"ln_xn{j}", [128, D], BF16) for j in range(4)]
        st = lsb("ln_stats", [128, 8], F32)
        pT = E["pT"]

        E["ln_sm"] = {}
        bank_free = {}
        xq_free = {}
        ln_mm_last = {}
        prev_sm_done = None
        for sm in range(4):
            for j in range(4):
                m = sm * 4 + j

                @blk.sync
                def _(s, m=m, j=j, prev=xq_free.get(j % 2)):
                    if prev is not None:
                        s.wait_ge(sV.h, prev)
                    sLN[j].mark(s.dma_start(out=xq[j % 2][:], in_=x_in[m * 128:(m + 1) * 128, :]), 16)
                ld = sLN[j].v

                @blk.vector
                def _(v, j=j, ld=ld):
                    v.wait_ge(sLN[j].h, ld)
                    sV.mark(v.reduce_sum(st[:, 0:1], xq[j % 2][:], AX.X))
                    v.wait_ge(sV.h, sV.v)
                    sV.mark(v.tensor_scalar(st[:, 4:5], st[:, 0:1], 1.0 / D, None, OP.mult))
                mean_ok = sV.v

                @blk.scalar
                def _(sc, j=j, ld=ld, tml=ln_mm_last.get(0)):
                    sc.wait_ge(sLN[j].h, ld)
                    if tml is not None:
                        sc.wait_ge(sP.h, tml)
                    sA.mark(sc.activation(xnq[j][:], xq[j % 2][:], AF.Square, accum_out=st[:, 1:2]))
                ssq_ok = sA.v

                @blk.vector
                def _(v, mean_ok=mean_ok, ssq_ok=ssq_ok):
                    v.wait_ge(sV.h, mean_ok)
                    v.wait_ge(sA.h, ssq_ok)
                    sV.mark(v.tensor_scalar(st[:, 5:6], st[:, 1:2], 1.0 / D, None, OP.mult))
                    sV.mark(v.tensor_tensor(st[:, 6:7], st[:, 4:5], st[:, 4:5], OP.mult))
                    v.wait_ge(sV.h, sV.v)
                    sV.mark(v.tensor_tensor(st[:, 2:3], st[:, 5:6], st[:, 6:7], OP.subtract))
                var_ok = sV.v

                @blk.scalar
                def _(sc, var_ok=var_ok):
                    sc.wait_ge(sV.h, var_ok)
                    sA.mark(sc.activation(st[:, 7:8], st[:, 2:3], AF.Ln, bias=E["epsc"][:]))
                    sc.wait_ge(sA.h, sA.v)
                    sA.mark(sc.activation(st[:, 3:4], st[:, 7:8], AF.Exp, scale=-0.5))
                rstd_ok = sA.v

                @blk.vector
                def _(v, j=j, rstd_ok=rstd_ok, var_ok=var_ok):
                    v.wait_ge(sA.h, rstd_ok)
                    v.wait_ge(sV.h, var_ok)
                    sV.mark(v.tensor_scalar(xnq[j][:], xq[j % 2][:], st[:, 4:5], st[:, 3:4],
                                            OP.subtract, OP.mult))
                xq_free[j % 2] = sV.v
            xn_ready = sV.v
            for kc in range(KC):
                pb = pT[kc % 2]
                war = bank_free.get(kc % 2)

                @blk.tensor
                def _(t, kc=kc, pb=pb, war=war, xn_ready=xn_ready):
                    t.wait_ge(sV.h, xn_ready)
                    if war is not None:
                        t.wait_ge(sV.h, war)
                    for j in range(4):
                        mi = t.matmul(pb[:, j * 128:(j + 1) * 128], xnq[j][:, kc * 128:(kc + 1) * 128],
                                      i128[:], start=(j == 0), stop=(j == 3))
                    sP.mark(mi)
                mm_ok = sP.v
                if kc == KC - 1:
                    ln_mm_last[0] = sP.v

                @blk.vector
                def _(v, sm=sm, kc=kc, pb=pb, mm_ok=mm_ok):
                    v.wait_ge(sP.h, mm_ok)
                    sV.mark(v.tensor_copy(xn[:, kc * L + sm * 512: kc * L + (sm + 1) * 512], pb[:]))
                bank_free[kc % 2] = sV.v
            prev_sm_done = sV.v
            E["ln_sm"][sm] = sV.v
    E["xn_ready"] = (sV, sV.v)


def _inproj(E):
    nc, blk = E["nc"], E["blk"]
    sP, sV, sA = E["sP"], E["sV"], E["sA"]
    xn = E["xn"]
    wconv, bconv, dtb, nega, dta = E["wconv"], E["bconv"], E["dtb"], E["nega"], E["dta"]
    bfm, cfm = E["bfm"], E["cfm"]
    zs_sp, x_sp, win_t = E["zs_sp"], E["x_sp"], E["win_t"]
    sWB = [mkstream(E, f"sWB{i}") for i in range(2)]
    sZS = [mkstream(E, f"sZS{i}") for i in range(2)]
    sXS = [mkstream(E, f"sXS{i}") for i in range(2)]

    if True:
        wblk = E["wblk"]
        pmm = E["pmm"]
        zstage = E["zstage"]
        xstage = E["xstage"]
        xpre = E["xpre"]
        cv = E["cv"]
        cva = E["cva"]
        dstage = E["dstage"]
        dt0 = E["dt0"]
        dt1 = E["dt1"]

        @blk.vector
        def _(v):
            sV.mark(v.memset(xpre[:, 0:3], 0.0))
        xpre0_ok = sV.v

        wload = {}
        mm_done = {}
        bank_free = {}
        zst_free = {}
        xst_free = {}
        cv_free = None
        conv_done = None

        for mb in range(67):
            msz = 64 if mb == 66 else 128
            buf = wblk[mb % 2]
            war = mm_done.get(mb - 2)

            @blk.gpsimd
            def _(gq, mb=mb, msz=msz, buf=buf, war=war):
                if war is not None:
                    gq.wait_ge(sP.h, war)
                src = win_t[:, mb * 128: mb * 128 + msz].rearrange("(kc p) m -> p kc m", p=128)
                sWB[mb % 2].mark(gq.dma_start(
                    out=buf[:, 0:KC * msz].rearrange("p (kc m) -> p kc m", kc=KC),
                    in_=src), 16)
            wload[mb] = sWB[mb % 2].v

            mm_oks = []
            for tb in range(4):
                pb = pmm[tb]
                pwar = bank_free.get(tb)

                @blk.tensor
                def _(t, mb=mb, tb=tb, msz=msz, buf=buf, pb=pb, pwar=pwar):
                    t.wait_ge(sWB[mb % 2].h, wload[mb])
                    if mb < 2:
                        t.wait_ge(sV.h, E["ln_sm"][tb])
                    if pwar is not None:
                        t.wait_ge(pwar[0].h, pwar[1])
                    dst = pb[0:msz, :] if msz < 128 else pb[:]
                    for kc in range(KC):
                        mi = t.matmul(dst, buf[:, kc * msz:(kc + 1) * msz],
                                      xn[:, kc * L + tb * 512: kc * L + (tb + 1) * 512],
                                      start=(kc == 0), stop=(kc == KC - 1))
                    sP.mark(mi)
                mm_oks.append(sP.v)
            mm_done[mb] = sP.v

            if mb < 32:
                zbuf = zstage[mb % 2]

                @blk.scalar
                def _(sc, mb=mb, zbuf=zbuf, mm_oks=tuple(mm_oks)):
                    w = zst_free.get(mb - 2)
                    if w is not None:
                        sc.wait_ge(w[0].h, w[1])
                    for tb in range(4):
                        sc.wait_ge(sP.h, mm_oks[tb])
                        sA.mark(sc.activation(zbuf[:, tb * 512:(tb + 1) * 512], pmm[tb][:], AF.Silu))
                        bank_free[tb] = (sA, sA.v)

                @blk.sync
                def _(s, mb=mb, zbuf=zbuf, a=sA.v):
                    s.wait_ge(sA.h, a)
                    sZS[mb % 2].mark(s.dma_start(out=zs_sp[mb * 128:(mb + 1) * 128, :],
                                                 in_=zbuf[:]), 16)
                zst_free[mb] = (sZS[mb % 2], sZS[mb % 2].v)
            elif mb < 66:
                cb = mb - 32

                @blk.vector
                def _(v, cb=cb, mm_oks=tuple(mm_oks), cd=conv_done, cvf=cv_free):
                    if cd is not None:
                        v.wait_ge(sV.h, cd)
                    for tb in range(4):
                        v.wait_ge(sP.h, mm_oks[tb])
                        sV.mark(v.tensor_copy(xpre[:, 3 + tb * 512: 3 + (tb + 1) * 512], pmm[tb][:]))
                        bank_free[tb] = (sV, sV.v)
                    if cb == 0:
                        v.wait_ge(sV.h, xpre0_ok)
                    if cvf is not None:
                        v.wait_ge(sA.h, cvf)
                    v.wait_ge(sV.h, sV.v)
                    c0 = cb * 4
                    sV.mark(v.tensor_scalar(cva[:], xpre[:, 0:L], wconv[:, c0:c0 + 1], None, OP.mult))
                    v.wait_ge(sV.h, sV.v)
                    sV.mark(v.scalar_tensor_tensor(cv[:], xpre[:, 1:1 + L], wconv[:, c0 + 1:c0 + 2], cva[:], OP.mult, OP.add))
                    v.wait_ge(sV.h, sV.v)
                    sV.mark(v.scalar_tensor_tensor(cva[:], xpre[:, 2:2 + L], wconv[:, c0 + 2:c0 + 3], cv[:], OP.mult, OP.add))
                    v.wait_ge(sV.h, sV.v)
                    sV.mark(v.scalar_tensor_tensor(cv[:], xpre[:, 3:3 + L], wconv[:, c0 + 3:c0 + 4], cva[:], OP.mult, OP.add))
                conv_done = sV.v
                cv_ok = sV.v
                if cb < 32:
                    xbuf = xstage[cb % 2]

                    @blk.scalar
                    def _(sc, cb=cb, xbuf=xbuf, cv_ok=cv_ok):
                        sc.wait_ge(sV.h, cv_ok)
                        w = xst_free.get(cb - 2)
                        if w is not None:
                            sc.wait_ge(w[0].h, w[1])
                        sA.mark(sc.activation(xbuf[:], cv[:], AF.Silu, bias=bconv[:, cb:cb + 1]))
                    cv_free = sA.v

                    @blk.sync
                    def _(s, cb=cb, xbuf=xbuf, a=sA.v):
                        s.wait_ge(sA.h, a)
                        sXS[cb % 2].mark(s.dma_start(out=x_sp[cb * 128:(cb + 1) * 128, :],
                                                     in_=xbuf[:]), 16)
                    xst_free[cb] = (sXS[cb % 2], sXS[cb % 2].v)
                else:
                    dstb = bfm if cb == 32 else cfm

                    @blk.scalar
                    def _(sc, cb=cb, dstb=dstb, cv_ok=cv_ok):
                        sc.wait_ge(sV.h, cv_ok)
                        sA.mark(sc.activation(dstb[:], cv[:], AF.Silu, bias=bconv[:, cb:cb + 1]))
                    cv_free = sA.v
            else:
                @blk.vector
                def _(v, mm_oks=tuple(mm_oks)):
                    for tb in range(4):
                        v.wait_ge(sP.h, mm_oks[tb])
                        sl = slice(tb * 512, (tb + 1) * 512)
                        sV.mark(v.tensor_scalar(dstage[:, sl], pmm[tb][0:NH, :], dtb[:], None, OP.add))
                        bank_free[tb] = (sV, sV.v)

        raw_ok = sV.v

        @blk.scalar
        def _(sc, raw_ok=raw_ok):
            sc.wait_ge(sV.h, raw_ok)
            sA.mark(sc.activation(dt0[:], dstage[:], AF.Abs))
            sc.wait_ge(sA.h, sA.v)
            sA.mark(sc.activation(dt1[:], dt0[:], AF.Exp, scale=-1.0))
            sc.wait_ge(sA.h, sA.v)
            sA.mark(sc.activation(dt0[:], dt1[:], AF.Ln, bias=1.0))
        sp_ok = sA.v

        @blk.vector
        def _(v, sp_ok=sp_ok):
            v.wait_ge(sA.h, sp_ok)
            v.wait_ge(sV.h, E["nega_ready"])
            sV.mark(v.tensor_scalar(dt1[:], dstage[:], 0.0, None, OP.max))
            v.wait_ge(sV.h, sV.v)
            sV.mark(v.tensor_tensor(dstage[:], dt1[:], dt0[:], OP.add))
            v.wait_ge(sV.h, sV.v)
            sV.mark(v.tensor_scalar(dta[:], dstage[:], nega[:], None, OP.mult))
    E["dta_ready_V"] = sV.v
    E["bc_ready_A"] = sA.v
    E["inproj_spill_finals"] = [(s, s.v) for s in sZS + sXS]


def _gate(E):
    nc, blk = E["nc"], E["blk"]
    sP, sV, sA = E["sP"], E["sV"], E["sA"]
    xn, cmb, gate_t = E["xn"], E["cmb"], E["gate_t"]
    sGW = mkstream(E, "sGW")
    gw, sc_all, pg = E["gwb"], E["gsc"], E["pg"]

    with ExitStack() as c2:
        lsb = lambda n, s, dt: c2.enter_context(nc.sbuf_tensor(n, s, dt))
        m8 = lsb("gt_m8", [128, 8], F32)
        z8 = lsb("gt_z8", [128, 8], F32)
        den = lsb("gt_den", [128, 1], F32)
        rcp = lsb("gt_rcp", [128, 1], F32)
        pick = lsb("gt_pick", [128, 8], F32)
        msk = lsb("gt_msk", [128, 8], U32)

        @blk.gpsimd
        def _(g):
            sGW.mark(g.dma_start(out=gw[:].rearrange("p (kc m) -> p kc m", kc=KC),
                                 in_=gate_t.rearrange("(kc p) m -> p kc m", p=128)), 16)
        gw_ok = sGW.v

        @blk.vector
        def _(v):
            v.memset(z8[:], 0.0)

        @blk.tensor
        def _(t):
            t.wait_ge(sGW.h, gw_ok)
            xr = E["xn_ready"]
            t.wait_ge(xr[0].h, xr[1])
            for m in range(16):
                for kc in range(KC):
                    mi = t.matmul(pg[:, m * 8:(m + 1) * 8],
                                  xn[:, kc * L + m * 128: kc * L + (m + 1) * 128],
                                  gw[:, kc * 8:(kc + 1) * 8],
                                  start=(kc == 0), stop=(kc == KC - 1))
            sP.mark(mi)
        mm_ok = sP.v

        @blk.scalar
        def _(sc, mm_ok=mm_ok):
            sc.wait_ge(sP.h, mm_ok)
            sA.mark(sc.activation(sc_all[:], pg[:], AF.Sigmoid))
        E["gate_sig_A"] = sA.v
        sig_ok = sA.v

        for m in range(16):
            @blk.vector
            def _(v, m=m, sig_ok=sig_ok):
                if m == 0:
                    v.wait_ge(sA.h, sig_ok)
                sV.mark(v.max(out=m8[:], in_=sc_all[:, m * 8:(m + 1) * 8]))
                v.wait_ge(sV.h, sV.v)
                sV.mark(v.tensor_scalar(den[:], m8[:, 0:1], m8[:, 1:2], 1e-20, OP.add, OP.add))
                v.wait_ge(sV.h, sV.v)
                sV.mark(v.reciprocal(rcp[:], den[:]))
                v.wait_ge(sV.h, sV.v)
                sV.mark(v.tensor_scalar(pick[:], sc_all[:, m * 8:(m + 1) * 8], rcp[:], 2.5, OP.mult, OP.mult))
                sV.mark(v.tensor_scalar(msk[:], sc_all[:, m * 8:(m + 1) * 8], m8[:, 1:2], None, OP.is_ge))
                v.wait_ge(sV.h, sV.v)
                sV.mark(v.select(cmb[:, m * 8:(m + 1) * 8], msk[:], pick[:], z8[:], add_drain=True))
    E["cmb_ready_V"] = sV.v


def _f1(E):
    nc, blk = E["nc"], E["blk"]
    sP, sV, sA = E["sP"], E["sV"], E["sA"]
    xn = E["xn"]
    h_sp, w1, w3, sw1, sw3 = E["h_sp"], E["w1"], E["w3"], E["sw1"], E["sw3"]
    sFW = [mkstream(E, f"sFW{i}") for i in range(2)]
    sHS = [mkstream(E, f"sHS{i}") for i in range(2)]

    with ExitStack() as c2:
        lsb = lambda n, s, dt: c2.enter_context(nc.sbuf_tensor(n, s, dt))
        wa = [lsb(f"f1_wa{i}", [128, KC * 128], BF16) for i in range(2)]
        wb = [lsb(f"f1_wb{i}", [128, KC * 128], BF16) for i in range(2)]
        sil = [lsb(f"f1_sil{i}", [128, 512], F32) for i in range(2)]
        hstage = [lsb(f"f1_h{i}", [128, L], BF16) for i in range(2)]
        p1 = [c2.enter_context(nc.psum_tensor(f"f1_p1{i}", [128, 512], F32)) for i in range(2)]
        p3 = [c2.enter_context(nc.psum_tensor(f"f1_p3{i}", [128, 512], F32)) for i in range(2)]

        wload = {}
        mm_done = {}
        bank_free = {}
        hst_free = {}
        sil_free = {}

        for idx in range(NE * 11):
            e, mb = divmod(idx, 11)
            bi = idx % 2
            war = mm_done.get(idx - 2)

            @blk.gpsimd
            def _(gq, idx=idx, e=e, mb=mb, bi=bi, war=war):
                if idx < 2:
                    gq.wait_ge(sV.h, E["cmb_ready_V"])
                    gq.wait_ge(sV.h, E["dta_ready_V"])
                    gq.wait_ge(sA.h, E["bc_ready_A"])
                    for s, v in E["inproj_spill_finals"]:
                        gq.wait_ge(s.h, v)
                if war is not None:
                    gq.wait_ge(sP.h, war)
                if e < 8:
                    s1 = w1[e, :, mb * 128:(mb + 1) * 128]
                    s3 = w3[e, :, mb * 128:(mb + 1) * 128]
                else:
                    s1 = sw1[:, mb * 128:(mb + 1) * 128]
                    s3 = sw3[:, mb * 128:(mb + 1) * 128]
                sFW[bi].mark(gq.dma_start(out=wa[bi][:].rearrange("p (kc m) -> p kc m", kc=KC),
                                          in_=s1.rearrange("(kc p) m -> p kc m", p=128)), 16)
                sFW[bi].mark(gq.dma_start(out=wb[bi][:].rearrange("p (kc m) -> p kc m", kc=KC),
                                          in_=s3.rearrange("(kc p) m -> p kc m", p=128)), 16)
            wload[idx] = sFW[bi].v

            hbuf = hstage[idx % 2]
            for tb in range(4):
                pi = tb % 2
                pwar = bank_free.get(pi)

                @blk.tensor
                def _(t, idx=idx, tb=tb, bi=bi, pi=pi, pwar=pwar):
                    t.wait_ge(sFW[bi].h, wload[idx])
                    if idx == 0 and tb == 0:
                        t.wait_ge(sA.h, E["gate_sig_A"])
                        t.wait_ge(sV.h, E["dta_ready_V"])
                    if pwar is not None:
                        t.wait_ge(sV.h, pwar)
                    for kc in range(KC):
                        t.matmul(p1[pi][:], wa[bi][:, kc * 128:(kc + 1) * 128],
                                 xn[:, kc * L + tb * 512: kc * L + (tb + 1) * 512],
                                 start=(kc == 0), stop=(kc == KC - 1))
                    for kc in range(KC):
                        mi = t.matmul(p3[pi][:], wb[bi][:, kc * 128:(kc + 1) * 128],
                                      xn[:, kc * L + tb * 512: kc * L + (tb + 1) * 512],
                                      start=(kc == 0), stop=(kc == KC - 1))
                    sP.mark(mi)
                mm_ok = sP.v
                if tb == 3:
                    mm_done[idx] = sP.v

                @blk.scalar
                def _(sc, pi=pi, mm_ok=mm_ok, sf=sil_free.get(tb % 2)):
                    sc.wait_ge(sP.h, mm_ok)
                    if sf is not None:
                        sc.wait_ge(sV.h, sf)
                    sA.mark(sc.activation(sil[pi][:], p1[pi][:], AF.Silu))
                sil_ok = sA.v

                @blk.vector
                def _(v, idx=idx, tb=tb, pi=pi, hbuf=hbuf, sil_ok=sil_ok):
                    v.wait_ge(sA.h, sil_ok)
                    if tb == 0:
                        w = hst_free.get(idx - 2)
                        if w is not None:
                            v.wait_ge(w[0].h, w[1])
                    sV.mark(v.tensor_tensor(hbuf[:, tb * 512:(tb + 1) * 512], sil[pi][:],
                                            p3[pi][:], OP.mult))
                bank_free[pi] = sV.v
                sil_free[tb % 2] = sV.v

            @blk.sync
            def _(s, idx=idx, e=e, mb=mb, hbuf=hbuf, vv=sV.v):
                s.wait_ge(sV.h, vv)
                sHS[idx % 2].mark(s.dma_start(
                    out=h_sp[e * FH + mb * 128: e * FH + (mb + 1) * 128, :],
                    in_=hbuf[:]), 16)
            hst_free[idx] = (sHS[idx % 2], sHS[idx % 2].v)
    E["f1_spill_finals"] = [(s, s.v) for s in sHS]


def _f2(E):
    nc, blk = E["nc"], E["blk"]
    sP, sV, sA = E["sP"], E["sV"], E["sA"]
    cmb, g_sp = E["cmb"], E["g_sp"]
    h_sp, w2, sw2 = E["h_sp"], E["w2"], E["sw2"]
    sW2 = [mkstream(E, f"sW2{i}") for i in range(3)]
    sHB = [mkstream(E, f"sHB{i}") for i in range(3)]
    sGS = [mkstream(E, f"sGS{i}") for i in range(2)]

    with ExitStack() as c2:
        lsb = lambda n, s, dt: c2.enter_context(nc.sbuf_tensor(n, s, dt))
        acc = lsb("f2_acc", [128, 8 * D], F32)
        w2h = [lsb(f"f2_w{i}", [128, 11 * 512], BF16) for i in range(3)]
        hb = [lsb(f"f2_h{i}", [128, 11 * 256], BF16) for i in range(3)]
        gst = [lsb(f"f2_g{i}", [128, D], BF16) for i in range(2)]
        pf = [c2.enter_context(nc.psum_tensor(f"f2_p{i}", [128, 512], F32)) for i in range(4)]

        wload = {}
        hload = {}
        w_used = {}
        h_used = {}
        bank_free = {}
        acc_mark = {}
        silu_done = None

        widx = 0
        hidx = 0
        for th in range(2):
            for e in range(NE):
                for nbq in range(4):
                    wbuf = w2h[widx % 3]
                    war = w_used.get(widx - 3)

                    @blk.gpsimd
                    def _(gq, widx=widx, e=e, nbq=nbq, wbuf=wbuf, war=war):
                        if widx < 2:
                            for s, v in E["f1_spill_finals"]:
                                gq.wait_ge(s.h, v)
                            if "dbg_xn_done" in E:
                                gq.wait_ge(E["dbg_xn_done"][0].h, E["dbg_xn_done"][1])
                        if war is not None:
                            gq.wait_ge(sP.h, war)
                        src = w2[e, :, nbq * 512:(nbq + 1) * 512] if e < 8 else \
                            sw2[:, nbq * 512:(nbq + 1) * 512]
                        sW2[widx % 3].mark(gq.dma_start(
                            out=wbuf[:].rearrange("p (kc n) -> p kc n", kc=11),
                            in_=src.rearrange("(kc p) n -> p kc n", p=128)), 16)
                    wload[widx] = sW2[widx % 3].v

                    for mb in range(4):
                        hbuf = hb[hidx % 3]
                        hwar = h_used.get(hidx - 3)

                        @blk.sync
                        def _(s, hidx=hidx, th=th, e=e, mb=mb, hbuf=hbuf, hwar=hwar):
                            if hidx < 2:
                                for st, v in E["f1_spill_finals"]:
                                    s.wait_ge(st.h, v)
                            if hwar is not None:
                                s.wait_ge(sP.h, hwar)
                            c0 = th * 1024 + mb * 256
                            sHB[hidx % 3].mark(s.dma_start(
                                out=hbuf[:].rearrange("p (kc t) -> p kc t", kc=11),
                                in_=h_sp[e * FH:(e + 1) * FH, c0:c0 + 256].rearrange("(kc p) t -> p kc t", p=128)), 16)
                        hload[hidx] = sHB[hidx % 3].v

                        for ms in range(2):
                            bk = (mb % 2) * 2 + ms
                            pwar = bank_free.get(bk)

                            @blk.tensor
                            def _(t, widx=widx, hidx=hidx, ms=ms, bk=bk, pwar=pwar,
                                  wbuf=wbuf, hbuf=hbuf):
                                t.wait_ge(sW2[widx % 3].h, wload[widx])
                                t.wait_ge(sHB[hidx % 3].h, hload[hidx])
                                if pwar is not None:
                                    t.wait_ge(sV.h, pwar)
                                for kc in range(11):
                                    mi = t.matmul(pf[bk][:], hbuf[:, kc * 256 + ms * 128: kc * 256 + (ms + 1) * 128],
                                                  wbuf[:, kc * 512:(kc + 1) * 512],
                                                  start=(kc == 0), stop=(kc == 10))
                                sP.mark(mi)
                            mm_ok = sP.v
                            w_used[widx] = sP.v
                            h_used[hidx] = sP.v

                            j = mb * 2 + ms
                            asl = slice(j * D + nbq * 512, j * D + (nbq + 1) * 512)
                            mtile = th * 8 + j
                            first = e == 0

                            @blk.vector
                            def _(v, e=e, bk=bk, asl=asl, mtile=mtile, nbq=nbq, first=first,
                                  mm_ok=mm_ok, sd=silu_done, am=acc_mark.get((mtile, nbq))):
                                v.wait_ge(sP.h, mm_ok)
                                if first:
                                    if sd is not None:
                                        v.wait_ge(sA.h, sd)
                                    v.wait_ge(sV.h, E["cmb_ready_V"])
                                    if mtile == 0 and "dbg_xn_done" in E:
                                        v.wait_ge(E["dbg_xn_done"][0].h, E["dbg_xn_done"][1])
                                else:
                                    v.wait_ge(sV.h, am)
                                if e < 8:
                                    col = cmb[:, mtile * 8 + e: mtile * 8 + e + 1]
                                    if first:
                                        sV.mark(v.tensor_scalar(acc[:, asl], pf[bk][:], col, None, OP.mult))
                                    else:
                                        sV.mark(v.scalar_tensor_tensor(acc[:, asl], pf[bk][:], col,
                                                                       acc[:, asl], OP.mult, OP.add))
                                else:
                                    sV.mark(v.scalar_tensor_tensor(acc[:, asl], pf[bk][:], 1.0,
                                                                   acc[:, asl], OP.mult, OP.add))
                            bank_free[bk] = sV.v
                            acc_mark[(mtile, nbq)] = sV.v
                        hidx += 1
                    widx += 1
            acc_ok = sV.v
            gsnap = [sGS[0].v, sGS[1].v]

            @blk.scalar
            def _(sc, th=th, acc_ok=acc_ok, gsnap=gsnap):
                sc.wait_ge(sV.h, acc_ok)
                for j in range(8):
                    if j >= 2:
                        sc.wait_ge(sGS[j % 2].h, gsnap[j % 2] + 16 * ((j - 2) // 2 + 1))
                    elif th:
                        sc.wait_ge(sGS[j % 2].h, gsnap[j % 2])
                    sA.mark(sc.activation(gst[j % 2][:], acc[:, j * D:(j + 1) * D], AF.Silu))

            @blk.sync
            def _(s, th=th, base=sA.v - 8):
                for j in range(8):
                    s.wait_ge(sA.h, base + j + 1)
                    sGS[j % 2].mark(s.dma_start(
                        out=g_sp[:, (th * 8 + j) * D:(th * 8 + j + 1) * D], in_=gst[j % 2][:]), 16)
            silu_done = sA.v
    E["g_ready_A"] = sA.v
    E["g_sp_finals"] = [(sGS[0], sGS[0].v), (sGS[1], sGS[1].v)]


def _ssd(E):
    nc, blk = E["nc"], E["blk"]
    sP, sV, sA = E["sP"], E["sV"], E["sA"]
    bfm, cfm, dta, dpb, nwb, i128, i64f = (E[k] for k in ("bfm", "cfm", "dta", "dpb", "nwb", "i128", "i64f"))
    ones64, mask128 = E["ones64"], E["mask128"]
    zs_sp, x_sp, yn_sp, cs_sp = E["zs_sp"], E["x_sp"], E["yn_sp"], E["cs_sp"]
    sXT = mkstream(E, "sXT")
    sZT = mkstream(E, "sZT")
    sCW = mkstream(E, "sCW")
    sBC = [mkstream(E, f"sBC{i}") for i in range(2)]
    sYN = mkstream(E, "sYN")
    sWH0 = [mkstream(E, f"sWH0{i}") for i in range(2)]
    sGT = [mkstream(E, f"sGT{i}") for i in range(2)]
    sXI = [mkstream(E, f"sXI{i}") for i in range(2)]
    sOB2 = [mkstream(E, f"sOB2{i}") for i in range(2)]
    whh, g_sp, x_in, out, wout_t = E["whh"], E["g_sp"], E["x_in"], E["out"], E["wout_t"]
    sG = Sem(E["nc"], E["ctx"], "sG")

    with ExitStack() as c2:
        lsb = lambda n, s, dt: c2.enter_context(nc.sbuf_tensor(n, s, dt))
        lps = lambda n: c2.enter_context(nc.psum_tensor(n, [128, 512], F32))
        gt2 = [lsb(f"ss_gt{i}", [128, 512], BF16) for i in range(2)]
        xt2 = [lsb(f"ss_xt{i}", [128, 512], F32) for i in range(2)]
        xtm = lsb("ss_x", [128, DI], BF16)
        zst = lsb("ss_z", [128, DI], BF16)
        Sbf = lsb("ss_Sbf", [128, DI], BF16)
        xdp = lsb("ss_xdp", [128, DI], BF16)
        bca = [lsb(f"ss_bc{i}", [128, 16 * CH], F32) for i in range(2)]
        ebuf = [lsb(f"ss_eb{i}", [128, 16 * CH], BF16) for i in range(4)]
        pt = [lsb(f"ss_pt{i}", [128, 16 * CH], BF16) for i in range(4)]
        xts = [lsb(f"ss_xts{i}", [128, 16 * 64], BF16) for i in range(4)]
        csf = lsb("ss_csf", [NH, CH], F32)
        zcs = lsb("ss_zcs", [NH, CH], F32)
        cst = lsb("ss_cst", [128, NH], F32)
        diag64 = lsb("ss_dg", [NH, NH], F32)
        expc = lsb("ss_expc", [128, NH], F32)
        eto = lsb("ss_eto", [128, NH], F32)
        btm = lsb("ss_btm", [128, CH], BF16)
        sctm = lsb("ss_sct", [128, CH], BF16)
        ybuf = lsb("ss_y", [128, DI], BF16)
        ytmp = lsb("ss_yt", [128, 512], F32)
        stmp = lsb("ss_st", [128, 512], F32)
        ybF = ybuf[:].bitcast(F32)
        ob2 = [ybF[:, 0:512], ybF[:, 512:1024]]
        t1w = ybF[:, 1024:1536]
        ssq = lsb("ss_ssq", [128, 4], F32)
        rstd = lsb("ss_rstd", [128, 1], F32)
        ynb = lsb("ss_ynb", [128, DI], BF16)
        psm = [lps(f"ss_sm{i}") for i in range(2)]
        pY = [lps(f"ss_pY{i}") for i in range(2)]
        pYo = [lps(f"ss_pYo{i}") for i in range(2)]
        pSt = [lps(f"ss_pSt{i}") for i in range(2)]

        @blk.vector
        def _(v):
            v.wait_ge(sA.h, E["g_ready_A"])
            for st, vv in E.get("g_sp_finals", []):
                v.wait_ge(st.h, vv)
            ics, icv = E["init_loads"]
            v.wait_ge(ics.h, icv)
            v.memset(Sbf[:], 0.0)
            sV.mark(v.memset(zcs[:], 0.0))
        init_ok = sV.v

        @blk.gpsimd
        def _(gq):
            gq.wait_ge(sA.h, E["g_ready_A"])
            for st, vv in E.get("g_sp_finals", []):
                gq.wait_ge(st.h, vv)
            for pc in range(2):
                sWH0[pc].mark(gq.dma_start(
                    out=whh[:, pc * 16384:(pc + 1) * 16384].rearrange("p (kc n) -> p kc n", kc=32),
                    in_=wout_t[:, pc * 512:(pc + 1) * 512].rearrange("(kc p) n -> p kc n", p=128)), 16)

        ebank = {}
        x_free = None
        z_free = None
        yn_free = {}
        bc_free = {}
        pt_free = {}
        bank_mm_g = {}
        sbf_war = None
        btl_prev = {}
        scale_ok = None
        cs_wr = None
        csf_rd = None
        cst_evac = None
        btm_evac = None
        sct_evac = None
        ynf_evac = None
        bcasts = {}

        st_pro = {}

        def xdp_pool(c):
            @blk.gpsimd
            def _(gq, c=c, x_ld=st_pro["x_ld"], pml=st_pro.get("piece_mm"),
                  ynw=yn_free.get("b")):
                gq.wait_ge(sXT.h, x_ld)
                if pml is not None:
                    gq.wait_ge(sP.h, pml)
                if ynw is not None:
                    gq.wait_ge(sYN.h, ynw)
                if c == 0:
                    gq.wait_ge(sA.h, E["dpbbf_ok"])
                    gq.wait_ge(sV.h, init_ok)
                sG.mark(gq.tensor_tensor(xdp[:].rearrange("p (h x) -> p h x", h=NH),
                                         xtm[:].rearrange("p (h x) -> p h x", h=NH),
                                         E["dpbbf"][:].unsqueeze(2).broadcast_to([128, NH, 64]),
                                         OP.mult))
            st_pro[("xdp_ok", c)] = sG.v

        def sched_cs(c):
            nonlocal cs_wr, csf_rd

            @blk.vector
            def _(v, c=c, cw=cs_wr, cr=csf_rd, em=st_pro.get("eto_mm"),
                  dgp=st_pro.get(("dg_ok", c - 1))):
                if c == 0:
                    v.wait_ge(sV.h, E["dta_ready_V"])
                    v.wait_ge(sV.h, init_ok)
                else:
                    v.wait_ge(sCW.h, cw)
                    v.wait_ge(sP.h, cr)
                    if dgp is not None:
                        v.wait_ge(sV.h, dgp)
                sV.mark(v.tensor_tensor_scan(csf[:], dta[:, c * CH:(c + 1) * CH], zcs[:],
                                             0.0, OP.add, OP.add))
                v.wait_ge(sV.h, sV.v)
                if em is not None:
                    v.wait_ge(sP.h, em)
                sV.mark(v.tensor_scalar(diag64[:], i64f[:], csf[:, CH - 1:CH], None, OP.mult))
            st_pro[("cs_ok", c)] = sV.v - 1
            st_pro[("dg_ok", c)] = sV.v

            @blk.gpsimd
            def _(gq, c=c, cs_ok=st_pro[("cs_ok", c)], bcw=bcasts.get(c - 2)):
                gq.wait_ge(sV.h, cs_ok)
                if bcw is not None:
                    for s_, v_ in bcw:
                        gq.wait_ge(s_.h, v_)
                sCW.mark(gq.dma_start(out=cs_sp[(c % 2) * NH:(c % 2 + 1) * NH, :], in_=csf[:]), 16)
            cs_wr = sCW.v
            st_pro[("cs_wr", c)] = sCW.v

        def bcast01(c):
            for g in range(2):
                gp = g % 2

                @blk.gpsimd
                def _(gq, c=c, g=g, gp=gp, bcf=bc_free.get(gp), cw=st_pro[("cs_wr", c)]):
                    if c == 0:
                        gq.wait_ge(sA.h, E["g_ready_A"])
                    if bcf is not None:
                        gq.wait_ge(sA.h, bcf)
                    gq.wait_ge(sCW.h, cw)
                    sBC[gp].mark(gq.dma_start(
                        out=bca[gp][:].rearrange("p (h l) -> p h l", h=16),
                        in_=cs_sp[(c % 2) * NH + g * 16:(c % 2) * NH + (g + 1) * 16, :]
                        .unsqueeze(0).broadcast_to([128, 16, CH])), 16)
                st_pro[("bc_ok", g)] = sBC[gp].v

        def prologue(c):
            t0 = c * CH
            nonlocal cst_evac, btm_evac, sct_evac, csf_rd
            x_free = st_pro.get("x_free")
            z_free = st_pro.get("z_free")

            @blk.sync
            def _(s, c=c, t0=t0, xf=x_free, zf=z_free, xtl=st_pro.get("xts_last")):
                if xf is not None:
                    s.wait_ge(sP.h, xf)
                if xtl is not None:
                    s.wait_ge(sG.h, xtl)
                if zf is not None:
                    s.wait_ge(sG.h, zf)
                if c == 0:
                    for st, v in E["inproj_spill_finals"]:
                        s.wait_ge(st.h, v)
                    s.wait_ge(sA.h, E["g_ready_A"])
                    for st, v in E.get("g_sp_finals", []):
                        s.wait_ge(st.h, v)
                sXT.mark(s.dma_start_transpose(xtm[:], x_sp[:, t0:t0 + CH]), 16)
                sZT.mark(s.dma_start_transpose(zst[:], zs_sp[:, t0:t0 + CH]), 16)
            st_pro["x_ld"] = sXT.v
            st_pro["z_ld"] = sZT.v

            if c == 0:
                sched_cs(0)
                bcast01(0)
                xdp_pool(0)
            cs_ok = st_pro[("cs_ok", c)]
            st_pro["dg_ok"] = st_pro[("dg_ok", c)]

            @blk.tensor
            def _(t, c=c, cs_ok=cs_ok, so=scale_ok):
                t.wait_ge(sV.h, cs_ok)
                if c == 0:
                    t.wait_ge(sA.h, E["g_ready_A"])
                else:
                    t.wait_ge(sA.h, so)
                sP.mark(t.matmul(psm[0][0:128, 0:NH], csf[:], i64f[:], start=True, stop=True))
            csf_rd = sP.v
            cst_mm = sP.v

            @blk.scalar
            def _(sc, c=c, cst_mm=cst_mm, el=st_pro.get("evac_last"),
                  bl=st_pro.get("bank_last")):
                sc.wait_ge(sP.h, cst_mm)
                if el is not None:
                    sc.wait_ge(sV.h, el[0])
                sA.mark(sc.activation(cst[:], psm[0][:, 0:NH], AF.Copy))
                sc.wait_ge(sA.h, sA.v)
                sA.mark(sc.activation(expc[:], cst[:], AF.Exp))
            cst_evac = sA.v - 1
            st_pro["cst_evac"] = cst_evac
            expc_ok = sA.v

            @blk.tensor
            def _(t, c=c, t0=t0, ce=cst_evac):
                t.wait_ge(sA.h, ce)
                if c == 0:
                    t.wait_ge(sA.h, E["bc_ready_A"])
                sP.mark(t.matmul(psm[0][:, 256:384], bfm[:, t0:t0 + CH], i128[:], start=True, stop=True))
            btm_mm = sP.v

            @blk.scalar
            def _(sc, btm_mm=btm_mm, bl=st_pro.get("bank_last")):
                sc.wait_ge(sP.h, btm_mm)
                if bl is not None:
                    sc.wait_ge(sP.h, bl)
                sA.mark(sc.activation(btm[:], psm[0][:, 256:384], AF.Copy))
            btm_evac = sA.v
            st_pro["btm_evac"] = btm_evac

            @blk.tensor
            def _(t, t0=t0, be=btm_evac):
                t.wait_ge(sA.h, be)
                sP.mark(t.matmul(psm[0][:, 128:256], bfm[:, t0:t0 + CH], cfm[:, t0:t0 + CH],
                                 start=True, stop=True))
            sct_mm = sP.v

            @blk.vector
            def _(v, sct_mm=sct_mm):
                v.wait_ge(sP.h, sct_mm)
                sV.mark(v.tensor_tensor(sctm[:], psm[0][:, 128:256], mask128[:], OP.mult))
            sct_evac = sV.v
            st_pro["sct_evac"] = sV.v

            @blk.tensor
            def _(t, dg_ok=st_pro["dg_ok"], se=sct_evac):
                t.wait_ge(sV.h, dg_ok)
                t.wait_ge(sV.h, se)
                sP.mark(t.matmul(psm[0][:, 64:128], ones64[:], diag64[:], start=True, stop=True))
            st_pro["eto_mm"] = sP.v

            @blk.scalar
            def _(sc, eto_mm=st_pro["eto_mm"], el=st_pro.get("evac_last")):
                if el is not None:
                    sc.wait_ge(sV.h, el[0])
                    sc.wait_ge(sG.h, el[1])
                sc.wait_ge(sP.h, eto_mm)
                sA.mark(sc.activation(eto[:], psm[0][:, 64:128], AF.Exp))
            st_pro["scale_ok"] = sA.v
            st_pro["expc_ok"] = expc_ok

        def main(c):
            nonlocal sbf_war, btl_prev, scale_ok, x_free_out
            scale_ok = st_pro["scale_ok"]
            t0 = c * CH
            x_ld = st_pro["x_ld"]
            z_ld = st_pro["z_ld"]

            if c + 1 < NCH:
                sched_cs(c + 1)
            xdp_ok = st_pro[("xdp_ok", c)]
            cst_ev = st_pro["cst_evac"]
            btm_ev = st_pro["btm_evac"]
            sct_ev = st_pro["sct_evac"]
            eto_mm = st_pro["eto_mm"]
            pt_oks = {}
            btl_oks = {}
            d_oks = {}
            e_oks = {}

            for g in range(4):
                gp = g % 2
                if g >= 2:
                    @blk.gpsimd
                    def _(gq, c=c, g=g, gp=gp, bcf=e_oks[g - 2], cw=st_pro[("cs_wr", c)]):
                        gq.wait_ge(sA.h, bcf)
                        gq.wait_ge(sCW.h, cw)
                        sBC[gp].mark(gq.dma_start(
                            out=bca[gp][:].rearrange("p (h l) -> p h l", h=16),
                            in_=cs_sp[(c % 2) * NH + g * 16:(c % 2) * NH + (g + 1) * 16, :]
                            .unsqueeze(0).broadcast_to([128, 16, CH])), 16)
                    st_pro[("bc_ok", g)] = sBC[gp].v
                    bcasts[c] = [(sBC[0], sBC[0].v), (sBC[1], sBC[1].v)]

                @blk.vector
                def _(v, g=g, gp=gp, bc_ok=st_pro[("bc_ok", g)], cst_ev=cst_ev,
                      ef=e_oks.get(g - 2),
                      ew=btl_prev.get(g) if btl_prev else None):
                    v.wait_ge(sBC[gp].h, bc_ok)
                    v.wait_ge(sA.h, cst_ev)
                    if ef is not None:
                        v.wait_ge(sA.h, ef)
                    if ew is not None:
                        v.wait_ge(sV.h, ew[0])
                        v.wait_ge(sG.h, ew[1])
                    sV.mark(v.tensor_tensor(
                        ebuf[g][:].rearrange("p (h l) -> p h l", h=16),
                        bca[gp][:].rearrange("p (h l) -> p h l", h=16),
                        cst[:, g * 16:(g + 1) * 16].unsqueeze(2).broadcast_to([128, 16, CH]),
                        OP.subtract))
                    v.wait_ge(sV.h, sV.v)
                    sV.mark(v.tensor_scalar(bca[gp][:], ebuf[g][:], 0.0, None, OP.min))
                d_oks[g] = sV.v

                @blk.scalar
                def _(sc, c=c, g=g, gp=gp, d_ok=d_oks[g]):
                    sc.wait_ge(sV.h, d_ok)
                    sA.mark(sc.activation(ebuf[g][:], bca[gp][:], AF.Exp))
                e_oks[g] = sA.v
                bc_free[gp] = sA.v

            xts_oks = {}
            for g in range(4):
                gp = g % 2

                @blk.vector
                def _(v, g=g, gp=gp, e_ok=e_oks[g], sct_ev=sct_ev, ptf=pt_free.get(g)):
                    v.wait_ge(sA.h, e_ok)
                    v.wait_ge(sV.h, sct_ev)
                    if ptf is not None:
                        v.wait_ge(sP.h, ptf)
                    sV.mark(v.tensor_tensor(pt[g][:].rearrange("p (h l) -> p h l", h=16),
                                            ebuf[g][:].rearrange("p (h l) -> p h l", h=16),
                                            sctm[:].unsqueeze(1).broadcast_to([128, 16, CH]),
                                            OP.mult))
                pt_oks[g] = sV.v

                @blk.gpsimd
                def _(gq, g=g, e_ok=e_oks[g], ptf=pt_free.get(g)):
                    gq.wait_ge(sA.h, e_ok)
                    if ptf is not None:
                        gq.wait_ge(sP.h, ptf)
                    sG.mark(gq.tensor_tensor(
                        xts[g][:].rearrange("p (h x) -> p h x", h=16),
                        xtm[:, g * 1024:(g + 1) * 1024].rearrange("p (h x) -> p h x", h=16),
                        ebuf[g][:, CH - 1::CH].unsqueeze(2).broadcast_to([128, 16, 64]),
                        OP.mult))
                xts_oks[g] = sG.v
                btl_oks[g] = (sV.v, sG.v)
            st_pro["xts_last"] = sG.v

            if c + 1 < NCH:
                bcast01(c + 1)

            if c > 0:
                pieces_evac(c - 1)

            @blk.gpsimd
            def _(gq, c=c, pw0=st_pro.get(("pev", 0)), pw1=st_pro.get(("pev", 1))):
                if c == 0:
                    for st, vv in E.get("g_sp_finals", []):
                        gq.wait_ge(st.h, vv)
                for pc, pw in ((0, pw0), (1, pw1)):
                    if pw is not None:
                        gq.wait_ge(sV.h, pw)
                    sGT[pc].mark(gq.dma_start(
                        out=gt2[pc][:], in_=g_sp[:, c * D + pc * 512:c * D + (pc + 1) * 512]), 16)
                    sXI[pc].mark(gq.dma_start(
                        out=xt2[pc][:], in_=x_in[c * CH:(c + 1) * CH, pc * 512:(pc + 1) * 512]), 16)
            st_pro[("gt_ok", 0)] = sGT[0].v
            st_pro[("gt_ok", 1)] = sGT[1].v
            st_pro[("xi_ok", 0)] = sXI[0].v
            st_pro[("xi_ok", 1)] = sXI[1].v

            sdec = [ynb[:, 0:512], ynb[:, 512:1024]]
            sbf_marks = st_pro.get("sbf_marks") or {b: init_ok for b in range(8)}
            new_sbf = {}
            stmp_oks = {}
            ybuf_oks = {}
            for b in range(8):
                g = b // 2
                bi = b % 2

                @blk.gpsimd
                def _(gq, c=c, b=b, so=scale_ok, sbw=sbf_marks[b],
                      sdw=new_sbf.get(b - 2), tpl=st_pro.get("tp_last") if b == 0 else None):
                    gq.wait_ge(sA.h, so)
                    gq.wait_ge(sV.h, sbw)
                    if tpl is not None:
                        gq.wait_ge(sP.h, tpl)
                    if sdw is not None:
                        gq.wait_ge(sV.h, sdw)
                    etv = eto[:, b * 8:(b + 1) * 8].unsqueeze(2).broadcast_to([128, 8, 64])
                    sG.mark(gq.tensor_tensor(
                        sdec[b % 2].rearrange("p (h x) -> p h x", h=8),
                        Sbf[:, b * 512:(b + 1) * 512].rearrange("p (h x) -> p h x", h=8),
                        etv, OP.mult))
                dec_ok = sG.v

                @blk.tensor
                def _(t, c=c, g=g, b=b, bi=bi, x_ld=x_ld, btl_ok=btl_oks[g],
                      btm_ev=btm_ev, sw=sbf_war, ew=ebank.get(bi), xdp_ok=xdp_ok,
                      pev=st_pro.get(("pev", b)) if b < 2 else None):
                    t.wait_ge(sV.h, btl_ok[0])
                    t.wait_ge(sG.h, btl_ok[1])
                    t.wait_ge(sXT.h, x_ld)
                    t.wait_ge(sG.h, xdp_ok)
                    if pev is not None:
                        t.wait_ge(sV.h, pev)
                    if sw is not None:
                        t.wait_ge(sV.h, sw)
                    if ew is not None:
                        t.wait_ge(sV.h, ew)
                    t.matmul(pY[bi][:], i128[:], xdp[:, b * 512:(b + 1) * 512],
                             start=True, stop=False)
                    for hh in range(8):
                        h = b * 8 + hh
                        hl = (h % 16) * CH
                        t.matmul(pY[bi][:, hh * 64:(hh + 1) * 64],
                                 pt[g][:, hl:hl + CH],
                                 xtm[:, h * 64:(h + 1) * 64], start=False, stop=(hh == 7))
                    for hh in range(8):
                        h = b * 8 + hh
                        t.matmul(pYo[bi][:, hh * 64:(hh + 1) * 64],
                                 cfm[:, c * CH:(c + 1) * CH],
                                 Sbf[:, h * 64:(h + 1) * 64], start=True, stop=True)
                    t.wait_ge(sA.h, btm_ev)
                    for hh in range(8):
                        h = b * 8 + hh
                        hx = (h % 16) * 64
                        mi = t.matmul(pSt[bi][:, hh * 64:(hh + 1) * 64],
                                      btm[:],
                                      xts[g][:, hx:hx + 64], start=True, stop=True)
                    sP.mark(mi)
                bank_mm = sP.v
                if b == 7:
                    st_pro["x_free"] = sP.v
                    st_pro["bank_last"] = sP.v
                pt_free[g] = sP.v

                @blk.vector
                def _(v, b=b, bi=bi, bank_mm=bank_mm, so=scale_ok, dec_ok=dec_ok,
                      ybw=ybuf_oks.get(b - 1)):
                    v.wait_ge(sP.h, bank_mm)
                    v.wait_ge(sA.h, so)
                    ecv = expc[:, b * 8:(b + 1) * 8].unsqueeze(2).broadcast_to([128, 8, 64])
                    sV.mark(v.tensor_tensor(ytmp[:].rearrange("p (h x) -> p h x", h=8),
                                            pYo[bi][:].rearrange("p (h x) -> p h x", h=8), ecv, OP.mult))
                    v.wait_ge(sV.h, sV.v)
                    if ybw is not None:
                        v.wait_ge(sG.h, ybw)
                    sV.mark(v.scalar_tensor_tensor(stmp[:], ytmp[:], 1.0, pY[bi][:], OP.mult, OP.add))
                    v.wait_ge(sV.h, sV.v)
                    v.wait_ge(sG.h, dec_ok)
                    sV.mark(v.tensor_tensor(Sbf[:, b * 512:(b + 1) * 512], sdec[b % 2], pSt[bi][:], OP.add))
                stmp_oks[b] = sV.v - 1
                new_sbf[b] = sV.v
                ebank[bi] = sV.v

                @blk.gpsimd
                def _(gq, b=b, z_ld=z_ld, smk=stmp_oks[b],
                      obs=(st_pro.get(("ob_st", 0)), st_pro.get(("ob_st", 1)))):
                    gq.wait_ge(sV.h, smk)
                    gq.wait_ge(sZT.h, z_ld)
                    if b == 0:
                        for pc in range(2):
                            if obs[pc] is not None:
                                gq.wait_ge(sOB2[pc].h, obs[pc])
                    sG.mark(gq.tensor_tensor(ybuf[:, b * 512:(b + 1) * 512], stmp[:],
                                             zst[:, b * 512:(b + 1) * 512], OP.mult))
                ybuf_oks[b] = sG.v
            sbf_war = sV.v
            btl_prev = dict(btl_oks)
            st_pro["sbf_marks"] = new_sbf
            st_pro["evac_last"] = (sV.v, sG.v)
            st_pro["z_free"] = sG.v
            st_pro["y_ok"] = (sV.v, sG.v)

        def tail(c):
            nonlocal ynf_evac
            y_ok = st_pro["y_ok"]

            @blk.scalar
            def _(sc, y_ok=y_ok, ynf=yn_free.get("b")):
                sc.wait_ge(sV.h, y_ok[0])
                sc.wait_ge(sG.h, y_ok[1])
                if ynf is not None:
                    sc.wait_ge(sYN.h, ynf)
                sA.mark(sc.activation(ynb[:], ybuf[:], AF.Square, accum_out=ssq[:, 0:1]))
                sc.wait_ge(sA.h, sA.v)
                sA.mark(sc.activation(ssq[:, 3:4], ssq[:, 0:1], AF.Ln, scale=1.0 / DI, bias=E["epsc"][:]))
                sc.wait_ge(sA.h, sA.v)
                sA.mark(sc.activation(rstd[:], ssq[:, 3:4], AF.Exp, scale=-0.5))
            rstd_ok = sA.v

            @blk.vector
            def _(v, rstd_ok=rstd_ok):
                v.wait_ge(sA.h, rstd_ok)
                sV.mark(v.scalar_tensor_tensor(ynb[:], ybuf[:], rstd[:], nwb[:], OP.mult, OP.mult))
            yn_ok = sV.v

            sPb, sAb = sP.v, sA.v
            for grp in range(8):
                pb = psm[1] if grp % 2 == 0 else pY[0]

                @blk.tensor
                def _(t, grp=grp, pb=pb, yn_ok=yn_ok, sAb=sAb):
                    t.wait_ge(sV.h, yn_ok)
                    if grp >= 2:
                        t.wait_ge(sA.h, sAb + grp - 1)
                    for j in range(4):
                        mi = t.matmul(pb[:, j * 128:(j + 1) * 128],
                                      ynb[:, (grp * 4 + j) * 128:(grp * 4 + j + 1) * 128],
                                      i128[:], start=(j == 0), stop=(j == 3))
                    sP.mark(mi)
                if grp == 7:
                    st_pro["tp_last"] = sP.v

            @blk.scalar
            def _(sc, c=c, sPb=sPb, ynw=yn_free.get("b")):
                for grp in range(8):
                    sc.wait_ge(sP.h, sPb + grp + 1)
                    if grp == 0 and ynw is not None:
                        sc.wait_ge(sYN.h, ynw)
                    pb = psm[1] if grp % 2 == 0 else pY[0]
                    sA.mark(sc.activation(xdp[:, grp * 512:(grp + 1) * 512], pb[:], AF.Copy))
            ynf_evac = sA.v

            @blk.sync
            def _(s, c=c, ye=ynf_evac):
                s.wait_ge(sA.h, ye)
                sYN.mark(s.dma_start(
                    out=yn_sp[:, c * CH:(c + 1) * CH].rearrange("(kc p) t -> p kc t", p=128),
                    in_=xdp[:].rearrange("p (kc t) -> p kc t", kc=32)), 16)
            yn_free["b"] = sYN.v

        def pieces_mm(c):
            ye = ynf_evac
            for pc in range(2):
                @blk.tensor
                def _(t, c=c, pc=pc, ye=ye, ebk=ebank.get(pc)):
                    t.wait_ge(sA.h, ye)
                    if ebk is not None:
                        t.wait_ge(sV.h, ebk)
                    if c == 0:
                        t.wait_ge(sWH0[pc].h, 16)
                    for kc in range(32):
                        mi = t.matmul(pSt[pc][:],
                                      xdp[:, kc * 128:(kc + 1) * 128],
                                      whh[:, pc * 16384 + kc * 512:pc * 16384 + (kc + 1) * 512],
                                      start=(kc == 0), stop=(kc == 31))
                    sP.mark(mi)
                st_pro[("pmm", pc)] = sP.v
            st_pro["piece_mm"] = sP.v
            if c + 1 < NCH:
                xdp_pool(c + 1)

        def pieces_evac(cc):
            for pc in range(2):
                @blk.vector
                def _(v, cc=cc, pc=pc, p_mm=st_pro[("pmm", pc)],
                      gt_ok=st_pro[("gt_ok", pc)], xi_ok=st_pro[("xi_ok", pc)],
                      obw=st_pro.get(("ob_st", pc)),
                      pev0=st_pro.get(("pev", 0)) if pc == 1 else None):
                    v.wait_ge(sP.h, p_mm)
                    v.wait_ge(sGT[pc].h, gt_ok)
                    v.wait_ge(sXI[pc].h, xi_ok)
                    if obw is not None:
                        v.wait_ge(sOB2[pc].h, obw)
                    if pev0 is not None:
                        v.wait_ge(sV.h, pev0)
                    sV.mark(v.tensor_tensor(t1w[:], pSt[pc][:], gt2[pc][:], OP.mult))
                    v.wait_ge(sV.h, sV.v)
                    sV.mark(v.tensor_tensor(ob2[pc][:], t1w[:], xt2[pc][:], OP.add))
                st_pro[("pev", pc)] = sV.v
                p_ev = sV.v

                @blk.sync
                def _(s, cc=cc, pc=pc, p_ev=p_ev):
                    s.wait_ge(sV.h, p_ev)
                    sOB2[pc].mark(s.dma_start(
                        out=out[cc * CH:(cc + 1) * CH, pc * 512:(pc + 1) * 512], in_=ob2[pc][:]), 16)
                st_pro[("ob_st", pc)] = sOB2[pc].v

        x_free_out = None
        prologue(0)
        for c in range(NCH):
            main(c)
            if c + 1 < NCH:
                prologue(c + 1)
            tail(c)
            pieces_mm(c)
        pieces_evac(NCH - 1)
    E["ssd_spill_final"] = (sYN, sYN.v)
    E["ssd_pe_final"] = E["sP"].v
    E["ssd_v_final"] = E["sV"].v
    E["ssd_a_final"] = E["sA"].v
    E["ssd_ob_finals"] = [(sOB2[i], sOB2[i].v) for i in range(2)]


def _wout(E):
    nc, blk = E["nc"], E["blk"]
    sP, sV, sA = E["sP"], E["sV"], E["sA"]
    g_sp, x_in, out, yn_sp, wout_t = E["g_sp"], E["x_in"], E["out"], E["yn_sp"], E["wout_t"]
    sWH = mkstream(E, "sWH")
    sYF = [mkstream(E, f"sYF{i}") for i in range(2)]
    sWX = [mkstream(E, f"sWX{i}") for i in range(4)]
    sGL = [mkstream(E, f"sGL{i}") for i in range(4)]
    sOB = [mkstream(E, f"sOB{i}") for i in range(4)]

    with ExitStack() as c2:
        lsb = lambda n, s, dt: c2.enter_context(nc.sbuf_tensor(n, s, dt))
        wh = [E["whh"]] * 2
        ynf = [lsb(f"wo_yn{i}", [128, 32 * 256], BF16) for i in range(2)]
        xt = [lsb(f"wo_x{i}", [128, 512], F32) for i in range(4)]
        gtl = [lsb(f"wo_g{i}", [128, 512], BF16) for i in range(4)]
        ob = [lsb(f"wo_o{i}", [128, 512], F32) for i in range(4)]
        t1 = [lsb(f"wo_t{i}", [128, 512], F32) for i in range(2)]
        pw = [c2.enter_context(nc.psum_tensor(f"wo_p{i}", [128, 512], F32)) for i in range(4)]

        bank_free = {}
        wh_rd = None
        ynf_rd = {}
        xt_free = {}
        ob_free = {}
        ii = 0
        syn, syv = E["ssd_spill_final"]

        def ssd_guards(q):
            q.wait_ge(syn.h, syv)
            q.wait_ge(sP.h, E["ssd_pe_final"])
            q.wait_ge(sV.h, E["ssd_v_final"])
            q.wait_ge(sA.h, E["ssd_a_final"])
            for st, v in E["ssd_ob_finals"]:
                q.wait_ge(st.h, v)

        for nh in (1,):
            whb = wh[0]

            @blk.gpsimd
            def _(gq):
                ssd_guards(gq)
                sWH.mark(gq.dma_start(
                    out=whb[:].rearrange("p (kc n) -> p kc n", kc=32),
                    in_=wout_t[:, 1024:2048].rearrange("(kc p) n -> p kc n", p=128)), 16)
            wh_ok = (sWH, sWH.v)

            for blk8 in range(8):
                yi = blk8 % 2
                ybuf2 = ynf[yi]
                yrd = ynf_rd.get(yi)
                @blk.sync
                def _(s, nh=nh, blk8=blk8, yi=yi, ybuf2=ybuf2, yrd=yrd):
                    if yrd is not None:
                        s.wait_ge(sP.h, yrd)
                    if blk8 < 2:
                        ssd_guards(s)
                    sYF[yi].mark(s.dma_start(
                        out=ybuf2[:].rearrange("p (kc t) -> p kc t", kc=32),
                        in_=yn_sp[:, blk8 * 256:(blk8 + 1) * 256].rearrange("(kc p) t -> p kc t", p=128)), 16)
                yn_ok0 = sYF[yi].v

                for ms in range(2):
                    for nb in range(2):
                        bi = ii % 4
                        pwar = bank_free.get(bi)
                        xwar = xt_free.get(bi)

                        mtile0 = blk8 * 2 + ms
                        gsl0 = slice(mtile0 * D + nh * 1024 + nb * 512,
                                     mtile0 * D + nh * 1024 + (nb + 1) * 512)

                        @blk.sync
                        def _(s, ii=ii, nh=nh, blk8=blk8, ms=ms, nb=nb, bi=bi, xwar=xwar, gsl0=gsl0):
                            if ii < 4:
                                ssd_guards(s)
                                for st, v in E["g_sp_finals"]:
                                    s.wait_ge(st.h, v)
                            if xwar is not None:
                                s.wait_ge(sV.h, xwar)
                            r0 = blk8 * 256 + ms * 128
                            c0 = nh * 1024 + nb * 512
                            sWX[bi].mark(s.dma_start(out=xt[bi][:], in_=x_in[r0:r0 + 128, c0:c0 + 512]), 16)
                            sGL[bi].mark(s.dma_start(out=gtl[bi][:], in_=g_sp[:, gsl0]), 16)
                        x_ok = sWX[bi].v
                        g_ok = sGL[bi].v

                        @blk.tensor
                        def _(t, ii=ii, yi=yi, ms=ms, nb=nb, bi=bi, wh_ok=wh_ok, whb2=whb,
                              yn_ok0=yn_ok0, ybuf2=ybuf2, pwar=pwar):
                            if ii == 0:
                                t.wait_ge(syn.h, syv)
                            t.wait_ge(wh_ok[0].h, wh_ok[1])
                            t.wait_ge(sYF[yi].h, yn_ok0)
                            if pwar is not None:
                                t.wait_ge(sV.h, pwar)
                            for kc in range(32):
                                mi = t.matmul(pw[bi][:], ybuf2[:, kc * 256 + ms * 128: kc * 256 + (ms + 1) * 128],
                                              whb2[:, kc * 1024 + nb * 512: kc * 1024 + (nb + 1) * 512],
                                              start=(kc == 0), stop=(kc == 31))
                            sP.mark(mi)
                        mm_ok = sP.v
                        wh_rd = sP.v
                        ynf_rd[yi] = sP.v

                        @blk.vector
                        def _(v, ii=ii, bi=bi, mm_ok=mm_ok, x_ok=x_ok, g_ok=g_ok,
                              obw=ob_free.get(ii % 4)):
                            v.wait_ge(sP.h, mm_ok)
                            v.wait_ge(sWX[bi].h, x_ok)
                            v.wait_ge(sGL[bi].h, g_ok)
                            if ii < 4:
                                for st, vv in E["ssd_ob_finals"]:
                                    v.wait_ge(st.h, vv)
                            if obw is not None:
                                v.wait_ge(obw[0].h, obw[1])
                            sV.mark(v.tensor_tensor(t1[bi % 2][:], pw[bi][:], gtl[bi][:], OP.mult))
                            v.wait_ge(sV.h, sV.v)
                            sV.mark(v.tensor_tensor(ob[bi][:], t1[bi % 2][:], xt[bi][:], OP.add))
                        bank_free[bi] = sV.v
                        xt_free[bi] = sV.v
                        o_ok = sV.v

                        @blk.sync
                        def _(s, nh=nh, blk8=blk8, ms=ms, nb=nb, bi=bi, o_ok=o_ok):
                            s.wait_ge(sV.h, o_ok)
                            r0 = blk8 * 256 + ms * 128
                            c0 = nh * 1024 + nb * 512
                            sOB[bi].mark(s.dma_start(out=out[r0:r0 + 128, c0:c0 + 512], in_=ob[bi][:]), 16)
                        ob_free[bi] = (sOB[bi], sOB[bi].v)
                        ii += 1


def _dbg_xn(E):
    blk, sA = E["blk"], E["sA"]
    sDX = mkstream(E, "sDX")

    @blk.gpsimd
    def _(g):
        xr = E["xn_ready"]
        g.wait_ge(xr[0].h, xr[1])
        sDX.mark(g.dma_start(out=E["dbg_xn"], in_=E["xn"][:]), 16)
    E["dbg_xn_done"] = (sDX, sDX.v)


def _dbg_dump(E):
    blk, sV, sA = E["blk"], E["sV"], E["sA"]
    sDB = mkstream(E, "sDB")

    @blk.gpsimd
    def _(g):
        g.wait_ge(sV.h, sV.v)
        g.wait_ge(sA.h, sA.v)
        if "cmb_ready_V" in E:
            sDB.mark(g.dma_start(out=E["dbg_cmb"], in_=E["cmb"][:]), 16)
        if "bc_ready_A" in E:
            sDB.mark(g.dma_start(out=E["dbg_bc"][0:128, :], in_=E["bfm"][:]), 16)
            sDB.mark(g.dma_start(out=E["dbg_bc"][128:256, :], in_=E["cfm"][:]), 16)
        if "dta_ready_V" in E:
            sDB.mark(g.dma_start(out=E["dbg_dta"], in_=E["dta"][:]), 16)


def make_in_maps(u, W_in, W_conv, b_conv, dt_bias, A_log, Dp, norm_w, W_out,
                 gate_w, w1, w2, w3, sw1, sw2, sw3):
    f = lambda a: np.ascontiguousarray(np.asarray(a, dtype=np.float32))
    shared = dict(
        win_t=f(np.asarray(W_in).T),
        wconv_r=f(np.asarray(W_conv)[:, 0, :].reshape(34, 128, 4).transpose(1, 0, 2).reshape(128, 34 * 4)),
        bconv_r=f(np.asarray(b_conv).reshape(34, 128).T),
        dtb_c=f(np.asarray(dt_bias).reshape(NH, 1)),
        alog_c=f(np.asarray(A_log).reshape(NH, 1)),
        dp_row=f(np.asarray(Dp).reshape(1, NH)),
        nw_row=f(np.asarray(norm_w).reshape(1, DI)),
        wout_t=f(np.asarray(W_out).T),
        gate_t=f(np.asarray(gate_w).T),
        w1=f(w1), w2=f(w2), w3=f(w3),
        sw1=f(sw1), sw2=f(sw2), sw3=f(sw3),
        i128_in=np.eye(128, dtype=ml_dtypes.bfloat16),
        i64_in=np.eye(NH, dtype=np.float32),
        mneg_in=np.where(np.arange(128)[:, None] > np.arange(CH)[None, :],
                         np.float32(BIGNEG), np.float32(0.0)),
        ones64_in=np.ones((NH, 128), np.float32),
        mask_in=np.where(np.arange(128)[:, None] <= np.arange(CH)[None, :],
                         ml_dtypes.bfloat16(1.0), ml_dtypes.bfloat16(0.0)),
    )
    u = np.asarray(u, np.float32)
    return [dict(shared, x=np.ascontiguousarray(u[i])) for i in range(8)]


def kernel(L0_attn_u, L0_attn_W_in, L0_attn_W_conv, L0_attn_b_conv,
           L0_attn_dt_bias, L0_attn_A_log, L0_attn_Dp, L0_attn_norm_w,
           L0_attn_W_out, L0_ffn_gate_w, L0_ffn_w1, L0_ffn_w2, L0_ffn_w3,
           L0_ffn_sw1, L0_ffn_sw2, L0_ffn_sw3):
    nc = build()
    in_maps = make_in_maps(
        L0_attn_u, L0_attn_W_in, L0_attn_W_conv, L0_attn_b_conv,
        L0_attn_dt_bias, L0_attn_A_log, L0_attn_Dp, L0_attn_norm_w,
        L0_attn_W_out, L0_ffn_gate_w, L0_ffn_w1, L0_ffn_w2, L0_ffn_w3,
        L0_ffn_sw1, L0_ffn_sw2, L0_ffn_sw3)
    res = run_bass_kernel_spmd(nc, in_maps, core_ids=list(range(8)))
    return np.stack([np.asarray(r["out"], np.float32) for r in res.results], axis=0)


# version 18
# speedup vs baseline: 1.0742x; 1.0103x over previous
import numpy as np
import ml_dtypes
from contextlib import ExitStack

import concourse.bass as bass
import concourse.mybir as mybir
from concourse.bass_utils import run_bass_kernel_spmd
from concourse.alu_op_type import AluOpType as OP

F32 = mybir.dt.float32
BF16 = mybir.dt.bfloat16
U32 = mybir.dt.uint32
AF = mybir.ActivationFunctionType
AX = mybir.AxisListType

D = 2048
L = 2048
DI = 4096
NH = 64
CH = 128
NCH = L // CH
FH = 1408
NE = 9
KC = D // 128
BIGNEG = -1e9
PH_LN, PH_INPROJ, PH_GATE, PH_F1, PH_F2, PH_SSD, PH_WOUT = range(7)


class Sem:
    def __init__(self, nc, ctx, name):
        self.h = ctx.enter_context(nc.semaphore(name))
        self.v = 0

    def mark(self, inst, n=1):
        inst.then_inc(self.h, n)
        self.v += n
        return self.v


def mkstream(E, name):
    s = Sem(E["nc"], E["ctx"], name)
    E.setdefault("streams", []).append(s)
    return s


def build(max_phase=PH_WOUT, dbg=False):
    nc = bass.Bass("TRN2", target_bir_lowering=False)
    dram = lambda n, s, dt=F32, kind="ExternalInput": nc.dram_tensor(n, s, dt, kind=kind).ap()

    E = {"nc": nc, "dbg": dbg}
    E["x_in"] = dram("x", [L, D])
    E["win_t"] = dram("win_t", [D, 8512])
    E["wconv_r"] = dram("wconv_r", [128, 34 * 4])
    E["bconv_r"] = dram("bconv_r", [128, 34])
    E["dtb_c"] = dram("dtb_c", [NH, 1])
    E["alog_c"] = dram("alog_c", [NH, 1])
    E["dp_row"] = dram("dp_row", [1, NH])
    E["nw_row"] = dram("nw_row", [1, DI])
    E["wout_t"] = dram("wout_t", [DI, D])
    E["gate_t"] = dram("gate_t", [D, 8])
    E["w1"] = dram("w1", [8, D, FH])
    E["w2"] = dram("w2", [8, FH, D])
    E["w3"] = dram("w3", [8, D, FH])
    E["sw1"] = dram("sw1", [D, FH])
    E["sw2"] = dram("sw2", [FH, D])
    E["sw3"] = dram("sw3", [D, FH])
    E["i128_in"] = dram("i128_in", [128, 128], BF16)
    E["i64_in"] = dram("i64_in", [NH, NH])
    E["mneg_in"] = dram("mneg_in", [128, CH])
    E["ones64_in"] = dram("ones64_in", [NH, 128])
    E["mask_in"] = dram("mask_in", [128, CH], BF16)
    E["out"] = dram("out", [L, D], F32, kind="ExternalOutput")

    ik = "ExternalOutput" if dbg else "Internal"
    E["zs_sp"] = dram("zs_sp", [DI, L], BF16, kind=ik)
    E["x_sp"] = dram("x_sp", [DI, L], BF16, kind=ik)
    E["h_sp"] = dram("h_sp", [NE * FH, L], BF16, kind=ik)
    E["yn_sp"] = dram("yn_sp", [DI, L], BF16, kind=ik)
    E["cs_sp"] = dram("cs_sp", [2 * NH, CH], F32, kind="Internal")
    E["g_sp"] = dram("g_sp", [128, 16 * D], BF16, kind=ik)
    if dbg:
        E["dbg_bc"] = dram("dbg_bc", [256, L], F32, kind="ExternalOutput")
        E["dbg_dta"] = dram("dbg_dta", [NH, L], F32, kind="ExternalOutput")
        E["dbg_cmb"] = dram("dbg_cmb", [128, 16 * 8], F32, kind="ExternalOutput")
        E["dbg_xn"] = dram("dbg_xn", [128, KC * L], F32, kind="ExternalOutput")

    with ExitStack() as ctx:
        E["ctx"] = ctx
        sb = lambda n, s, dt: ctx.enter_context(nc.sbuf_tensor(n, s, dt))
        for n, s, dt in [
            ("i128", [128, 128], BF16), ("i64f", [NH, NH], F32),
            ("bfm", [128, L], BF16), ("cfm", [128, L], BF16),
            ("dta", [NH, L], F32), ("cmb", [128, 16 * 8], F32),
            ("wconv", [128, 34 * 4], F32), ("bconv", [128, 34], F32),
            ("dtb", [NH, 1], F32), ("nega", [NH, 1], F32), ("scrc", [NH, 1], F32),
            ("dpb", [128, NH], F32), ("dpbbf", [128, NH], BF16),
            ("nwb", [128, DI], BF16),
            ("epsc", [128, 1], F32), ("ones64", [NH, 128], F32),
            ("mask128", [128, CH], BF16),
        ]:
            E[n] = sb(n, s, dt)
        for n in ("sP", "sV", "sA"):
            E[n] = Sem(nc, ctx, n)
        E["blk"] = ctx.enter_context(nc.Block())

        _init(E)
        with ExitStack() as ctx_xn:
            E["xn"] = ctx_xn.enter_context(nc.sbuf_tensor("xn", [128, KC * L], BF16))
            with ExitStack() as ctx_head:
                E["pT"] = [ctx_head.enter_context(nc.psum_tensor(f"h_pT{i}", [128, 512], F32)) for i in range(2)]
                E["pmm"] = [ctx_head.enter_context(nc.psum_tensor(f"h_pm{i}", [128, 512], F32)) for i in range(4)]
                E["pg"] = ctx_head.enter_context(nc.psum_tensor("h_pg", [128, 128], F32))
                hsb = lambda n, s, dt: ctx_head.enter_context(nc.sbuf_tensor(n, s, dt))
                E["wblk"] = [hsb(f"ip_w{i}", [128, KC * 128], BF16) for i in range(2)]
                E["zstage"] = [hsb(f"ip_zs{i}", [128, L], BF16) for i in range(2)]
                E["xstage"] = [hsb(f"ip_xst{i}", [128, L], BF16) for i in range(2)]
                E["xpre"] = hsb("ip_xpre", [128, L + 3], F32)
                E["cv"] = hsb("ip_cv", [128, L], F32)
                E["cva"] = hsb("ip_cva", [128, L], F32)
                E["dstage"] = hsb("ip_dst", [NH, L], F32)
                E["dt0"] = hsb("ip_dt0", [NH, L], F32)
                E["dt1"] = hsb("ip_dt1", [NH, L], F32)
                E["gwb"] = hsb("gt_w", [128, KC * 8], BF16)
                E["gsc"] = hsb("gt_s", [128, 128], F32)
                _ln(E)
                if max_phase >= PH_INPROJ:
                    _inproj(E)
                if max_phase >= PH_GATE:
                    _gate(E)
            if max_phase >= PH_F1:
                _f1(E)
            if dbg:
                _dbg_xn(E)
        if max_phase >= PH_F2:
            _f2(E)
        if max_phase >= PH_SSD:
            E["whh"] = ctx.enter_context(nc.sbuf_tensor("whh", [128, 32 * 1024], BF16))
            _ssd(E)
        if max_phase >= PH_WOUT:
            _wout(E)
        if dbg:
            _dbg_dump(E)

        @E["blk"].sync
        def _(s):
            for st in E.get("streams", []):
                s.wait_ge(st.h, st.v)
            s.wait_ge(E["sP"].h, E["sP"].v)
            s.wait_ge(E["sV"].h, E["sV"].v)
            s.wait_ge(E["sA"].h, E["sA"].v)
    return nc


def _init(E):
    blk, sA, sV = E["blk"], E["sA"], E["sV"]
    sIC = mkstream(E, "sIC")

    @blk.gpsimd
    def _(g):
        sIC.mark(g.dma_start(out=E["i128"][:], in_=E["i128_in"]), 16)
        sIC.mark(g.dma_start(out=E["i64f"][:], in_=E["i64_in"]), 16)
        sIC.mark(g.dma_start(out=E["wconv"][:], in_=E["wconv_r"]), 16)
        sIC.mark(g.dma_start(out=E["bconv"][:], in_=E["bconv_r"]), 16)
        sIC.mark(g.dma_start(out=E["dtb"][:], in_=E["dtb_c"]), 16)
        sIC.mark(g.dma_start(out=E["scrc"][:], in_=E["alog_c"]), 16)
        sIC.mark(g.dma_start(out=E["dpb"][:], in_=E["dp_row"].broadcast_to([128, NH])), 16)
        sIC.mark(g.dma_start(out=E["nwb"][:], in_=E["nw_row"].broadcast_to([128, DI])), 16)
        sIC.mark(g.dma_start(out=E["ones64"][:], in_=E["ones64_in"]), 16)
        sIC.mark(g.dma_start(out=E["mask128"][:], in_=E["mask_in"]), 16)
    E["init_loads"] = (sIC, sIC.v)

    @blk.scalar
    def _(sc):
        sc.wait_ge(sIC.h, sIC.v)
        sA.mark(sc.activation(E["nega"][:], E["scrc"][:], AF.Exp))
        sA.mark(sc.activation(E["dpbbf"][:], E["dpb"][:], AF.Copy))
    E["dpbbf_ok"] = sA.v

    @blk.vector
    def _(v):
        v.memset(E["epsc"][:], 1e-5)
        v.wait_ge(sA.h, sA.v)
        sV.mark(v.tensor_scalar(E["scrc"][:], E["nega"][:], -1.0, None, OP.mult))
        v.wait_ge(sV.h, sV.v)
        sV.mark(v.tensor_copy(E["nega"][:], E["scrc"][:]))
    E["nega_ready"] = sV.v


def _ln(E):
    nc, blk = E["nc"], E["blk"]
    sP, sV, sA = E["sP"], E["sV"], E["sA"]
    xn, i128, x_in = E["xn"], E["i128"], E["x_in"]
    sLN = [mkstream(E, f"sLN{j}") for j in range(4)]

    with ExitStack() as c2:
        lsb = lambda n, s, dt: c2.enter_context(nc.sbuf_tensor(n, s, dt))
        xq = [lsb(f"ln_x{j}", [128, D], F32) for j in range(2)]
        xnq = [lsb(f"ln_xn{j}", [128, D], BF16) for j in range(4)]
        st = lsb("ln_stats", [128, 8], F32)
        pT = E["pT"]

        E["ln_sm"] = {}
        bank_free = {}
        xq_free = {}
        ln_mm_last = {}
        prev_sm_done = None
        for sm in range(4):
            for j in range(4):
                m = sm * 4 + j

                @blk.sync
                def _(s, m=m, j=j, prev=xq_free.get(j % 2)):
                    if prev is not None:
                        s.wait_ge(sV.h, prev)
                    sLN[j].mark(s.dma_start(out=xq[j % 2][:], in_=x_in[m * 128:(m + 1) * 128, :]), 16)
                ld = sLN[j].v

                @blk.vector
                def _(v, j=j, ld=ld):
                    v.wait_ge(sLN[j].h, ld)
                    sV.mark(v.reduce_sum(st[:, 0:1], xq[j % 2][:], AX.X))
                    v.wait_ge(sV.h, sV.v)
                    sV.mark(v.tensor_scalar(st[:, 4:5], st[:, 0:1], 1.0 / D, None, OP.mult))
                mean_ok = sV.v

                @blk.scalar
                def _(sc, j=j, ld=ld, tml=ln_mm_last.get(0)):
                    sc.wait_ge(sLN[j].h, ld)
                    if tml is not None:
                        sc.wait_ge(sP.h, tml)
                    sA.mark(sc.activation(xnq[j][:], xq[j % 2][:], AF.Square, accum_out=st[:, 1:2]))
                ssq_ok = sA.v

                @blk.vector
                def _(v, mean_ok=mean_ok, ssq_ok=ssq_ok):
                    v.wait_ge(sV.h, mean_ok)
                    v.wait_ge(sA.h, ssq_ok)
                    sV.mark(v.tensor_scalar(st[:, 5:6], st[:, 1:2], 1.0 / D, None, OP.mult))
                    sV.mark(v.tensor_tensor(st[:, 6:7], st[:, 4:5], st[:, 4:5], OP.mult))
                    v.wait_ge(sV.h, sV.v)
                    sV.mark(v.tensor_tensor(st[:, 2:3], st[:, 5:6], st[:, 6:7], OP.subtract))
                var_ok = sV.v

                @blk.scalar
                def _(sc, var_ok=var_ok):
                    sc.wait_ge(sV.h, var_ok)
                    sA.mark(sc.activation(st[:, 7:8], st[:, 2:3], AF.Ln, bias=E["epsc"][:]))
                    sc.wait_ge(sA.h, sA.v)
                    sA.mark(sc.activation(st[:, 3:4], st[:, 7:8], AF.Exp, scale=-0.5))
                rstd_ok = sA.v

                @blk.vector
                def _(v, j=j, rstd_ok=rstd_ok, var_ok=var_ok):
                    v.wait_ge(sA.h, rstd_ok)
                    v.wait_ge(sV.h, var_ok)
                    sV.mark(v.tensor_scalar(xnq[j][:], xq[j % 2][:], st[:, 4:5], st[:, 3:4],
                                            OP.subtract, OP.mult))
                xq_free[j % 2] = sV.v
            xn_ready = sV.v
            for kc in range(KC):
                pb = pT[kc % 2]
                war = bank_free.get(kc % 2)

                @blk.tensor
                def _(t, kc=kc, pb=pb, war=war, xn_ready=xn_ready):
                    t.wait_ge(sV.h, xn_ready)
                    if war is not None:
                        t.wait_ge(sV.h, war)
                    for j in range(4):
                        mi = t.matmul(pb[:, j * 128:(j + 1) * 128], xnq[j][:, kc * 128:(kc + 1) * 128],
                                      i128[:], start=(j == 0), stop=(j == 3))
                    sP.mark(mi)
                mm_ok = sP.v
                if kc == KC - 1:
                    ln_mm_last[0] = sP.v

                @blk.vector
                def _(v, sm=sm, kc=kc, pb=pb, mm_ok=mm_ok):
                    v.wait_ge(sP.h, mm_ok)
                    sV.mark(v.tensor_copy(xn[:, kc * L + sm * 512: kc * L + (sm + 1) * 512], pb[:]))
                bank_free[kc % 2] = sV.v
            prev_sm_done = sV.v
            E["ln_sm"][sm] = sV.v
    E["xn_ready"] = (sV, sV.v)


def _inproj(E):
    nc, blk = E["nc"], E["blk"]
    sP, sV, sA = E["sP"], E["sV"], E["sA"]
    xn = E["xn"]
    wconv, bconv, dtb, nega, dta = E["wconv"], E["bconv"], E["dtb"], E["nega"], E["dta"]
    bfm, cfm = E["bfm"], E["cfm"]
    zs_sp, x_sp, win_t = E["zs_sp"], E["x_sp"], E["win_t"]
    sWB = [mkstream(E, f"sWB{i}") for i in range(2)]
    sZS = [mkstream(E, f"sZS{i}") for i in range(2)]
    sXS = [mkstream(E, f"sXS{i}") for i in range(2)]

    if True:
        wblk = E["wblk"]
        pmm = E["pmm"]
        zstage = E["zstage"]
        xstage = E["xstage"]
        xpre = E["xpre"]
        cv = E["cv"]
        cva = E["cva"]
        dstage = E["dstage"]
        dt0 = E["dt0"]
        dt1 = E["dt1"]

        @blk.vector
        def _(v):
            sV.mark(v.memset(xpre[:, 0:3], 0.0))
        xpre0_ok = sV.v

        wload = {}
        mm_done = {}
        bank_free = {}
        zst_free = {}
        xst_free = {}
        cv_free = None
        conv_done = None

        for mb in range(67):
            msz = 64 if mb == 66 else 128
            buf = wblk[mb % 2]
            war = mm_done.get(mb - 2)

            @blk.gpsimd
            def _(gq, mb=mb, msz=msz, buf=buf, war=war):
                if war is not None:
                    gq.wait_ge(sP.h, war)
                src = win_t[:, mb * 128: mb * 128 + msz].rearrange("(kc p) m -> p kc m", p=128)
                sWB[mb % 2].mark(gq.dma_start(
                    out=buf[:, 0:KC * msz].rearrange("p (kc m) -> p kc m", kc=KC),
                    in_=src), 16)
            wload[mb] = sWB[mb % 2].v

            mm_oks = []
            for tb in range(4):
                pb = pmm[tb]
                pwar = bank_free.get(tb)

                @blk.tensor
                def _(t, mb=mb, tb=tb, msz=msz, buf=buf, pb=pb, pwar=pwar):
                    t.wait_ge(sWB[mb % 2].h, wload[mb])
                    if mb < 2:
                        t.wait_ge(sV.h, E["ln_sm"][tb])
                    if pwar is not None:
                        t.wait_ge(pwar[0].h, pwar[1])
                    dst = pb[0:msz, :] if msz < 128 else pb[:]
                    for kc in range(KC):
                        mi = t.matmul(dst, buf[:, kc * msz:(kc + 1) * msz],
                                      xn[:, kc * L + tb * 512: kc * L + (tb + 1) * 512],
                                      start=(kc == 0), stop=(kc == KC - 1))
                    sP.mark(mi)
                mm_oks.append(sP.v)
            mm_done[mb] = sP.v

            if mb < 32:
                zbuf = zstage[mb % 2]

                @blk.scalar
                def _(sc, mb=mb, zbuf=zbuf, mm_oks=tuple(mm_oks)):
                    w = zst_free.get(mb - 2)
                    if w is not None:
                        sc.wait_ge(w[0].h, w[1])
                    for tb in range(4):
                        sc.wait_ge(sP.h, mm_oks[tb])
                        sA.mark(sc.activation(zbuf[:, tb * 512:(tb + 1) * 512], pmm[tb][:], AF.Silu))
                        bank_free[tb] = (sA, sA.v)

                @blk.sync
                def _(s, mb=mb, zbuf=zbuf, a=sA.v):
                    s.wait_ge(sA.h, a)
                    sZS[mb % 2].mark(s.dma_start(out=zs_sp[mb * 128:(mb + 1) * 128, :],
                                                 in_=zbuf[:]), 16)
                zst_free[mb] = (sZS[mb % 2], sZS[mb % 2].v)
            elif mb < 66:
                cb = mb - 32

                @blk.vector
                def _(v, cb=cb, mm_oks=tuple(mm_oks), cd=conv_done, cvf=cv_free):
                    if cd is not None:
                        v.wait_ge(sV.h, cd)
                    for tb in range(4):
                        v.wait_ge(sP.h, mm_oks[tb])
                        sV.mark(v.tensor_copy(xpre[:, 3 + tb * 512: 3 + (tb + 1) * 512], pmm[tb][:]))
                        bank_free[tb] = (sV, sV.v)
                    if cb == 0:
                        v.wait_ge(sV.h, xpre0_ok)
                    if cvf is not None:
                        v.wait_ge(sA.h, cvf)
                    v.wait_ge(sV.h, sV.v)
                    c0 = cb * 4
                    sV.mark(v.tensor_scalar(cva[:], xpre[:, 0:L], wconv[:, c0:c0 + 1], None, OP.mult))
                    v.wait_ge(sV.h, sV.v)
                    sV.mark(v.scalar_tensor_tensor(cv[:], xpre[:, 1:1 + L], wconv[:, c0 + 1:c0 + 2], cva[:], OP.mult, OP.add))
                    v.wait_ge(sV.h, sV.v)
                    sV.mark(v.scalar_tensor_tensor(cva[:], xpre[:, 2:2 + L], wconv[:, c0 + 2:c0 + 3], cv[:], OP.mult, OP.add))
                    v.wait_ge(sV.h, sV.v)
                    sV.mark(v.scalar_tensor_tensor(cv[:], xpre[:, 3:3 + L], wconv[:, c0 + 3:c0 + 4], cva[:], OP.mult, OP.add))
                conv_done = sV.v
                cv_ok = sV.v
                if cb < 32:
                    xbuf = xstage[cb % 2]

                    @blk.scalar
                    def _(sc, cb=cb, xbuf=xbuf, cv_ok=cv_ok):
                        sc.wait_ge(sV.h, cv_ok)
                        w = xst_free.get(cb - 2)
                        if w is not None:
                            sc.wait_ge(w[0].h, w[1])
                        sA.mark(sc.activation(xbuf[:], cv[:], AF.Silu, bias=bconv[:, cb:cb + 1]))
                    cv_free = sA.v

                    @blk.sync
                    def _(s, cb=cb, xbuf=xbuf, a=sA.v):
                        s.wait_ge(sA.h, a)
                        sXS[cb % 2].mark(s.dma_start(out=x_sp[cb * 128:(cb + 1) * 128, :],
                                                     in_=xbuf[:]), 16)
                    xst_free[cb] = (sXS[cb % 2], sXS[cb % 2].v)
                else:
                    dstb = bfm if cb == 32 else cfm

                    @blk.scalar
                    def _(sc, cb=cb, dstb=dstb, cv_ok=cv_ok):
                        sc.wait_ge(sV.h, cv_ok)
                        sA.mark(sc.activation(dstb[:], cv[:], AF.Silu, bias=bconv[:, cb:cb + 1]))
                    cv_free = sA.v
            else:
                @blk.vector
                def _(v, mm_oks=tuple(mm_oks)):
                    for tb in range(4):
                        v.wait_ge(sP.h, mm_oks[tb])
                        sl = slice(tb * 512, (tb + 1) * 512)
                        sV.mark(v.tensor_scalar(dstage[:, sl], pmm[tb][0:NH, :], dtb[:], None, OP.add))
                        bank_free[tb] = (sV, sV.v)

        raw_ok = sV.v

        @blk.scalar
        def _(sc, raw_ok=raw_ok):
            sc.wait_ge(sV.h, raw_ok)
            sA.mark(sc.activation(dt0[:], dstage[:], AF.Abs))
            sc.wait_ge(sA.h, sA.v)
            sA.mark(sc.activation(dt1[:], dt0[:], AF.Exp, scale=-1.0))
            sc.wait_ge(sA.h, sA.v)
            sA.mark(sc.activation(dt0[:], dt1[:], AF.Ln, bias=1.0))
        sp_ok = sA.v

        @blk.vector
        def _(v, sp_ok=sp_ok):
            v.wait_ge(sA.h, sp_ok)
            v.wait_ge(sV.h, E["nega_ready"])
            sV.mark(v.tensor_scalar(dt1[:], dstage[:], 0.0, None, OP.max))
            v.wait_ge(sV.h, sV.v)
            sV.mark(v.tensor_tensor(dstage[:], dt1[:], dt0[:], OP.add))
            v.wait_ge(sV.h, sV.v)
            sV.mark(v.tensor_scalar(dta[:], dstage[:], nega[:], None, OP.mult))
    E["dta_ready_V"] = sV.v
    E["bc_ready_A"] = sA.v
    E["inproj_spill_finals"] = [(s, s.v) for s in sZS + sXS]


def _gate(E):
    nc, blk = E["nc"], E["blk"]
    sP, sV, sA = E["sP"], E["sV"], E["sA"]
    xn, cmb, gate_t = E["xn"], E["cmb"], E["gate_t"]
    sGW = mkstream(E, "sGW")
    gw, sc_all, pg = E["gwb"], E["gsc"], E["pg"]

    with ExitStack() as c2:
        lsb = lambda n, s, dt: c2.enter_context(nc.sbuf_tensor(n, s, dt))
        m8 = lsb("gt_m8", [128, 8], F32)
        z8 = lsb("gt_z8", [128, 8], F32)
        den = lsb("gt_den", [128, 1], F32)
        rcp = lsb("gt_rcp", [128, 1], F32)
        pick = lsb("gt_pick", [128, 8], F32)
        msk = lsb("gt_msk", [128, 8], U32)

        @blk.gpsimd
        def _(g):
            sGW.mark(g.dma_start(out=gw[:].rearrange("p (kc m) -> p kc m", kc=KC),
                                 in_=gate_t.rearrange("(kc p) m -> p kc m", p=128)), 16)
        gw_ok = sGW.v

        @blk.vector
        def _(v):
            v.memset(z8[:], 0.0)

        @blk.tensor
        def _(t):
            t.wait_ge(sGW.h, gw_ok)
            xr = E["xn_ready"]
            t.wait_ge(xr[0].h, xr[1])
            for m in range(16):
                for kc in range(KC):
                    mi = t.matmul(pg[:, m * 8:(m + 1) * 8],
                                  xn[:, kc * L + m * 128: kc * L + (m + 1) * 128],
                                  gw[:, kc * 8:(kc + 1) * 8],
                                  start=(kc == 0), stop=(kc == KC - 1))
            sP.mark(mi)
        mm_ok = sP.v

        @blk.scalar
        def _(sc, mm_ok=mm_ok):
            sc.wait_ge(sP.h, mm_ok)
            sA.mark(sc.activation(sc_all[:], pg[:], AF.Sigmoid))
        E["gate_sig_A"] = sA.v
        sig_ok = sA.v

        for m in range(16):
            @blk.vector
            def _(v, m=m, sig_ok=sig_ok):
                if m == 0:
                    v.wait_ge(sA.h, sig_ok)
                sV.mark(v.max(out=m8[:], in_=sc_all[:, m * 8:(m + 1) * 8]))
                v.wait_ge(sV.h, sV.v)
                sV.mark(v.tensor_scalar(den[:], m8[:, 0:1], m8[:, 1:2], 1e-20, OP.add, OP.add))
                v.wait_ge(sV.h, sV.v)
                sV.mark(v.reciprocal(rcp[:], den[:]))
                v.wait_ge(sV.h, sV.v)
                sV.mark(v.tensor_scalar(pick[:], sc_all[:, m * 8:(m + 1) * 8], rcp[:], 2.5, OP.mult, OP.mult))
                sV.mark(v.tensor_scalar(msk[:], sc_all[:, m * 8:(m + 1) * 8], m8[:, 1:2], None, OP.is_ge))
                v.wait_ge(sV.h, sV.v)
                sV.mark(v.select(cmb[:, m * 8:(m + 1) * 8], msk[:], pick[:], z8[:], add_drain=True))
    E["cmb_ready_V"] = sV.v


def _f1(E):
    nc, blk = E["nc"], E["blk"]
    sP, sV, sA = E["sP"], E["sV"], E["sA"]
    xn = E["xn"]
    h_sp, w1, w3, sw1, sw3 = E["h_sp"], E["w1"], E["w3"], E["sw1"], E["sw3"]
    sFW = [mkstream(E, f"sFW{i}") for i in range(2)]
    sHS = [mkstream(E, f"sHS{i}") for i in range(2)]

    with ExitStack() as c2:
        lsb = lambda n, s, dt: c2.enter_context(nc.sbuf_tensor(n, s, dt))
        wa = [lsb(f"f1_wa{i}", [128, KC * 128], BF16) for i in range(2)]
        wb = [lsb(f"f1_wb{i}", [128, KC * 128], BF16) for i in range(2)]
        sil = [lsb(f"f1_sil{i}", [128, 512], F32) for i in range(2)]
        hstage = [lsb(f"f1_h{i}", [128, L], BF16) for i in range(2)]
        p1 = [c2.enter_context(nc.psum_tensor(f"f1_p1{i}", [128, 512], F32)) for i in range(2)]
        p3 = [c2.enter_context(nc.psum_tensor(f"f1_p3{i}", [128, 512], F32)) for i in range(2)]

        wload = {}
        mm_done = {}
        bank_free = {}
        hst_free = {}
        sil_free = {}

        for idx in range(NE * 11):
            e, mb = divmod(idx, 11)
            bi = idx % 2
            war = mm_done.get(idx - 2)

            @blk.gpsimd
            def _(gq, idx=idx, e=e, mb=mb, bi=bi, war=war):
                if idx < 2:
                    gq.wait_ge(sV.h, E["cmb_ready_V"])
                    gq.wait_ge(sV.h, E["dta_ready_V"])
                    gq.wait_ge(sA.h, E["bc_ready_A"])
                    for s, v in E["inproj_spill_finals"]:
                        gq.wait_ge(s.h, v)
                if war is not None:
                    gq.wait_ge(sP.h, war)
                if e < 8:
                    s1 = w1[e, :, mb * 128:(mb + 1) * 128]
                    s3 = w3[e, :, mb * 128:(mb + 1) * 128]
                else:
                    s1 = sw1[:, mb * 128:(mb + 1) * 128]
                    s3 = sw3[:, mb * 128:(mb + 1) * 128]
                sFW[bi].mark(gq.dma_start(out=wa[bi][:].rearrange("p (kc m) -> p kc m", kc=KC),
                                          in_=s1.rearrange("(kc p) m -> p kc m", p=128)), 16)
                sFW[bi].mark(gq.dma_start(out=wb[bi][:].rearrange("p (kc m) -> p kc m", kc=KC),
                                          in_=s3.rearrange("(kc p) m -> p kc m", p=128)), 16)
            wload[idx] = sFW[bi].v

            hbuf = hstage[idx % 2]
            for tb in range(4):
                pi = tb % 2
                pwar = bank_free.get(pi)

                @blk.tensor
                def _(t, idx=idx, tb=tb, bi=bi, pi=pi, pwar=pwar):
                    t.wait_ge(sFW[bi].h, wload[idx])
                    if idx == 0 and tb == 0:
                        t.wait_ge(sA.h, E["gate_sig_A"])
                        t.wait_ge(sV.h, E["dta_ready_V"])
                    if pwar is not None:
                        t.wait_ge(sV.h, pwar)
                    for kc in range(KC):
                        t.matmul(p1[pi][:], wa[bi][:, kc * 128:(kc + 1) * 128],
                                 xn[:, kc * L + tb * 512: kc * L + (tb + 1) * 512],
                                 start=(kc == 0), stop=(kc == KC - 1))
                    for kc in range(KC):
                        mi = t.matmul(p3[pi][:], wb[bi][:, kc * 128:(kc + 1) * 128],
                                      xn[:, kc * L + tb * 512: kc * L + (tb + 1) * 512],
                                      start=(kc == 0), stop=(kc == KC - 1))
                    sP.mark(mi)
                mm_ok = sP.v
                if tb == 3:
                    mm_done[idx] = sP.v

                @blk.scalar
                def _(sc, pi=pi, mm_ok=mm_ok, sf=sil_free.get(tb % 2)):
                    sc.wait_ge(sP.h, mm_ok)
                    if sf is not None:
                        sc.wait_ge(sV.h, sf)
                    sA.mark(sc.activation(sil[pi][:], p1[pi][:], AF.Silu))
                sil_ok = sA.v

                @blk.vector
                def _(v, idx=idx, tb=tb, pi=pi, hbuf=hbuf, sil_ok=sil_ok):
                    v.wait_ge(sA.h, sil_ok)
                    if tb == 0:
                        w = hst_free.get(idx - 2)
                        if w is not None:
                            v.wait_ge(w[0].h, w[1])
                    sV.mark(v.tensor_tensor(hbuf[:, tb * 512:(tb + 1) * 512], sil[pi][:],
                                            p3[pi][:], OP.mult))
                bank_free[pi] = sV.v
                sil_free[tb % 2] = sV.v

            @blk.sync
            def _(s, idx=idx, e=e, mb=mb, hbuf=hbuf, vv=sV.v):
                s.wait_ge(sV.h, vv)
                sHS[idx % 2].mark(s.dma_start(
                    out=h_sp[e * FH + mb * 128: e * FH + (mb + 1) * 128, :],
                    in_=hbuf[:]), 16)
            hst_free[idx] = (sHS[idx % 2], sHS[idx % 2].v)
    E["f1_spill_finals"] = [(s, s.v) for s in sHS]


def _f2(E):
    nc, blk = E["nc"], E["blk"]
    sP, sV, sA = E["sP"], E["sV"], E["sA"]
    cmb, g_sp = E["cmb"], E["g_sp"]
    h_sp, w2, sw2 = E["h_sp"], E["w2"], E["sw2"]
    sW2 = [mkstream(E, f"sW2{i}") for i in range(3)]
    sHB = [mkstream(E, f"sHB{i}") for i in range(3)]
    sGS = [mkstream(E, f"sGS{i}") for i in range(2)]

    with ExitStack() as c2:
        lsb = lambda n, s, dt: c2.enter_context(nc.sbuf_tensor(n, s, dt))
        acc = lsb("f2_acc", [128, 8 * D], F32)
        w2h = [lsb(f"f2_w{i}", [128, 11 * 512], BF16) for i in range(3)]
        hb = [lsb(f"f2_h{i}", [128, 11 * 256], BF16) for i in range(3)]
        gst = [lsb(f"f2_g{i}", [128, D], BF16) for i in range(2)]
        pf = [c2.enter_context(nc.psum_tensor(f"f2_p{i}", [128, 512], F32)) for i in range(4)]

        wload = {}
        hload = {}
        w_used = {}
        h_used = {}
        bank_free = {}
        acc_mark = {}
        silu_done = None

        widx = 0
        hidx = 0
        for th in range(2):
            for e in range(NE):
                for nbq in range(4):
                    wbuf = w2h[widx % 3]
                    war = w_used.get(widx - 3)

                    @blk.gpsimd
                    def _(gq, widx=widx, e=e, nbq=nbq, wbuf=wbuf, war=war):
                        if widx < 2:
                            for s, v in E["f1_spill_finals"]:
                                gq.wait_ge(s.h, v)
                            if "dbg_xn_done" in E:
                                gq.wait_ge(E["dbg_xn_done"][0].h, E["dbg_xn_done"][1])
                        if war is not None:
                            gq.wait_ge(sP.h, war)
                        src = w2[e, :, nbq * 512:(nbq + 1) * 512] if e < 8 else \
                            sw2[:, nbq * 512:(nbq + 1) * 512]
                        sW2[widx % 3].mark(gq.dma_start(
                            out=wbuf[:].rearrange("p (kc n) -> p kc n", kc=11),
                            in_=src.rearrange("(kc p) n -> p kc n", p=128)), 16)
                    wload[widx] = sW2[widx % 3].v

                    for mb in range(4):
                        hbuf = hb[hidx % 3]
                        hwar = h_used.get(hidx - 3)

                        @blk.sync
                        def _(s, hidx=hidx, th=th, e=e, mb=mb, hbuf=hbuf, hwar=hwar):
                            if hidx < 2:
                                for st, v in E["f1_spill_finals"]:
                                    s.wait_ge(st.h, v)
                            if hwar is not None:
                                s.wait_ge(sP.h, hwar)
                            c0 = th * 1024 + mb * 256
                            sHB[hidx % 3].mark(s.dma_start(
                                out=hbuf[:].rearrange("p (kc t) -> p kc t", kc=11),
                                in_=h_sp[e * FH:(e + 1) * FH, c0:c0 + 256].rearrange("(kc p) t -> p kc t", p=128)), 16)
                        hload[hidx] = sHB[hidx % 3].v

                        for ms in range(2):
                            bk = (mb % 2) * 2 + ms
                            pwar = bank_free.get(bk)

                            @blk.tensor
                            def _(t, widx=widx, hidx=hidx, ms=ms, bk=bk, pwar=pwar,
                                  wbuf=wbuf, hbuf=hbuf):
                                t.wait_ge(sW2[widx % 3].h, wload[widx])
                                t.wait_ge(sHB[hidx % 3].h, hload[hidx])
                                if pwar is not None:
                                    t.wait_ge(sV.h, pwar)
                                for kc in range(11):
                                    mi = t.matmul(pf[bk][:], hbuf[:, kc * 256 + ms * 128: kc * 256 + (ms + 1) * 128],
                                                  wbuf[:, kc * 512:(kc + 1) * 512],
                                                  start=(kc == 0), stop=(kc == 10))
                                sP.mark(mi)
                            mm_ok = sP.v
                            w_used[widx] = sP.v
                            h_used[hidx] = sP.v

                            j = mb * 2 + ms
                            asl = slice(j * D + nbq * 512, j * D + (nbq + 1) * 512)
                            mtile = th * 8 + j
                            first = e == 0

                            @blk.vector
                            def _(v, e=e, bk=bk, asl=asl, mtile=mtile, nbq=nbq, first=first,
                                  mm_ok=mm_ok, sd=silu_done, am=acc_mark.get((mtile, nbq))):
                                v.wait_ge(sP.h, mm_ok)
                                if first:
                                    if sd is not None:
                                        v.wait_ge(sA.h, sd)
                                    v.wait_ge(sV.h, E["cmb_ready_V"])
                                    if mtile == 0 and "dbg_xn_done" in E:
                                        v.wait_ge(E["dbg_xn_done"][0].h, E["dbg_xn_done"][1])
                                else:
                                    v.wait_ge(sV.h, am)
                                if e < 8:
                                    col = cmb[:, mtile * 8 + e: mtile * 8 + e + 1]
                                    if first:
                                        sV.mark(v.tensor_scalar(acc[:, asl], pf[bk][:], col, None, OP.mult))
                                    else:
                                        sV.mark(v.scalar_tensor_tensor(acc[:, asl], pf[bk][:], col,
                                                                       acc[:, asl], OP.mult, OP.add))
                                else:
                                    sV.mark(v.scalar_tensor_tensor(acc[:, asl], pf[bk][:], 1.0,
                                                                   acc[:, asl], OP.mult, OP.add))
                            bank_free[bk] = sV.v
                            acc_mark[(mtile, nbq)] = sV.v
                        hidx += 1
                    widx += 1
            acc_ok = sV.v
            gsnap = [sGS[0].v, sGS[1].v]

            @blk.scalar
            def _(sc, th=th, acc_ok=acc_ok, gsnap=gsnap):
                sc.wait_ge(sV.h, acc_ok)
                for j in range(8):
                    if j >= 2:
                        sc.wait_ge(sGS[j % 2].h, gsnap[j % 2] + 16 * ((j - 2) // 2 + 1))
                    elif th:
                        sc.wait_ge(sGS[j % 2].h, gsnap[j % 2])
                    sA.mark(sc.activation(gst[j % 2][:], acc[:, j * D:(j + 1) * D], AF.Silu))

            @blk.sync
            def _(s, th=th, base=sA.v - 8):
                for j in range(8):
                    s.wait_ge(sA.h, base + j + 1)
                    sGS[j % 2].mark(s.dma_start(
                        out=g_sp[:, (th * 8 + j) * D:(th * 8 + j + 1) * D], in_=gst[j % 2][:]), 16)
            silu_done = sA.v
    E["g_ready_A"] = sA.v
    E["g_sp_finals"] = [(sGS[0], sGS[0].v), (sGS[1], sGS[1].v)]


def _ssd(E):
    nc, blk = E["nc"], E["blk"]
    sP, sV, sA = E["sP"], E["sV"], E["sA"]
    bfm, cfm, dta, dpb, nwb, i128, i64f = (E[k] for k in ("bfm", "cfm", "dta", "dpb", "nwb", "i128", "i64f"))
    ones64, mask128 = E["ones64"], E["mask128"]
    zs_sp, x_sp, yn_sp, cs_sp = E["zs_sp"], E["x_sp"], E["yn_sp"], E["cs_sp"]
    sXT = mkstream(E, "sXT")
    sZT = mkstream(E, "sZT")
    sCW = mkstream(E, "sCW")
    sBC = [mkstream(E, f"sBC{i}") for i in range(2)]
    sYN = mkstream(E, "sYN")
    sWH0 = [mkstream(E, f"sWH0{i}") for i in range(2)]
    sGT = [mkstream(E, f"sGT{i}") for i in range(2)]
    sXI = [mkstream(E, f"sXI{i}") for i in range(2)]
    sOB2 = [mkstream(E, f"sOB2{i}") for i in range(2)]
    whh, g_sp, x_in, out, wout_t = E["whh"], E["g_sp"], E["x_in"], E["out"], E["wout_t"]
    sG = Sem(E["nc"], E["ctx"], "sG")

    with ExitStack() as c2:
        lsb = lambda n, s, dt: c2.enter_context(nc.sbuf_tensor(n, s, dt))
        lps = lambda n: c2.enter_context(nc.psum_tensor(n, [128, 512], F32))
        gt2 = [lsb(f"ss_gt{i}", [128, 512], BF16) for i in range(2)]
        xt2 = [lsb(f"ss_xt{i}", [128, 512], F32) for i in range(2)]
        xtm = lsb("ss_x", [128, DI], BF16)
        zst = lsb("ss_z", [128, DI], BF16)
        Sbf = lsb("ss_Sbf", [128, DI], BF16)
        xdp = lsb("ss_xdp", [128, DI], BF16)
        bca = [lsb(f"ss_bc{i}", [128, 16 * CH], F32) for i in range(2)]
        ebuf = [lsb(f"ss_eb{i}", [128, 16 * CH], BF16) for i in range(4)]
        pt = [lsb(f"ss_pt{i}", [128, 16 * CH], BF16) for i in range(4)]
        xts = [lsb(f"ss_xts{i}", [128, 16 * 64], BF16) for i in range(4)]
        csf = lsb("ss_csf", [NH, CH], F32)
        zcs = lsb("ss_zcs", [NH, CH], F32)
        cst = lsb("ss_cst", [128, NH], F32)
        diag64 = lsb("ss_dg", [NH, NH], F32)
        expc = lsb("ss_expc", [128, NH], F32)
        eto = lsb("ss_eto", [128, NH], F32)
        btm = lsb("ss_btm", [128, CH], BF16)
        sctm = lsb("ss_sct", [128, CH], BF16)
        ybuf = lsb("ss_y", [128, DI], BF16)
        ytmp = lsb("ss_yt", [128, 512], F32)
        stmp = lsb("ss_st", [128, 512], F32)
        ybF = ybuf[:].bitcast(F32)
        ob2 = [ybF[:, 0:512], ybF[:, 512:1024]]
        t1w = ybF[:, 1024:1536]
        ssq = lsb("ss_ssq", [128, 4], F32)
        rstd = lsb("ss_rstd", [128, 1], F32)
        ynb = lsb("ss_ynb", [128, DI], BF16)
        psm = [lps(f"ss_sm{i}") for i in range(2)]
        pY = [lps(f"ss_pY{i}") for i in range(2)]
        pYo = [lps(f"ss_pYo{i}") for i in range(2)]
        pSt = [lps(f"ss_pSt{i}") for i in range(2)]

        @blk.vector
        def _(v):
            v.wait_ge(sA.h, E["g_ready_A"])
            for st, vv in E.get("g_sp_finals", []):
                v.wait_ge(st.h, vv)
            ics, icv = E["init_loads"]
            v.wait_ge(ics.h, icv)
            v.memset(Sbf[:], 0.0)
            sV.mark(v.memset(zcs[:], 0.0))
        init_ok = sV.v

        @blk.gpsimd
        def _(gq):
            gq.wait_ge(sA.h, E["g_ready_A"])
            for st, vv in E.get("g_sp_finals", []):
                gq.wait_ge(st.h, vv)
            for pc in range(2):
                sWH0[pc].mark(gq.dma_start(
                    out=whh[:, pc * 16384:(pc + 1) * 16384].rearrange("p (kc n) -> p kc n", kc=32),
                    in_=wout_t[:, pc * 512:(pc + 1) * 512].rearrange("(kc p) n -> p kc n", p=128)), 16)

        ebank = {}
        x_free = None
        z_free = None
        yn_free = {}
        bc_free = {}
        pt_free = {}
        bank_mm_g = {}
        sbf_war = None
        btl_prev = {}
        scale_ok = None
        cs_wr = None
        csf_rd = None
        cst_evac = None
        btm_evac = None
        sct_evac = None
        ynf_evac = None
        bcasts = {}

        st_pro = {}

        def xdp_pool(c):
            @blk.gpsimd
            def _(gq, c=c, x_ld=st_pro["x_ld"], pml=st_pro.get("piece_mm"),
                  ynw=yn_free.get("b")):
                gq.wait_ge(sXT.h, x_ld)
                if pml is not None:
                    gq.wait_ge(sP.h, pml)
                if ynw is not None:
                    gq.wait_ge(sYN.h, ynw)
                if c == 0:
                    gq.wait_ge(sA.h, E["dpbbf_ok"])
                    gq.wait_ge(sV.h, init_ok)
                sG.mark(gq.tensor_tensor(xdp[:].rearrange("p (h x) -> p h x", h=NH),
                                         xtm[:].rearrange("p (h x) -> p h x", h=NH),
                                         E["dpbbf"][:].unsqueeze(2).broadcast_to([128, NH, 64]),
                                         OP.mult))
            st_pro[("xdp_ok", c)] = sG.v

        def sched_cs(c):
            nonlocal cs_wr, csf_rd

            @blk.vector
            def _(v, c=c, cw=cs_wr, cr=csf_rd, em=st_pro.get("eto_mm"),
                  dgp=st_pro.get(("dg_ok", c - 1))):
                if c == 0:
                    v.wait_ge(sV.h, E["dta_ready_V"])
                    v.wait_ge(sV.h, init_ok)
                else:
                    v.wait_ge(sCW.h, cw)
                    v.wait_ge(sP.h, cr)
                    if dgp is not None:
                        v.wait_ge(sV.h, dgp)
                sV.mark(v.tensor_tensor_scan(csf[:], dta[:, c * CH:(c + 1) * CH], zcs[:],
                                             0.0, OP.add, OP.add))
                v.wait_ge(sV.h, sV.v)
                if em is not None:
                    v.wait_ge(sP.h, em)
                sV.mark(v.tensor_scalar(diag64[:], i64f[:], csf[:, CH - 1:CH], None, OP.mult))
            st_pro[("cs_ok", c)] = sV.v - 1
            st_pro[("dg_ok", c)] = sV.v

            @blk.gpsimd
            def _(gq, c=c, cs_ok=st_pro[("cs_ok", c)], bcw=bcasts.get(c - 2)):
                gq.wait_ge(sV.h, cs_ok)
                if bcw is not None:
                    for s_, v_ in bcw:
                        gq.wait_ge(s_.h, v_)
                sCW.mark(gq.dma_start(out=cs_sp[(c % 2) * NH:(c % 2 + 1) * NH, :], in_=csf[:]), 16)
            cs_wr = sCW.v
            st_pro[("cs_wr", c)] = sCW.v

        def bcast01(c):
            for g in range(2):
                gp = g % 2

                @blk.gpsimd
                def _(gq, c=c, g=g, gp=gp, bcf=bc_free.get(gp), cw=st_pro[("cs_wr", c)]):
                    if c == 0:
                        gq.wait_ge(sA.h, E["g_ready_A"])
                    if bcf is not None:
                        gq.wait_ge(sA.h, bcf)
                    gq.wait_ge(sCW.h, cw)
                    sBC[gp].mark(gq.dma_start(
                        out=bca[gp][:].rearrange("p (h l) -> p h l", h=16),
                        in_=cs_sp[(c % 2) * NH + g * 16:(c % 2) * NH + (g + 1) * 16, :]
                        .unsqueeze(0).broadcast_to([128, 16, CH])), 16)
                st_pro[("bc_ok", g)] = sBC[gp].v

        def prologue(c):
            t0 = c * CH
            nonlocal cst_evac, btm_evac, sct_evac, csf_rd
            x_free = st_pro.get("x_free")
            z_free = st_pro.get("z_free")

            @blk.sync
            def _(s, c=c, t0=t0, xf=x_free, zf=z_free, xtl=st_pro.get("xts_last")):
                if xf is not None:
                    s.wait_ge(sP.h, xf)
                if xtl is not None:
                    s.wait_ge(sG.h, xtl)
                if zf is not None:
                    s.wait_ge(sG.h, zf)
                if c == 0:
                    for st, v in E["inproj_spill_finals"]:
                        s.wait_ge(st.h, v)
                    s.wait_ge(sA.h, E["g_ready_A"])
                    for st, v in E.get("g_sp_finals", []):
                        s.wait_ge(st.h, v)
                sXT.mark(s.dma_start_transpose(xtm[:], x_sp[:, t0:t0 + CH]), 16)
                sZT.mark(s.dma_start_transpose(zst[:], zs_sp[:, t0:t0 + CH]), 16)
            st_pro["x_ld"] = sXT.v
            st_pro["z_ld"] = sZT.v

            if c == 0:
                sched_cs(0)
                bcast01(0)
                xdp_pool(0)
            cs_ok = st_pro[("cs_ok", c)]
            st_pro["dg_ok"] = st_pro[("dg_ok", c)]

            @blk.tensor
            def _(t, c=c, cs_ok=cs_ok, so=scale_ok):
                t.wait_ge(sV.h, cs_ok)
                if c == 0:
                    t.wait_ge(sA.h, E["g_ready_A"])
                else:
                    t.wait_ge(sA.h, so)
                sP.mark(t.matmul(psm[0][0:128, 0:NH], csf[:], i64f[:], start=True, stop=True))
            csf_rd = sP.v
            cst_mm = sP.v

            btm_mm = cst_mm + 1
            sct_mm = cst_mm + 2
            eto_mm = cst_mm + 3
            cst_evac = sA.v + 1
            expc_ok = sA.v + 2
            btm_evac = sA.v + 3
            scale_ok_new = sA.v + 4

            @blk.scalar
            def _(sc, c=c, cst_mm=cst_mm, btm_mm=btm_mm, eto_mm=eto_mm,
                  el=st_pro.get("evac_last"), bl=st_pro.get("bank_last")):
                sc.wait_ge(sP.h, cst_mm)
                if el is not None:
                    sc.wait_ge(sV.h, el[0])
                sA.mark(sc.activation(cst[:], psm[0][:, 0:NH], AF.Copy))
                sc.wait_ge(sA.h, sA.v)
                sA.mark(sc.activation(expc[:], cst[:], AF.Exp))
                sc.wait_ge(sP.h, btm_mm)
                if bl is not None:
                    sc.wait_ge(sP.h, bl)
                sA.mark(sc.activation(btm[:], psm[0][:, 256:384], AF.Copy))
                if el is not None:
                    sc.wait_ge(sG.h, el[1])
                sc.wait_ge(sP.h, eto_mm)
                sA.mark(sc.activation(eto[:], psm[0][:, 64:128], AF.Exp))
            assert sA.v == scale_ok_new
            st_pro["cst_evac"] = cst_evac
            st_pro["btm_evac"] = btm_evac

            @blk.tensor
            def _(t, c=c, t0=t0, ce=cst_evac):
                t.wait_ge(sA.h, ce)
                if c == 0:
                    t.wait_ge(sA.h, E["bc_ready_A"])
                sP.mark(t.matmul(psm[0][:, 256:384], bfm[:, t0:t0 + CH], i128[:], start=True, stop=True))
            assert sP.v == btm_mm

            @blk.tensor
            def _(t, t0=t0, be=btm_evac):
                t.wait_ge(sA.h, be)
                sP.mark(t.matmul(psm[0][:, 128:256], bfm[:, t0:t0 + CH], cfm[:, t0:t0 + CH],
                                 start=True, stop=True))
            assert sP.v == sct_mm

            @blk.vector
            def _(v, sct_mm=sct_mm):
                v.wait_ge(sP.h, sct_mm)
                sV.mark(v.tensor_tensor(sctm[:], psm[0][:, 128:256], mask128[:], OP.mult))
            sct_evac = sV.v
            st_pro["sct_evac"] = sV.v

            @blk.tensor
            def _(t, dg_ok=st_pro["dg_ok"], se=sct_evac):
                t.wait_ge(sV.h, dg_ok)
                t.wait_ge(sV.h, se)
                sP.mark(t.matmul(psm[0][:, 64:128], ones64[:], diag64[:], start=True, stop=True))
            assert sP.v == eto_mm
            st_pro["eto_mm"] = eto_mm
            st_pro["scale_ok"] = scale_ok_new
            st_pro["expc_ok"] = expc_ok

        def main(c):
            nonlocal sbf_war, btl_prev, scale_ok, x_free_out
            scale_ok = st_pro["scale_ok"]
            t0 = c * CH
            x_ld = st_pro["x_ld"]
            z_ld = st_pro["z_ld"]

            if c + 1 < NCH:
                sched_cs(c + 1)
            xdp_ok = st_pro[("xdp_ok", c)]
            cst_ev = st_pro["cst_evac"]
            btm_ev = st_pro["btm_evac"]
            sct_ev = st_pro["sct_evac"]
            eto_mm = st_pro["eto_mm"]
            pt_oks = {}
            btl_oks = {}
            d_oks = {}
            e_oks = {}

            eb0 = sA.v
            for g in range(4):
                gp = g % 2
                if g >= 2:
                    @blk.gpsimd
                    def _(gq, c=c, g=g, gp=gp, bcf=eb0 + g - 1, cw=st_pro[("cs_wr", c)]):
                        gq.wait_ge(sA.h, bcf)
                        gq.wait_ge(sCW.h, cw)
                        sBC[gp].mark(gq.dma_start(
                            out=bca[gp][:].rearrange("p (h l) -> p h l", h=16),
                            in_=cs_sp[(c % 2) * NH + g * 16:(c % 2) * NH + (g + 1) * 16, :]
                            .unsqueeze(0).broadcast_to([128, 16, CH])), 16)
                    st_pro[("bc_ok", g)] = sBC[gp].v
                    bcasts[c] = [(sBC[0], sBC[0].v), (sBC[1], sBC[1].v)]

                @blk.vector
                def _(v, g=g, gp=gp, bc_ok=st_pro[("bc_ok", g)], cst_ev=cst_ev,
                      ef=(eb0 + g - 1) if g >= 2 else None,
                      ew=btl_prev.get(g) if btl_prev else None):
                    v.wait_ge(sBC[gp].h, bc_ok)
                    v.wait_ge(sA.h, cst_ev)
                    if ef is not None:
                        v.wait_ge(sA.h, ef)
                    if ew is not None:
                        v.wait_ge(sV.h, ew[0])
                        v.wait_ge(sG.h, ew[1])
                    sV.mark(v.tensor_tensor(
                        ebuf[g][:].rearrange("p (h l) -> p h l", h=16),
                        bca[gp][:].rearrange("p (h l) -> p h l", h=16),
                        cst[:, g * 16:(g + 1) * 16].unsqueeze(2).broadcast_to([128, 16, CH]),
                        OP.subtract))
                    v.wait_ge(sV.h, sV.v)
                    sV.mark(v.tensor_scalar(bca[gp][:], ebuf[g][:], 0.0, None, OP.min))
                d_oks[g] = sV.v

            @blk.scalar
            def _(sc, d_oks=dict(d_oks)):
                for g in range(4):
                    sc.wait_ge(sV.h, d_oks[g])
                    sA.mark(sc.activation(ebuf[g][:], bca[g % 2][:], AF.Exp))
            assert sA.v == eb0 + 4
            for g in range(4):
                e_oks[g] = eb0 + g + 1
                bc_free[g % 2] = eb0 + g + 1

            xts_oks = {}
            for g in range(4):
                gp = g % 2

                @blk.vector
                def _(v, g=g, gp=gp, e_ok=e_oks[g], sct_ev=sct_ev, ptf=pt_free.get(g)):
                    v.wait_ge(sA.h, e_ok)
                    v.wait_ge(sV.h, sct_ev)
                    if ptf is not None:
                        v.wait_ge(sP.h, ptf)
                    sV.mark(v.tensor_tensor(pt[g][:].rearrange("p (h l) -> p h l", h=16),
                                            ebuf[g][:].rearrange("p (h l) -> p h l", h=16),
                                            sctm[:].unsqueeze(1).broadcast_to([128, 16, CH]),
                                            OP.mult))
                pt_oks[g] = sV.v

                @blk.gpsimd
                def _(gq, g=g, e_ok=e_oks[g], ptf=pt_free.get(g)):
                    gq.wait_ge(sA.h, e_ok)
                    if ptf is not None:
                        gq.wait_ge(sP.h, ptf)
                    sG.mark(gq.tensor_tensor(
                        xts[g][:].rearrange("p (h x) -> p h x", h=16),
                        xtm[:, g * 1024:(g + 1) * 1024].rearrange("p (h x) -> p h x", h=16),
                        ebuf[g][:, CH - 1::CH].unsqueeze(2).broadcast_to([128, 16, 64]),
                        OP.mult))
                xts_oks[g] = sG.v
                btl_oks[g] = (sV.v, sG.v)
            st_pro["xts_last"] = sG.v

            if c + 1 < NCH:
                bcast01(c + 1)

            if c > 0:
                pieces_evac(c - 1)

            @blk.gpsimd
            def _(gq, c=c, pw0=st_pro.get(("pev", 0)), pw1=st_pro.get(("pev", 1))):
                if c == 0:
                    for st, vv in E.get("g_sp_finals", []):
                        gq.wait_ge(st.h, vv)
                for pc, pw in ((0, pw0), (1, pw1)):
                    if pw is not None:
                        gq.wait_ge(sV.h, pw)
                    sGT[pc].mark(gq.dma_start(
                        out=gt2[pc][:], in_=g_sp[:, c * D + pc * 512:c * D + (pc + 1) * 512]), 16)
                    sXI[pc].mark(gq.dma_start(
                        out=xt2[pc][:], in_=x_in[c * CH:(c + 1) * CH, pc * 512:(pc + 1) * 512]), 16)
            st_pro[("gt_ok", 0)] = sGT[0].v
            st_pro[("gt_ok", 1)] = sGT[1].v
            st_pro[("xi_ok", 0)] = sXI[0].v
            st_pro[("xi_ok", 1)] = sXI[1].v

            sdec = [ynb[:, 0:512], ynb[:, 512:1024]]
            sbf_marks = st_pro.get("sbf_marks") or {b: init_ok for b in range(8)}
            new_sbf = {}
            stmp_oks = {}
            ybuf_oks = {}
            for b in range(8):
                g = b // 2
                bi = b % 2

                @blk.gpsimd
                def _(gq, c=c, b=b, so=scale_ok, sbw=sbf_marks[b],
                      sdw=new_sbf.get(b - 2), tpl=st_pro.get("tp_last") if b == 0 else None):
                    gq.wait_ge(sA.h, so)
                    gq.wait_ge(sV.h, sbw)
                    if tpl is not None:
                        gq.wait_ge(sP.h, tpl)
                    if sdw is not None:
                        gq.wait_ge(sV.h, sdw)
                    etv = eto[:, b * 8:(b + 1) * 8].unsqueeze(2).broadcast_to([128, 8, 64])
                    sG.mark(gq.tensor_tensor(
                        sdec[b % 2].rearrange("p (h x) -> p h x", h=8),
                        Sbf[:, b * 512:(b + 1) * 512].rearrange("p (h x) -> p h x", h=8),
                        etv, OP.mult))
                dec_ok = sG.v

                @blk.tensor
                def _(t, c=c, g=g, b=b, bi=bi, x_ld=x_ld, btl_ok=btl_oks[g],
                      btm_ev=btm_ev, sw=sbf_war, ew=ebank.get(bi), xdp_ok=xdp_ok,
                      pev=st_pro.get(("pev", b)) if b < 2 else None):
                    t.wait_ge(sV.h, btl_ok[0])
                    t.wait_ge(sG.h, btl_ok[1])
                    t.wait_ge(sXT.h, x_ld)
                    t.wait_ge(sG.h, xdp_ok)
                    if pev is not None:
                        t.wait_ge(sV.h, pev)
                    if sw is not None:
                        t.wait_ge(sV.h, sw)
                    if ew is not None:
                        t.wait_ge(sV.h, ew)
                    t.matmul(pY[bi][:], i128[:], xdp[:, b * 512:(b + 1) * 512],
                             start=True, stop=False)
                    for hh in range(8):
                        h = b * 8 + hh
                        hl = (h % 16) * CH
                        t.matmul(pY[bi][:, hh * 64:(hh + 1) * 64],
                                 pt[g][:, hl:hl + CH],
                                 xtm[:, h * 64:(h + 1) * 64], start=False, stop=(hh == 7))
                    for hh in range(8):
                        h = b * 8 + hh
                        t.matmul(pYo[bi][:, hh * 64:(hh + 1) * 64],
                                 cfm[:, c * CH:(c + 1) * CH],
                                 Sbf[:, h * 64:(h + 1) * 64], start=True, stop=True)
                    t.wait_ge(sA.h, btm_ev)
                    for hh in range(8):
                        h = b * 8 + hh
                        hx = (h % 16) * 64
                        mi = t.matmul(pSt[bi][:, hh * 64:(hh + 1) * 64],
                                      btm[:],
                                      xts[g][:, hx:hx + 64], start=True, stop=True)
                    sP.mark(mi)
                bank_mm = sP.v
                if b == 7:
                    st_pro["x_free"] = sP.v
                    st_pro["bank_last"] = sP.v
                pt_free[g] = sP.v

                @blk.vector
                def _(v, b=b, bi=bi, bank_mm=bank_mm, so=scale_ok, dec_ok=dec_ok,
                      ybw=ybuf_oks.get(b - 1)):
                    v.wait_ge(sP.h, bank_mm)
                    v.wait_ge(sA.h, so)
                    ecv = expc[:, b * 8:(b + 1) * 8].unsqueeze(2).broadcast_to([128, 8, 64])
                    sV.mark(v.tensor_tensor(ytmp[:].rearrange("p (h x) -> p h x", h=8),
                                            pYo[bi][:].rearrange("p (h x) -> p h x", h=8), ecv, OP.mult))
                    v.wait_ge(sV.h, sV.v)
                    if ybw is not None:
                        v.wait_ge(sG.h, ybw)
                    sV.mark(v.scalar_tensor_tensor(stmp[:], ytmp[:], 1.0, pY[bi][:], OP.mult, OP.add))
                    v.wait_ge(sV.h, sV.v)
                    v.wait_ge(sG.h, dec_ok)
                    sV.mark(v.tensor_tensor(Sbf[:, b * 512:(b + 1) * 512], sdec[b % 2], pSt[bi][:], OP.add))
                stmp_oks[b] = sV.v - 1
                new_sbf[b] = sV.v
                ebank[bi] = sV.v

                @blk.gpsimd
                def _(gq, b=b, z_ld=z_ld, smk=stmp_oks[b],
                      obs=(st_pro.get(("ob_st", 0)), st_pro.get(("ob_st", 1)))):
                    gq.wait_ge(sV.h, smk)
                    gq.wait_ge(sZT.h, z_ld)
                    if b == 0:
                        for pc in range(2):
                            if obs[pc] is not None:
                                gq.wait_ge(sOB2[pc].h, obs[pc])
                    sG.mark(gq.tensor_tensor(ybuf[:, b * 512:(b + 1) * 512], stmp[:],
                                             zst[:, b * 512:(b + 1) * 512], OP.mult))
                ybuf_oks[b] = sG.v
            sbf_war = sV.v
            btl_prev = dict(btl_oks)
            st_pro["sbf_marks"] = new_sbf
            st_pro["evac_last"] = (sV.v, sG.v)
            st_pro["z_free"] = sG.v
            st_pro["y_ok"] = (sV.v, sG.v)

        def tail(c):
            nonlocal ynf_evac
            y_ok = st_pro["y_ok"]

            sPb, sAb0 = sP.v, sA.v
            yn_ok = sV.v + 1

            @blk.scalar
            def _(sc, y_ok=y_ok, ynf=yn_free.get("b"), sPb=sPb):
                sc.wait_ge(sV.h, y_ok[0])
                sc.wait_ge(sG.h, y_ok[1])
                if ynf is not None:
                    sc.wait_ge(sYN.h, ynf)
                sA.mark(sc.activation(ynb[:], ybuf[:], AF.Square, accum_out=ssq[:, 0:1]))
                sc.wait_ge(sA.h, sA.v)
                sA.mark(sc.activation(ssq[:, 3:4], ssq[:, 0:1], AF.Ln, scale=1.0 / DI, bias=E["epsc"][:]))
                sc.wait_ge(sA.h, sA.v)
                sA.mark(sc.activation(rstd[:], ssq[:, 3:4], AF.Exp, scale=-0.5))
                for grp in range(8):
                    sc.wait_ge(sP.h, sPb + grp + 1)
                    pb = psm[1] if grp % 2 == 0 else pY[0]
                    sA.mark(sc.activation(xdp[:, grp * 512:(grp + 1) * 512], pb[:], AF.Copy))
            rstd_ok = sAb0 + 3

            @blk.vector
            def _(v, rstd_ok=rstd_ok):
                v.wait_ge(sA.h, rstd_ok)
                sV.mark(v.scalar_tensor_tensor(ynb[:], ybuf[:], rstd[:], nwb[:], OP.mult, OP.mult))
            assert sV.v == yn_ok

            for grp in range(8):
                pb = psm[1] if grp % 2 == 0 else pY[0]

                @blk.tensor
                def _(t, grp=grp, pb=pb, yn_ok=yn_ok, sAb=sAb0 + 3):
                    t.wait_ge(sV.h, yn_ok)
                    if grp >= 2:
                        t.wait_ge(sA.h, sAb + grp - 1)
                    for j in range(4):
                        mi = t.matmul(pb[:, j * 128:(j + 1) * 128],
                                      ynb[:, (grp * 4 + j) * 128:(grp * 4 + j + 1) * 128],
                                      i128[:], start=(j == 0), stop=(j == 3))
                    sP.mark(mi)
                if grp == 7:
                    st_pro["tp_last"] = sP.v
            ynf_evac = sAb0 + 11

            @blk.sync
            def _(s, c=c, ye=ynf_evac):
                s.wait_ge(sA.h, ye)
                sYN.mark(s.dma_start(
                    out=yn_sp[:, c * CH:(c + 1) * CH].rearrange("(kc p) t -> p kc t", p=128),
                    in_=xdp[:].rearrange("p (kc t) -> p kc t", kc=32)), 16)
            yn_free["b"] = sYN.v

        def pieces_mm(c):
            ye = ynf_evac
            for pc in range(2):
                @blk.tensor
                def _(t, c=c, pc=pc, ye=ye, ebk=ebank.get(pc)):
                    t.wait_ge(sA.h, ye)
                    if ebk is not None:
                        t.wait_ge(sV.h, ebk)
                    if c == 0:
                        t.wait_ge(sWH0[pc].h, 16)
                    for kc in range(32):
                        mi = t.matmul(pSt[pc][:],
                                      xdp[:, kc * 128:(kc + 1) * 128],
                                      whh[:, pc * 16384 + kc * 512:pc * 16384 + (kc + 1) * 512],
                                      start=(kc == 0), stop=(kc == 31))
                    sP.mark(mi)
                st_pro[("pmm", pc)] = sP.v
            st_pro["piece_mm"] = sP.v
            if c + 1 < NCH:
                xdp_pool(c + 1)

        def pieces_evac(cc):
            for pc in range(2):
                @blk.vector
                def _(v, cc=cc, pc=pc, p_mm=st_pro[("pmm", pc)],
                      gt_ok=st_pro[("gt_ok", pc)], xi_ok=st_pro[("xi_ok", pc)],
                      obw=st_pro.get(("ob_st", pc)),
                      pev0=st_pro.get(("pev", 0)) if pc == 1 else None):
                    v.wait_ge(sP.h, p_mm)
                    v.wait_ge(sGT[pc].h, gt_ok)
                    v.wait_ge(sXI[pc].h, xi_ok)
                    if obw is not None:
                        v.wait_ge(sOB2[pc].h, obw)
                    if pev0 is not None:
                        v.wait_ge(sV.h, pev0)
                    sV.mark(v.tensor_tensor(t1w[:], pSt[pc][:], gt2[pc][:], OP.mult))
                    v.wait_ge(sV.h, sV.v)
                    sV.mark(v.tensor_tensor(ob2[pc][:], t1w[:], xt2[pc][:], OP.add))
                st_pro[("pev", pc)] = sV.v
                p_ev = sV.v

                @blk.sync
                def _(s, cc=cc, pc=pc, p_ev=p_ev):
                    s.wait_ge(sV.h, p_ev)
                    sOB2[pc].mark(s.dma_start(
                        out=out[cc * CH:(cc + 1) * CH, pc * 512:(pc + 1) * 512], in_=ob2[pc][:]), 16)
                st_pro[("ob_st", pc)] = sOB2[pc].v

        x_free_out = None
        prologue(0)
        for c in range(NCH):
            main(c)
            if c + 1 < NCH:
                prologue(c + 1)
            tail(c)
            pieces_mm(c)
        pieces_evac(NCH - 1)
    E["ssd_spill_final"] = (sYN, sYN.v)
    E["ssd_pe_final"] = E["sP"].v
    E["ssd_v_final"] = E["sV"].v
    E["ssd_a_final"] = E["sA"].v
    E["ssd_ob_finals"] = [(sOB2[i], sOB2[i].v) for i in range(2)]


def _wout(E):
    nc, blk = E["nc"], E["blk"]
    sP, sV, sA = E["sP"], E["sV"], E["sA"]
    g_sp, x_in, out, yn_sp, wout_t = E["g_sp"], E["x_in"], E["out"], E["yn_sp"], E["wout_t"]
    sWH = mkstream(E, "sWH")
    sYF = [mkstream(E, f"sYF{i}") for i in range(2)]
    sWX = [mkstream(E, f"sWX{i}") for i in range(4)]
    sGL = [mkstream(E, f"sGL{i}") for i in range(4)]
    sOB = [mkstream(E, f"sOB{i}") for i in range(4)]

    with ExitStack() as c2:
        lsb = lambda n, s, dt: c2.enter_context(nc.sbuf_tensor(n, s, dt))
        wh = [E["whh"]] * 2
        ynf = [lsb(f"wo_yn{i}", [128, 32 * 256], BF16) for i in range(2)]
        xt = [lsb(f"wo_x{i}", [128, 512], F32) for i in range(4)]
        gtl = [lsb(f"wo_g{i}", [128, 512], BF16) for i in range(4)]
        ob = [lsb(f"wo_o{i}", [128, 512], F32) for i in range(4)]
        t1 = [lsb(f"wo_t{i}", [128, 512], F32) for i in range(2)]
        pw = [c2.enter_context(nc.psum_tensor(f"wo_p{i}", [128, 512], F32)) for i in range(4)]

        bank_free = {}
        wh_rd = None
        ynf_rd = {}
        xt_free = {}
        ob_free = {}
        ii = 0
        syn, syv = E["ssd_spill_final"]

        def ssd_guards(q):
            q.wait_ge(syn.h, syv)
            q.wait_ge(sP.h, E["ssd_pe_final"])
            q.wait_ge(sV.h, E["ssd_v_final"])
            q.wait_ge(sA.h, E["ssd_a_final"])
            for st, v in E["ssd_ob_finals"]:
                q.wait_ge(st.h, v)

        for nh in (1,):
            whb = wh[0]

            @blk.gpsimd
            def _(gq):
                ssd_guards(gq)
                sWH.mark(gq.dma_start(
                    out=whb[:].rearrange("p (kc n) -> p kc n", kc=32),
                    in_=wout_t[:, 1024:2048].rearrange("(kc p) n -> p kc n", p=128)), 16)
            wh_ok = (sWH, sWH.v)

            for blk8 in range(8):
                yi = blk8 % 2
                ybuf2 = ynf[yi]
                yrd = ynf_rd.get(yi)
                @blk.sync
                def _(s, nh=nh, blk8=blk8, yi=yi, ybuf2=ybuf2, yrd=yrd):
                    if yrd is not None:
                        s.wait_ge(sP.h, yrd)
                    if blk8 < 2:
                        ssd_guards(s)
                    sYF[yi].mark(s.dma_start(
                        out=ybuf2[:].rearrange("p (kc t) -> p kc t", kc=32),
                        in_=yn_sp[:, blk8 * 256:(blk8 + 1) * 256].rearrange("(kc p) t -> p kc t", p=128)), 16)
                yn_ok0 = sYF[yi].v

                for ms in range(2):
                    for nb in range(2):
                        bi = ii % 4
                        pwar = bank_free.get(bi)
                        xwar = xt_free.get(bi)

                        mtile0 = blk8 * 2 + ms
                        gsl0 = slice(mtile0 * D + nh * 1024 + nb * 512,
                                     mtile0 * D + nh * 1024 + (nb + 1) * 512)

                        @blk.sync
                        def _(s, ii=ii, nh=nh, blk8=blk8, ms=ms, nb=nb, bi=bi, xwar=xwar, gsl0=gsl0):
                            if ii < 4:
                                ssd_guards(s)
                                for st, v in E["g_sp_finals"]:
                                    s.wait_ge(st.h, v)
                            if xwar is not None:
                                s.wait_ge(sV.h, xwar)
                            r0 = blk8 * 256 + ms * 128
                            c0 = nh * 1024 + nb * 512
                            sWX[bi].mark(s.dma_start(out=xt[bi][:], in_=x_in[r0:r0 + 128, c0:c0 + 512]), 16)
                            sGL[bi].mark(s.dma_start(out=gtl[bi][:], in_=g_sp[:, gsl0]), 16)
                        x_ok = sWX[bi].v
                        g_ok = sGL[bi].v

                        @blk.tensor
                        def _(t, ii=ii, yi=yi, ms=ms, nb=nb, bi=bi, wh_ok=wh_ok, whb2=whb,
                              yn_ok0=yn_ok0, ybuf2=ybuf2, pwar=pwar):
                            if ii == 0:
                                t.wait_ge(syn.h, syv)
                            t.wait_ge(wh_ok[0].h, wh_ok[1])
                            t.wait_ge(sYF[yi].h, yn_ok0)
                            if pwar is not None:
                                t.wait_ge(sV.h, pwar)
                            for kc in range(32):
                                mi = t.matmul(pw[bi][:], ybuf2[:, kc * 256 + ms * 128: kc * 256 + (ms + 1) * 128],
                                              whb2[:, kc * 1024 + nb * 512: kc * 1024 + (nb + 1) * 512],
                                              start=(kc == 0), stop=(kc == 31))
                            sP.mark(mi)
                        mm_ok = sP.v
                        wh_rd = sP.v
                        ynf_rd[yi] = sP.v

                        @blk.vector
                        def _(v, ii=ii, bi=bi, mm_ok=mm_ok, x_ok=x_ok, g_ok=g_ok,
                              obw=ob_free.get(ii % 4)):
                            v.wait_ge(sP.h, mm_ok)
                            v.wait_ge(sWX[bi].h, x_ok)
                            v.wait_ge(sGL[bi].h, g_ok)
                            if ii < 4:
                                for st, vv in E["ssd_ob_finals"]:
                                    v.wait_ge(st.h, vv)
                            if obw is not None:
                                v.wait_ge(obw[0].h, obw[1])
                            sV.mark(v.tensor_tensor(t1[bi % 2][:], pw[bi][:], gtl[bi][:], OP.mult))
                            v.wait_ge(sV.h, sV.v)
                            sV.mark(v.tensor_tensor(ob[bi][:], t1[bi % 2][:], xt[bi][:], OP.add))
                        bank_free[bi] = sV.v
                        xt_free[bi] = sV.v
                        o_ok = sV.v

                        @blk.sync
                        def _(s, nh=nh, blk8=blk8, ms=ms, nb=nb, bi=bi, o_ok=o_ok):
                            s.wait_ge(sV.h, o_ok)
                            r0 = blk8 * 256 + ms * 128
                            c0 = nh * 1024 + nb * 512
                            sOB[bi].mark(s.dma_start(out=out[r0:r0 + 128, c0:c0 + 512], in_=ob[bi][:]), 16)
                        ob_free[bi] = (sOB[bi], sOB[bi].v)
                        ii += 1


def _dbg_xn(E):
    blk, sA = E["blk"], E["sA"]
    sDX = mkstream(E, "sDX")

    @blk.gpsimd
    def _(g):
        xr = E["xn_ready"]
        g.wait_ge(xr[0].h, xr[1])
        sDX.mark(g.dma_start(out=E["dbg_xn"], in_=E["xn"][:]), 16)
    E["dbg_xn_done"] = (sDX, sDX.v)


def _dbg_dump(E):
    blk, sV, sA = E["blk"], E["sV"], E["sA"]
    sDB = mkstream(E, "sDB")

    @blk.gpsimd
    def _(g):
        g.wait_ge(sV.h, sV.v)
        g.wait_ge(sA.h, sA.v)
        if "cmb_ready_V" in E:
            sDB.mark(g.dma_start(out=E["dbg_cmb"], in_=E["cmb"][:]), 16)
        if "bc_ready_A" in E:
            sDB.mark(g.dma_start(out=E["dbg_bc"][0:128, :], in_=E["bfm"][:]), 16)
            sDB.mark(g.dma_start(out=E["dbg_bc"][128:256, :], in_=E["cfm"][:]), 16)
        if "dta_ready_V" in E:
            sDB.mark(g.dma_start(out=E["dbg_dta"], in_=E["dta"][:]), 16)


def make_in_maps(u, W_in, W_conv, b_conv, dt_bias, A_log, Dp, norm_w, W_out,
                 gate_w, w1, w2, w3, sw1, sw2, sw3):
    f = lambda a: np.ascontiguousarray(np.asarray(a, dtype=np.float32))
    shared = dict(
        win_t=f(np.asarray(W_in).T),
        wconv_r=f(np.asarray(W_conv)[:, 0, :].reshape(34, 128, 4).transpose(1, 0, 2).reshape(128, 34 * 4)),
        bconv_r=f(np.asarray(b_conv).reshape(34, 128).T),
        dtb_c=f(np.asarray(dt_bias).reshape(NH, 1)),
        alog_c=f(np.asarray(A_log).reshape(NH, 1)),
        dp_row=f(np.asarray(Dp).reshape(1, NH)),
        nw_row=f(np.asarray(norm_w).reshape(1, DI)),
        wout_t=f(np.asarray(W_out).T),
        gate_t=f(np.asarray(gate_w).T),
        w1=f(w1), w2=f(w2), w3=f(w3),
        sw1=f(sw1), sw2=f(sw2), sw3=f(sw3),
        i128_in=np.eye(128, dtype=ml_dtypes.bfloat16),
        i64_in=np.eye(NH, dtype=np.float32),
        mneg_in=np.where(np.arange(128)[:, None] > np.arange(CH)[None, :],
                         np.float32(BIGNEG), np.float32(0.0)),
        ones64_in=np.ones((NH, 128), np.float32),
        mask_in=np.where(np.arange(128)[:, None] <= np.arange(CH)[None, :],
                         ml_dtypes.bfloat16(1.0), ml_dtypes.bfloat16(0.0)),
    )
    u = np.asarray(u, np.float32)
    return [dict(shared, x=np.ascontiguousarray(u[i])) for i in range(8)]


def kernel(L0_attn_u, L0_attn_W_in, L0_attn_W_conv, L0_attn_b_conv,
           L0_attn_dt_bias, L0_attn_A_log, L0_attn_Dp, L0_attn_norm_w,
           L0_attn_W_out, L0_ffn_gate_w, L0_ffn_w1, L0_ffn_w2, L0_ffn_w3,
           L0_ffn_sw1, L0_ffn_sw2, L0_ffn_sw3):
    nc = build()
    in_maps = make_in_maps(
        L0_attn_u, L0_attn_W_in, L0_attn_W_conv, L0_attn_b_conv,
        L0_attn_dt_bias, L0_attn_A_log, L0_attn_Dp, L0_attn_norm_w,
        L0_attn_W_out, L0_ffn_gate_w, L0_ffn_w1, L0_ffn_w2, L0_ffn_w3,
        L0_ffn_sw1, L0_ffn_sw2, L0_ffn_sw3)
    res = run_bass_kernel_spmd(nc, in_maps, core_ids=list(range(8)))
    return np.stack([np.asarray(r["out"], np.float32) for r in res.results], axis=0)
